# version 05
# baseline (speedup 1.0000x reference)
import numpy as np
import ml_dtypes
from contextlib import ExitStack

import concourse.bass as bass
import concourse.mybir as mybir
from concourse.bass_utils import run_bass_kernel_spmd

F32 = mybir.dt.float32
BF16 = mybir.dt.bfloat16
I32 = mybir.dt.int32
AF = mybir.ActivationFunctionType
ALU = mybir.AluOpType
NPBF16 = ml_dtypes.bfloat16


class Ctx:
    def __init__(self, nc):
        self.nc = nc
        self.es = ExitStack()
        self.sems = {}
        self.cnt = {}

    def sb(self, name, shape, dtype, es=None):
        return (es or self.es).enter_context(self.nc.sbuf_tensor(name, shape, dtype))

    def ps(self, name, shape, dtype=F32, es=None):
        return (es or self.es).enter_context(self.nc.psum_tensor(name, shape, dtype))

    def sem(self, name):
        if name not in self.sems:
            self.sems[name] = self.es.enter_context(self.nc.semaphore(name))
            self.cnt[name] = 0
        return self.sems[name]

    def inc(self, name, by=1):
        self.cnt[name] += by
        return self.cnt[name]

    def val(self, name):
        return self.cnt[name]

    def close(self):
        self.es.close()


def common_setup(ctx, block):
    ctx.ident = ctx.sb("c_ident", [128, 128], BF16)
    ctx.ones_bf = ctx.sb("c_ones_bf", [1, 512], BF16)
    ctx.ones_col = ctx.sb("c_ones_col", [128, 1], BF16)
    ctx.ones_c64 = ctx.sb("c_ones_c64", [64, 1], BF16)
    ctx.ones_f32 = ctx.sb("c_ones_f32", [1, 128], F32)
    ctx.ident_f = ctx.sb("c_ident_f", [128, 128], F32)
    ctx.consts = ctx.sb("c_consts", [128, 8], F32)
    gsem = ctx.sem("g_init")

    @block.gpsimd
    def _(g):
        g.memset(ctx.ident[:], 0.0).then_inc(gsem, 1)
        ctx.inc("g_init")
        g.wait_ge(gsem, ctx.val("g_init"))
        g.affine_select(out=ctx.ident[:], in_=ctx.ident[:],
                        compare_op=ALU.not_equal, fill=1.0, base=0,
                        pattern=[[-1, 128]], channel_multiplier=1)
        g.memset(ctx.ones_bf[:], 1.0)
        g.memset(ctx.ones_col[:], 1.0)
        g.memset(ctx.ones_c64[:], 1.0)
        g.memset(ctx.ones_f32[:], 1.0)
        g.memset(ctx.ident_f[:], 0.0).then_inc(gsem, 1)
        ctx.inc("g_init")
        g.wait_ge(gsem, ctx.val("g_init"))
        g.affine_select(out=ctx.ident_f[:], in_=ctx.ident_f[:],
                        compare_op=ALU.not_equal, fill=1.0, base=0,
                        pattern=[[-1, 128]], channel_multiplier=1)
        g.memset(ctx.consts[:, 0:1], 1e-5)
        g.memset(ctx.consts[:, 1:2], -0.5)
        g.memset(ctx.consts[:, 2:3], 0.5)
        g.memset(ctx.consts[:, 3:4], 1.0 / 32.0)
        g.memset(ctx.consts[:, 4:5], 1.0 / float(np.sqrt(192.0)))
        g.memset(ctx.consts[:, 5:6], -1.01).then_inc(gsem, 1)
        ctx.inc("g_init")
    ctx.cnt["g_ready"] = ctx.val("g_init")


def emit_bcast(ctx, block, row_dram, width, out_sb, tag, scale=None, extra_waits=()):
    if len(row_dram.shape) == 1:
        row_dram = row_dram.rearrange("(a b) -> a b", a=1)
    with ExitStack() as es:
        row_sb = ctx.sb(f"{tag}_row", [1, width], F32, es=es)
        psum = ctx.ps(f"{tag}_ps", [128, 512], es=es)
        ld = ctx.sem("bc_ld")
        mm = ctx.sem("bc_mm")
        ev = ctx.sem("bc_ev")
        nblk = (width + 511) // 512
        ld0, ev0 = ctx.val("bc_ld"), ctx.val("bc_ev")
        mm0 = ctx.val("bc_mm")

        @block.sync
        def _(sy):
            sy.wait_ge(mm, mm0)
            for sname, sval in extra_waits:
                sy.wait_ge(ctx.sems[sname], sval)
            sy.dma_start(out=row_sb[:], in_=row_dram).then_inc(ld, 16)
            ctx.inc("bc_ld", 16)

        @block.tensor
        def _(t):
            t.wait_ge(ld, ld0 + 16)
            t.wait_ge(ctx.sems["g_init"], ctx.cnt["g_ready"])
            for sname, sval in extra_waits:
                t.wait_ge(ctx.sems[sname], sval)
            for b in range(nblk):
                w = min(512, width - b * 512)
                t.wait_ge(ev, ev0 + b)
                t.matmul(psum[:, 0:w], ctx.ones_f32[0:1, :],
                         row_sb[0:1, b * 512:b * 512 + w],
                         start=True, stop=True).then_inc(mm, 1)
                ctx.inc("bc_mm")

        @block.vector
        def _(v):
            for b in range(nblk):
                w = min(512, width - b * 512)
                v.wait_ge(mm, mm0 + b + 1)
                if scale is None:
                    v.tensor_copy(out_sb[:, b * 512:b * 512 + w],
                                  psum[:, 0:w]).then_inc(ev, 1)
                else:
                    v.tensor_scalar_mul(out_sb[:, b * 512:b * 512 + w],
                                        psum[:, 0:w], scale).then_inc(ev, 1)
                ctx.inc("bc_ev")
    ctx.cnt[f"{tag}_done"] = ctx.val("bc_ev")


def emit_fence(ctx, block, waits):
    waits = [(n, v) for (n, v) in waits if v > 0]
    if not waits:
        return

    def w(eng):
        for sname, sval in waits:
            eng.wait_ge(ctx.sems[sname], sval)

    @block.sync
    def _(sy):
        w(sy)

    @block.gpsimd
    def _(g):
        w(g)

    @block.scalar
    def _(sc):
        w(sc)

    @block.vector
    def _(v):
        w(v)

    @block.tensor
    def _(t):
        w(t)



def alloc_ln_bufs(ctx, es, pfx="ln"):
    return dict(
        ybuf=ctx.sb(f"{pfx}_ybuf", [128, 4096], F32, es=es),
        xbuf=ctx.sb(f"{pfx}_xbuf", [128, 4096], BF16, es=es),
        stats=ctx.sb(f"{pfx}_stats", [128, 48], F32, es=es),
        aggr=ctx.sb(f"{pfx}_aggr", [128, 2], F32, es=es),
        scr=ctx.sb(f"{pfx}_scr", [128, 4], F32, es=es),
    )


def emit_ln(ctx, block, bufs, y_dram, xlnT, tag, start_waits=()):
    ysem = [ctx.sem(f"{tag}_y{i}") for i in range(2)]
    dv = ctx.sem(f"{tag}_dv")
    ac = ctx.sem(f"{tag}_ac")
    pe = ctx.sem(f"{tag}_pe")
    xr = ctx.sem(f"{tag}_xr")
    ybuf, xbuf = bufs["ybuf"], bufs["xbuf"]
    stats, aggr, scr = bufs["stats"], bufs["aggr"], bufs["scr"]

    y0_y = [ctx.val(f"{tag}_y{i}") for i in range(2)]
    y0_dv = ctx.val(f"{tag}_dv")
    y0_ac, y0_pe, y0_xr = ctx.val(f"{tag}_ac"), ctx.val(f"{tag}_pe"), ctx.val(f"{tag}_xr")

    uq = ctx.cnt.setdefault("_ln_uq", 0)
    ctx.cnt["_ln_uq"] = uq + 1
    with ExitStack() as es:
        pstr = [ctx.ps(f"{tag}{uq}_pstr{i}", [128, 1024], BF16, es=es)
                for i in range(2)]

        @block.sync
        def _(sy):
            for sname, sval in start_waits:
                sy.wait_ge(ctx.sems[sname], sval)
            for ti in range(8):
                for c in range(8):
                    if ti >= 1:
                        sy.wait_ge(ac, y0_ac + (ti - 1) * 10 + 2 + c + 1)
                    if c >= 2:
                        sy.wait_ge(dv, y0_dv + ti * 10 + c - 1)
                    sy.dma_start(out=ybuf[:, c * 512:(c + 1) * 512],
                                 in_=y_dram[ti * 128:(ti + 1) * 128,
                                            c * 512:(c + 1) * 512]
                                 ).then_inc(ysem[c % 2], 16)
                    ctx.inc(f"{tag}_y{c % 2}", 16)

        @block.vector
        def _(v):
            for ti in range(8):
                for c in range(8):
                    v.wait_ge(ysem[c % 2], y0_y[c % 2]
                              + 16 * (ti * 4 + c // 2 + 1))
                    v.bn_stats(stats[:, c * 6:(c + 1) * 6],
                               ybuf[:, c * 512:(c + 1) * 512]).then_inc(dv, 1)
                    ctx.inc(f"{tag}_dv")
                v.wait_ge(dv, ctx.val(f"{tag}_dv"))
                v.bn_aggr(aggr[:], stats[:]).then_inc(dv, 1)
                ctx.inc(f"{tag}_dv")
                v.wait_ge(ac, y0_ac + ti * 10 + 2)
                v.tensor_scalar(scr[:, 2:3], aggr[:, 0:1], scr[:, 1:2], -1.0,
                                ALU.mult, ALU.mult).then_inc(dv, 1)
                ctx.inc(f"{tag}_dv")
                for pk in range(4):
                    p_idx = ti * 4 + pk
                    v.wait_ge(pe, y0_pe + p_idx + 1)
                    out_ap = (xlnT[:].rearrange("p (a x) -> p a x", a=32)
                              [:, pk * 8:(pk + 1) * 8, ti * 128:(ti + 1) * 128])
                    in_ap = pstr[p_idx % 2][:].rearrange("p (a x) -> p a x", a=8)
                    v.tensor_copy(out_ap, in_ap).then_inc(xr, 1)
                    ctx.inc(f"{tag}_xr")

        @block.scalar
        def _(s):
            s.wait_ge(ctx.sems["g_init"], ctx.cnt["g_ready"])
            for ti in range(8):
                s.wait_ge(dv, y0_dv + ti * 10 + 9)
                s.activation(scr[:, 0:1], aggr[:, 1:2], AF.Ln,
                             bias=ctx.consts[:, 0:1], scale=1.0).then_inc(ac, 1)
                ctx.inc(f"{tag}_ac")
                s.wait_ge(ac, y0_ac + ti * 10 + 1)
                s.activation(scr[:, 1:2], scr[:, 0:1], AF.Exp,
                             bias=0.0, scale=ctx.consts[:, 1:2]).then_inc(ac, 1)
                ctx.inc(f"{tag}_ac")
                s.wait_ge(dv, y0_dv + ti * 10 + 10)
                for c in range(8):
                    if ti >= 1:
                        s.wait_ge(pe, y0_pe + (ti - 1) * 4 + c // 2 + 1)
                    s.activation(xbuf[:, c * 512:(c + 1) * 512],
                                 ybuf[:, c * 512:(c + 1) * 512], AF.Identity,
                                 bias=scr[:, 2:3], scale=scr[:, 1:2]).then_inc(ac, 1)
                    ctx.inc(f"{tag}_ac")

        @block.tensor
        def _(t):
            t.wait_ge(ctx.sems["g_init"], ctx.cnt["g_ready"])
            for ti in range(8):
                for pk in range(4):
                    t.wait_ge(ac, y0_ac + ti * 10 + 2 + 2 * pk + 2)
                    p_idx = ti * 4 + pk
                    if p_idx >= 2:
                        t.wait_ge(xr, y0_xr + p_idx - 1)
                    for j in range(8):
                        k = pk * 8 + j
                        ins = t.transpose(pstr[p_idx % 2][:, j * 128:(j + 1) * 128],
                                          xbuf[:, k * 128:(k + 1) * 128],
                                          ctx.ident[:])
                        if j == 7:
                            ins.then_inc(pe, 1)
                    ctx.inc(f"{tag}_pe")

    ctx.cnt[f"{tag}_done"] = ctx.val(f"{tag}_xr")



def emit_dequant(ctx, block, es, qtg_dram, zp_dram, scale_dram, w0t_dram,
                 tag="dq"):
    qs = ctx.sem(f"{tag}_q")
    dv = ctx.sem(f"{tag}_dv")
    st = [ctx.sem(f"{tag}_st{i}") for i in range(2)]

    qbuf = ctx.sb(f"{tag}_qb", [128, 2048], I32, es=es)
    vint = ctx.sb(f"{tag}_vint", [128, 2048], I32, es=es)
    vf = ctx.sb(f"{tag}_vf", [128, 2048], F32, es=es)
    wout = [ctx.sb(f"{tag}_wo{i}", [128, 2048], BF16, es=es) for i in range(2)]
    zpB = ctx.sb(f"{tag}_zpB", [128, 2048], F32, es=es)
    scB = ctx.sb(f"{tag}_scB", [128, 2048], F32, es=es)

    q0, dv0 = ctx.val(f"{tag}_q"), ctx.val(f"{tag}_dv")
    for h in range(2):
        nsl = slice(h * 2048, (h + 1) * 2048)
        emit_bcast(ctx, block, zp_dram[nsl], 2048, zpB, f"{tag}_zb{h}")
        emit_bcast(ctx, block, scale_dram[nsl], 2048, scB, f"{tag}_sb{h}")
        h_dv0 = ctx.val(f"{tag}_dv")
        h_q0 = ctx.val(f"{tag}_q")
        h_st0 = [ctx.val(f"{tag}_st{i}") for i in range(2)]

        @block.sync
        def _(sy, h=h, nsl=nsl, h_dv0=h_dv0):
            for gt in range(8):
                base = h_dv0 + gt * 12
                if gt >= 1:
                    sy.wait_ge(dv, base)
                elif h >= 1:
                    sy.wait_ge(dv, h_dv0)
                sy.dma_start(out=qbuf[:],
                             in_=qtg_dram[gt * 128:(gt + 1) * 128, 0, nsl]
                             ).then_inc(qs, 16)
                ctx.inc(f"{tag}_q", 16)
                sy.wait_ge(dv, base + 6)
                sy.dma_start(out=qbuf[:],
                             in_=qtg_dram[gt * 128:(gt + 1) * 128, 1, nsl]
                             ).then_inc(qs, 16)
                ctx.inc(f"{tag}_q", 16)
                sy.wait_ge(dv, base + 9)
                sy.dma_start(out=qbuf[:],
                             in_=qtg_dram[gt * 128:(gt + 1) * 128, 2, nsl]
                             ).then_inc(qs, 16)
                ctx.inc(f"{tag}_q", 16)

        zs_done = ctx.val("bc_ev")

        @block.vector
        def _(v, h=h, h_q0=h_q0, h_st0=h_st0, zs_done=zs_done):
            v.wait_ge(ctx.sems["bc_ev"], zs_done)
            for gt in range(8):
                for jj in range(4):
                    if jj in (0, 1):
                        v.wait_ge(qs, h_q0 + gt * 48 + 16)
                    elif jj == 2:
                        v.wait_ge(qs, h_q0 + gt * 48 + 32)
                    else:
                        v.wait_ge(qs, h_q0 + gt * 48 + 48)
                    p = gt * 4 + jj
                    if p >= 2:
                        v.wait_ge(st[p % 2], h_st0[p % 2] + 16 * (p // 2))
                    elif h >= 1:
                        v.wait_ge(st[p % 2], h_st0[p % 2])
                    if jj == 1:
                        v.tensor_single_scalar(vint[:], qbuf[:], 6,
                                               ALU.logical_shift_right).then_inc(dv, 1)
                    else:
                        v.tensor_single_scalar(vint[:], qbuf[:], 63,
                                               ALU.bitwise_and).then_inc(dv, 1)
                    ctx.inc(f"{tag}_dv")
                    v.wait_ge(dv, ctx.val(f"{tag}_dv"))
                    v.tensor_sub(vf[:], vint[:], zpB[:]).then_inc(dv, 1)
                    ctx.inc(f"{tag}_dv")
                    v.wait_ge(dv, ctx.val(f"{tag}_dv"))
                    v.tensor_mul(wout[(gt * 4 + jj) % 2][:], vf[:],
                                 scB[:]).then_inc(dv, 1)
                    ctx.inc(f"{tag}_dv")

        @block.scalar
        def _(sc, h=h, h_dv0=h_dv0, nsl=nsl):
            for gt in range(8):
                for jj in range(4):
                    sc.wait_ge(dv, h_dv0 + gt * 12 + jj * 3 + 3)
                    sc.dma_start(
                        out=w0t_dram.rearrange("(gg four) n -> gg four n", four=4)
                            [gt * 128:(gt + 1) * 128, jj, nsl],
                        in_=wout[(gt * 4 + jj) % 2][:]
                        ).then_inc(st[(gt * 4 + jj) % 2], 16)
                    ctx.inc(f"{tag}_st{(gt * 4 + jj) % 2}", 16)

    ctx.cnt[f"{tag}_done"] = [(f"{tag}_st{i}", ctx.val(f"{tag}_st{i}"))
                              for i in range(2)]



def alloc_ffn0_bufs(ctx, es, ntmp=8):
    return dict(
        wblk=ctx.sb("f0_wblk", [128, 32 * 512], BF16, es=es),
        tmp=[ctx.sb(f"f0_tmp{i}", [128, 512], F32, es=es) for i in range(ntmp)],
    )


alloc_ffn0_bufs2 = alloc_ffn0_bufs


def emit_ffn0(ctx, block, bufs, xlnT, w0t_dram, halfb0B, y_out, tag,
              xr_wait=None, w_wait=None, b_wait=0, y_wait=None):
    wq = [ctx.sem(f"{tag}_w{q}") for q in range(4)]
    mmq = ctx.sem(f"{tag}_mmq")
    mm = ctx.sem(f"{tag}_mm")
    ev = ctx.sem(f"{tag}_ev")
    ysts = [ctx.sem(f"{tag}_yst{i}") for i in range(8)]
    wblk, tmp = bufs["wblk"], bufs["tmp"]

    w0 = [ctx.val(f"{tag}_w{q}") for q in range(4)]
    mq0, mm0 = ctx.val(f"{tag}_mmq"), ctx.val(f"{tag}_mm")
    ev0 = ctx.val(f"{tag}_ev")
    yst0 = [ctx.val(f"{tag}_yst{i}") for i in range(8)]

    with ExitStack() as es:
        psums = [ctx.ps(f"{tag}_pb{i}", [128, 512], es=es) for i in range(8)]

        @block.sync
        def _(sy):
            for sn, sv in (w_wait or []):
                sy.wait_ge(ctx.sems[sn], sv)
            for n in range(8):
                for q in range(4):
                    if n >= 1:
                        if q < 3:
                            sy.wait_ge(mmq, mq0 + 3 * (n - 1) + q + 1)
                        else:
                            sy.wait_ge(mm, mm0 + n)
                    sy.dma_start(
                        out=wblk[:].rearrange("p (a x) -> p a x", a=32)
                            [:, q * 8:(q + 1) * 8, :],
                        in_=w0t_dram.rearrange("(a p) n -> p a n", p=128)
                            [:, q * 8:(q + 1) * 8, n * 512:(n + 1) * 512],
                    ).then_inc(wq[q], 16)
                    ctx.inc(f"{tag}_w{q}", 16)

        @block.tensor
        def _(t):
            if xr_wait is not None:
                t.wait_ge(ctx.sems[xr_wait[0]], xr_wait[1])
            for n in range(8):
                if n >= 1:
                    t.wait_ge(ev, ev0 + 8 * n)
                for k in range(32):
                    if k % 8 == 0:
                        t.wait_ge(wq[k // 8], w0[k // 8] + 16 * (n + 1))
                    for ti in range(8):
                        ins = t.matmul(
                            psums[ti][:],
                            xlnT[:, k * 1024 + ti * 128:k * 1024 + (ti + 1) * 128],
                            wblk[:, k * 512:(k + 1) * 512],
                            start=(k == 0), stop=(k == 31))
                        if ti == 7 and k % 8 == 7:
                            if k == 31:
                                ins.then_inc(mm, 1)
                                ctx.inc(f"{tag}_mm")
                            else:
                                ins.then_inc(mmq, 1)
                                ctx.inc(f"{tag}_mmq")

        @block.vector
        def _(v):
            v.wait_ge(ctx.sems["bc_ev"], b_wait)
            if y_wait is not None:
                v.wait_ge(ctx.sems[y_wait[0]], y_wait[1])
            for n in range(8):
                v.wait_ge(mm, mm0 + n + 1)
                for ti in range(8):
                    idx = n * 8 + ti
                    slot = idx % len(tmp)
                    if idx >= len(tmp):
                        v.wait_ge(ysts[slot], yst0[slot] + 16 * (idx // len(tmp)))
                    v.scalar_tensor_tensor(tmp[slot][:], psums[ti][:], 0.5,
                                           halfb0B[:, n * 512:(n + 1) * 512],
                                           ALU.mult, ALU.add).then_inc(ev, 1)
                    ctx.inc(f"{tag}_ev")

        @block.gpsimd
        def _(g):
            for n in range(8):
                for ti in range(8):
                    idx = n * 8 + ti
                    slot = idx % len(tmp)
                    g.wait_ge(ev, ev0 + idx + 1)
                    g.dma_start(out=y_out[ti * 128:(ti + 1) * 128,
                                          n * 512:(n + 1) * 512],
                                in_=tmp[slot][:],
                                accum_op=ALU.add).then_inc(ysts[slot], 16)
                    ctx.inc(f"{tag}_yst{slot}", 16)
    ctx.cnt[f"{tag}_done"] = [(f"{tag}_yst{i}", ctx.val(f"{tag}_yst{i}"))
                              for i in range(8)]



def emit_qkv(ctx, block, es, xlnT, wqkv_dram, bqkv_dram, qfeat_dram,
             kown_dram, vown_dram, tag="qkv", start_waits=(), xr_wait=None):
    wkv = ctx.sb(f"{tag}_wkv", [128, 16384], BF16, es=es)
    qf = [ctx.sb(f"{tag}_qf{i}", [128, 1024], BF16, es=es) for i in range(2)]
    kvt = [ctx.sb(f"{tag}_kvt{i}", [128, 512], BF16, es=es) for i in range(2)]
    bkvB = ctx.sb(f"{tag}_bkvB", [128, 2048], F32, es=es)
    bq_pm = ctx.sb(f"{tag}_bqpm", [128, 8], F32, es=es)
    bq_sc = ctx.sb(f"{tag}_bqsc", [128, 8], F32, es=es)

    wqa = [ctx.sem(f"{tag}_wqa{i}") for i in range(2)]
    kw = [ctx.sem(f"{tag}_kw{q}") for q in range(4)]
    qmm = ctx.sem(f"{tag}_qmm")
    qev = ctx.sem(f"{tag}_qev")
    qsts = [ctx.sem(f"{tag}_qst{i}") for i in range(2)]
    kmq = ctx.sem(f"{tag}_kmq")
    kbd = ctx.sem(f"{tag}_kbd")
    kev = ctx.sem(f"{tag}_kev")
    ksts = [ctx.sem(f"{tag}_kst{i}") for i in range(2)]
    pvl = ctx.sem(f"{tag}_pvl")
    dvx = ctx.sem(f"{tag}_dvx")

    @block.sync
    def _(sy):
        sy.dma_start(out=bq_pm[:],
                     in_=bqkv_dram[0:1024].rearrange("(c p) -> p c", p=128),
                     allow_slow_non_contiguous=True).then_inc(pvl, 16)
        ctx.inc(f"{tag}_pvl", 16)

    @block.vector
    def _(v):
        v.wait_ge(pvl, ctx.val(f"{tag}_pvl"))
        v.tensor_scalar_mul(bq_sc[:], bq_pm[:], 1.0 / 32.0).then_inc(dvx, 1)
        ctx.inc(f"{tag}_dvx")

    emit_bcast(ctx, block, bqkv_dram[1024:3072], 2048, bkvB, f"{tag}_bkv",
               extra_waits=([xr_wait] if xr_wait is not None else []))
    bkv_done = ctx.val(f"{tag}_bkv_done")

    with ExitStack() as pses:
        psq = [ctx.ps(f"{tag}_psq{i}", [128, 512], es=pses) for i in range(4)]
        psk = [ctx.ps(f"{tag}_psk{i}", [128, 512], es=pses) for i in range(4)]
        allps = psq + psk

        wqa0 = [ctx.val(f"{tag}_wqa{i}") for i in range(2)]
        qmm0, qev0 = ctx.val(f"{tag}_qmm"), ctx.val(f"{tag}_qev")
        qst0 = [ctx.val(f"{tag}_qst{i}") for i in range(2)]

        @block.gpsimd
        def _(g):
            for sname, sval in start_waits:
                g.wait_ge(ctx.sems[sname], sval)
            for fc in range(8):
                if fc >= 2:
                    g.wait_ge(qmm, qmm0 + fc - 1)
                half = (fc % 2) * 8192
                for q in range(4):
                    g.dma_start(
                        out=wkv[:, half:half + 4096].rearrange("p (a x) -> p a x", a=32)
                            [:, q * 8:(q + 1) * 8, :],
                        in_=wqkv_dram.rearrange("(a p) f -> p a f", p=128)
                            [:, q * 8:(q + 1) * 8, fc * 128:(fc + 1) * 128],
                    ).then_inc(wqa[fc % 2], 16)
                    ctx.inc(f"{tag}_wqa{fc % 2}", 16)

        @block.tensor
        def _(t):
            if xr_wait is not None:
                t.wait_ge(ctx.sems[xr_wait[0]], xr_wait[1])
            t.wait_ge(ctx.sems["bc_ev"], bkv_done)
            for fc in range(8):
                t.wait_ge(wqa[fc % 2], wqa0[fc % 2] + 64 * (fc // 2 + 1))
                half = (fc % 2) * 8192
                if fc >= 2:
                    t.wait_ge(qev, qev0 + 2 * (fc - 1))
                for k in range(32):
                    for th in range(2):
                        ins = t.matmul(
                            psq[(fc % 2) * 2 + th][:],
                            wkv[:, half + k * 128:half + (k + 1) * 128],
                            xlnT[:, k * 1024 + th * 512:k * 1024 + (th + 1) * 512],
                            start=(k == 0), stop=(k == 31))
                        if k == 31 and th == 1:
                            ins.then_inc(qmm, 1)
                ctx.inc(f"{tag}_qmm")

        @block.scalar
        def _(s):
            s.wait_ge(dvx, ctx.val(f"{tag}_dvx"))
            s.wait_ge(ctx.sems["g_init"], ctx.cnt["g_ready"])
            for fc in range(8):
                s.wait_ge(qmm, qmm0 + fc + 1)
                if fc >= 2:
                    s.wait_ge(qsts[fc % 2], qst0[fc % 2] + 16 * (fc // 2))
                for th in range(2):
                    s.activation(qf[fc % 2][:, th * 512:(th + 1) * 512],
                                 psq[(fc % 2) * 2 + th][:], AF.Identity,
                                 bias=bq_sc[:, fc:fc + 1],
                                 scale=ctx.consts[:, 3:4]).then_inc(qev, 1)
                    ctx.inc(f"{tag}_qev")

        @block.sync
        def _(sy):
            for fc in range(8):
                sy.wait_ge(qev, qev0 + 2 * (fc + 1))
                sy.dma_start(out=qfeat_dram[fc * 128:(fc + 1) * 128, :],
                             in_=qf[fc % 2][:]).then_inc(qsts[fc % 2], 16)
                ctx.inc(f"{tag}_qst{fc % 2}", 16)

        kw0 = [ctx.val(f"{tag}_kw{q}") for q in range(4)]
        kmq0, kev0 = ctx.val(f"{tag}_kmq"), ctx.val(f"{tag}_kev")
        kst0 = [ctx.val(f"{tag}_kst{i}") for i in range(2)]
        kbd0 = ctx.val(f"{tag}_kbd")

        @block.gpsimd
        def _(g):
            g.wait_ge(qmm, qmm0 + 8)
            for fv in range(4):
                for q in range(4):
                    if fv >= 1:
                        if q < 3:
                            g.wait_ge(kmq, kmq0 + 3 * (fv - 1) + q + 1)
                        else:
                            g.wait_ge(kbd, kbd0 + fv)
                    g.dma_start(
                        out=wkv[:].rearrange("p (a x) -> p a x", a=32)
                            [:, q * 8:(q + 1) * 8, :],
                        in_=wqkv_dram.rearrange("(a p) f -> p a f", p=128)
                            [:, q * 8:(q + 1) * 8,
                             1024 + fv * 512:1024 + (fv + 1) * 512],
                    ).then_inc(kw[q], 16)
                    ctx.inc(f"{tag}_kw{q}", 16)

        @block.tensor
        def _(t):
            t.wait_ge(qev, qev0 + 16)
            for fv in range(4):
                if fv >= 1:
                    t.wait_ge(kev, kev0 + 8 * fv)
                for k in range(32):
                    if k % 8 == 0:
                        t.wait_ge(kw[k // 8], kw0[k // 8] + 16 * (fv + 1))
                    for ti in range(8):
                        ins = t.matmul(
                            allps[ti][:],
                            xlnT[:, k * 1024 + ti * 128:k * 1024 + (ti + 1) * 128],
                            wkv[:, k * 512:(k + 1) * 512],
                            start=(k == 0), stop=(k == 31))
                        if ti == 7 and k % 8 == 7:
                            if k == 31:
                                ins.then_inc(kbd, 1)
                                ctx.inc(f"{tag}_kbd")
                            else:
                                ins.then_inc(kmq, 1)
                                ctx.inc(f"{tag}_kmq")

        @block.vector
        def _(v):
            v.wait_ge(ctx.sems["bc_ev"], bkv_done)
            for fv in range(4):
                v.wait_ge(kbd, kbd0 + fv + 1)
                for ti in range(8):
                    idx = fv * 8 + ti
                    slot = idx % 2
                    if idx >= 2:
                        v.wait_ge(ksts[slot], kst0[slot] + 16 * (idx // 2))
                    v.tensor_add(kvt[slot][:], allps[ti][:],
                                 bkvB[:, fv * 512:(fv + 1) * 512]).then_inc(kev, 1)
                    ctx.inc(f"{tag}_kev")

        @block.sync
        def _(sy):
            for fv in range(4):
                for ti in range(8):
                    idx = fv * 8 + ti
                    slot = idx % 2
                    sy.wait_ge(kev, kev0 + idx + 1)
                    dst = kown_dram if fv < 2 else vown_dram
                    col = (fv % 2) * 512
                    sy.dma_start(out=dst[ti * 128:(ti + 1) * 128, col:col + 512],
                                 in_=kvt[slot][:]).then_inc(ksts[slot], 16)
                    ctx.inc(f"{tag}_kst{slot}", 16)

    ctx.cnt[f"{tag}_done"] = ([(f"{tag}_kst{i}", ctx.val(f"{tag}_kst{i}")) for i in range(2)]
                              + [(f"{tag}_qst{i}", ctx.val(f"{tag}_qst{i}")) for i in range(2)])



def emit_s2attn(ctx, block, qfeat_dram, kfull_dram, vfull_dram, wo_dram,
                bo_dram, y_out, tag="s2", y_wait=None):
    with ExitStack() as esA:
        bo0B = ctx.sb(f"{tag}_bo0B", [128, 4096], F32, es=esA)
        emit_bcast(ctx, block, bo_dram, 4096, bo0B, f"{tag}_bo")
        obf = ctx.sb(f"{tag}_obf", [128, 8 * 1024], BF16, es=esA)
        rlrow = ctx.sb(f"{tag}_rlrow", [1, 1024], F32, es=esA)
        rlT = ctx.sb(f"{tag}_rlT", [128, 8], F32, es=esA)

        ld = ctx.sem(f"{tag}_ld")
        nrm = ctx.sem(f"{tag}_nrm")
        nmm = ctx.sem(f"{tag}_nmm")
        nac = ctx.sem(f"{tag}_nac")
        smm = ctx.sem(f"{tag}_smm")
        sex = ctx.sem(f"{tag}_sex")
        lmm = ctx.sem(f"{tag}_lmm")
        pvm = ctx.sem(f"{tag}_pvm")
        pve = ctx.sem(f"{tag}_pve")
        vts = [ctx.sem(f"{tag}_vt{i}") for i in range(4)]
        rle = ctx.sem(f"{tag}_rle")

        with ExitStack() as esB:
            kf = ctx.sb(f"{tag}_kf", [128, 8 * 4096], BF16, es=esB)
            qsb = ctx.sb(f"{tag}_qsb", [128, 8 * 1024], BF16, es=esB)
            pt = [ctx.sb(f"{tag}_pt{i}", [128, 512], BF16, es=esB) for i in range(32)]
            sq = ctx.sb(f"{tag}_sq", [128, 1024], BF16, es=esB)
            qn2row = ctx.sb(f"{tag}_qn2", [1, 1024], F32, es=esB)
            sqrow = ctx.sb(f"{tag}_sqrow", [1, 1024], F32, es=esB)
            negb = ctx.sb(f"{tag}_negb", [1, 1024], BF16, es=esB)
            kmx = ctx.sb(f"{tag}_kmx", [1, 2], F32, es=esB)
            vt = [ctx.sb(f"{tag}_vtb{i}", [128, 512], BF16, es=esB) for i in range(4)]

            @block.sync
            def _(sy):
                for c in range(8):
                    sy.dma_start(out=qsb[:, c * 1024:(c + 1) * 1024],
                                 in_=qfeat_dram[c * 128:(c + 1) * 128, :]
                                 ).then_inc(ld, 16)
                    ctx.inc(f"{tag}_ld", 16)
                for c in range(8):
                    sy.dma_start(out=kf[:, c * 4096:(c + 1) * 4096],
                                 in_=kfull_dram[:, c * 128:(c + 1) * 128],
                                 transpose=True).then_inc(ld, 16)
                    ctx.inc(f"{tag}_ld", 16)
            kld = ctx.val(f"{tag}_ld")

            with ExitStack() as esP:
                psn = ctx.ps(f"{tag}_psn", [1, 512], es=esP)
                psq2 = [ctx.ps(f"{tag}_psq2{i}", [1, 512], es=esP) for i in range(2)]
                nac0 = ctx.val(f"{tag}_nac")
                nmm0 = ctx.val(f"{tag}_nmm")
                nrm0 = ctx.val(f"{tag}_nrm")

                @block.scalar
                def _(s):
                    s.wait_ge(ld, kld)
                    for i in range(64):
                        if i >= 1:
                            s.wait_ge(nmm, nmm0 + i)
                        b, c = divmod(i, 8)
                        s.activation(sq[:, 0:512],
                                     kf[:, c * 4096 + b * 512:c * 4096 + (b + 1) * 512],
                                     AF.Square, bias=0.0, scale=1.0).then_inc(nac, 1)
                        ctx.inc(f"{tag}_nac")
                    for c in range(8):
                        s.wait_ge(nmm, nmm0 + 64 + 2 * c)
                        s.activation(sq[:], qsb[:, c * 1024:(c + 1) * 1024],
                                     AF.Square, bias=0.0, scale=1.0).then_inc(nac, 1)
                        ctx.inc(f"{tag}_nac")

                @block.tensor
                def _(t):
                    t.wait_ge(ctx.sems["g_init"], ctx.cnt["g_ready"])
                    t.wait_ge(ctx.sems["bc_ev"], ctx.val(f"{tag}_bo_done"))
                    for i in range(64):
                        b, c = divmod(i, 8)
                        t.wait_ge(nac, nac0 + i + 1)
                        if c == 0 and b > 0:
                            t.wait_ge(nrm, nrm0 + 2 * b)
                        t.matmul(psn[:], ctx.ones_col[:], sq[:, 0:512],
                                 start=(c == 0), stop=(c == 7)).then_inc(nmm, 1)
                        ctx.inc(f"{tag}_nmm")
                    for c in range(8):
                        t.wait_ge(nac, nac0 + 64 + c + 1)
                        for q2 in range(2):
                            t.matmul(psq2[q2][:], ctx.ones_col[:],
                                     sq[:, q2 * 512:(q2 + 1) * 512],
                                     start=(c == 0), stop=(c == 7)).then_inc(nmm, 1)
                            ctx.inc(f"{tag}_nmm")

                @block.vector
                def _(v):
                    v.memset(kmx[0:1, 0:1], 0.0).then_inc(nrm, 1)
                    ctx.inc(f"{tag}_nrm")
                    for b in range(8):
                        v.wait_ge(nmm, nmm0 + (b + 1) * 8)
                        v.wait_ge(nrm, ctx.val(f"{tag}_nrm"))
                        v.reduce_max(kmx[0:1, 1:2], psn[:],
                                     mybir.AxisListType.X).then_inc(nrm, 1)
                        ctx.inc(f"{tag}_nrm")
                        v.wait_ge(nrm, ctx.val(f"{tag}_nrm"))
                        v.tensor_max(kmx[0:1, 0:1], kmx[0:1, 0:1],
                                     kmx[0:1, 1:2]).then_inc(nrm, 1)
                        ctx.inc(f"{tag}_nrm")
                    v.wait_ge(nmm, nmm0 + 64 + 16)
                    v.tensor_copy(qn2row[0:1, 0:512], psq2[0][:])
                    v.tensor_copy(qn2row[0:1, 512:1024], psq2[1][:]).then_inc(nrm, 1)
                    ctx.inc(f"{tag}_nrm")

                @block.scalar
                def _(s):
                    s.wait_ge(nrm, ctx.val(f"{tag}_nrm"))
                    s.activation(kmx[0:1, 1:2], kmx[0:1, 0:1], AF.Sqrt,
                                 bias=0.0, scale=1.0).then_inc(nac, 1)
                    ctx.inc(f"{tag}_nac")
                    s.wait_ge(nac, ctx.val(f"{tag}_nac"))
                    s.activation(sqrow[:], qn2row[:], AF.Sqrt,
                                 bias=0.0, scale=1.0).then_inc(nac, 1)
                    ctx.inc(f"{tag}_nac")

                @block.vector
                def _(v):
                    v.wait_ge(nac, ctx.val(f"{tag}_nac"))
                    v.tensor_scalar(negb[:], sqrow[:], kmx[0:1, 1:2], -1.01,
                                    ALU.mult, ALU.mult).then_inc(nrm, 1)
                    ctx.inc(f"{tag}_nrm")
            nrm_done = ctx.val(f"{tag}_nrm")

            with ExitStack() as esP:
                pss = [ctx.ps(f"{tag}_pss{i}", [128, 512], es=esP) for i in range(4)]
                psl = ctx.ps(f"{tag}_psl", [1, 512], es=esP)

                for qb in range(2):
                    smm0 = ctx.val(f"{tag}_smm")
                    sex0 = ctx.val(f"{tag}_sex")
                    lmm0 = ctx.val(f"{tag}_lmm")

                    @block.tensor
                    def _(t, qb=qb, smm0=smm0, sex0=sex0):
                        t.wait_ge(nrm, nrm_done)
                        if qb == 1:
                            t.wait_ge(pve, ctx.val(f"{tag}_pve"))
                            t.wait_ge(rle, ctx.val(f"{tag}_rle"))
                        for kt in range(32):
                            if kt >= 4:
                                t.wait_ge(sex, sex0 + kt - 3)
                            for c in range(8):
                                t.matmul(pss[kt % 4][:],
                                         kf[:, c * 4096 + kt * 128:c * 4096 + (kt + 1) * 128],
                                         qsb[:, c * 1024 + qb * 512:c * 1024 + (qb + 1) * 512],
                                         start=(c == 0), stop=False)
                            t.matmul(pss[kt % 4][:], ctx.ones_bf[0:1, 0:128],
                                     negb[0:1, qb * 512:(qb + 1) * 512],
                                     start=False, stop=True).then_inc(smm, 1)
                            ctx.inc(f"{tag}_smm")
                        for kt in range(32):
                            t.wait_ge(sex, sex0 + kt + 1)
                            t.matmul(psl[:], ctx.ones_col[:], pt[kt][:],
                                     start=(kt == 0), stop=(kt == 31)).then_inc(lmm, 1)
                            ctx.inc(f"{tag}_lmm")

                    @block.scalar
                    def _(s, qb=qb, smm0=smm0):
                        for kt in range(32):
                            s.wait_ge(smm, smm0 + kt + 1)
                            s.activation(pt[kt][:], pss[kt % 4][:], AF.Exp,
                                         bias=0.0, scale=1.0).then_inc(sex, 1)
                            ctx.inc(f"{tag}_sex")

                    @block.vector
                    def _(v, qb=qb):
                        v.wait_ge(lmm, ctx.val(f"{tag}_lmm"))
                        v.reciprocal(rlrow[0:1, qb * 512:(qb + 1) * 512],
                                     psl[:]).then_inc(rle, 1)
                        ctx.inc(f"{tag}_rle")

                    for half in range(2):
                        vbase = [ctx.val(f"{tag}_vt{i}") for i in range(4)]
                        pvm0 = ctx.val(f"{tag}_pvm")

                        @block.sync
                        def _(sy, qb=qb, half=half, vbase=vbase, pvm0=pvm0):
                            for kt in range(32):
                                slot = kt % 4
                                if kt >= 4:
                                    sy.wait_ge(pvm, pvm0 + kt - 3)
                                elif pvm0 > 0:
                                    sy.wait_ge(pvm, pvm0)
                                sy.dma_start(
                                    out=vt[slot][:],
                                    in_=vfull_dram[kt * 128:(kt + 1) * 128,
                                                   half * 512:(half + 1) * 512]
                                    ).then_inc(vts[slot], 16)
                                ctx.inc(f"{tag}_vt{slot}", 16)

                        @block.tensor
                        def _(t, qb=qb, half=half, vbase=vbase, smm0=smm0,
                              lmm0=lmm0):
                            if half == 0:
                                t.wait_ge(sex, sex0 + 32)
                            if ctx.val(f"{tag}_pve") > 0:
                                t.wait_ge(pve, ctx.val(f"{tag}_pve"))
                            for kt in range(32):
                                slot = kt % 4
                                t.wait_ge(vts[slot], vbase[slot] + 16 * (kt // 4 + 1))
                                for vc in range(4):
                                    ins = t.matmul(
                                        pss[vc][:],
                                        vt[slot][:, vc * 128:(vc + 1) * 128],
                                        pt[kt][:],
                                        start=(kt == 0), stop=(kt == 31))
                                ins.then_inc(pvm, 1)
                                ctx.inc(f"{tag}_pvm")

                        @block.vector
                        def _(v, qb=qb, half=half):
                            v.wait_ge(pvm, ctx.val(f"{tag}_pvm"))
                            for vc in range(4):
                                vdc = half * 4 + vc
                                v.tensor_copy(
                                    obf[:, vdc * 1024 + qb * 512:vdc * 1024 + (qb + 1) * 512],
                                    pss[vc][:]).then_inc(pve, 1)
                                ctx.inc(f"{tag}_pve")

        with ExitStack() as esP:
            pst = ctx.ps(f"{tag}_pst", [128, 8], es=esP)

            @block.tensor
            def _(t):
                t.wait_ge(rle, ctx.val(f"{tag}_rle"))
                t.wait_ge(pve, ctx.val(f"{tag}_pve"))
                for j in range(8):
                    ins = t.transpose(pst[:, j:j + 1],
                                      rlrow[0:1, j * 128:(j + 1) * 128],
                                      ctx.ident_f[0:1, 0:1])
                ins.then_inc(smm, 1)
                ctx.inc(f"{tag}_smm")

            @block.vector
            def _(v):
                v.wait_ge(smm, ctx.val(f"{tag}_smm"))
                v.tensor_copy(rlT[:], pst[:]).then_inc(rle, 1)
                ctx.inc(f"{tag}_rle")
        rl_done = ctx.val(f"{tag}_rle")

        with ExitStack() as esC:
            wos = [ctx.sb(f"{tag}_wos{i}", [128, 512], BF16, es=esC) for i in range(4)]
            stgo = [ctx.sb(f"{tag}_sgo{i}", [128, 512], F32, es=esC) for i in range(4)]
            tmp = [ctx.sb(f"{tag}_tmp{i}", [128, 512], F32, es=esC) for i in range(8)]
            wls = [ctx.sem(f"{tag}_wl{i}") for i in range(4)]
            wcs = ctx.sem(f"{tag}_wcs")
            sst = ctx.sem(f"{tag}_sst")
            oyss = [ctx.sem(f"{tag}_oys{i}") for i in range(8)]
            och = ctx.sem(f"{tag}_och")
            with ExitStack() as esP:
                pso = [ctx.ps(f"{tag}_pso{i}", [128, 512], es=esP) for i in range(8)]
                sst0 = ctx.val(f"{tag}_sst")
                och0 = ctx.val(f"{tag}_och")
                wcs0 = ctx.val(f"{tag}_wcs")
                wls0 = [ctx.val(f"{tag}_wl{i}") for i in range(4)]
                pv_done = ctx.val(f"{tag}_pve")

                def _wo_trig(eng, sidx):
                    cb, j = divmod(sidx, 8)
                    eng.dma_start(
                        out=stgo[sidx % 4][:],
                        in_=wo_dram[j * 128:(j + 1) * 128,
                                    cb * 512:(cb + 1) * 512]
                        ).then_inc(wls[sidx % 4], 16)
                    ctx.inc(f"{tag}_wl{sidx % 4}", 16)

                @block.sync
                def _(sy):
                    for sidx in range(0, 64, 2):
                        if sidx >= 4:
                            sy.wait_ge(wcs, wcs0 + sidx - 3)
                        _wo_trig(sy, sidx)

                @block.scalar
                def _(s):
                    for sidx in (1, 3):
                        _wo_trig(s, sidx)
                    for sidx in range(64):
                        s.wait_ge(wls[sidx % 4], wls0[sidx % 4]
                                  + 16 * (sidx // 4 + 1))
                        if sidx >= 4:
                            s.wait_ge(sst, sst0 + sidx - 3)
                        s.activation(wos[sidx % 4][:], stgo[sidx % 4][:],
                                     AF.Copy, bias=0.0, scale=1.0
                                     ).then_inc(wcs, 1)
                        ctx.inc(f"{tag}_wcs")
                        so = sidx + 4
                        if so % 2 == 1 and so < 64:
                            s.wait_ge(wcs, wcs0 + so - 3)
                            _wo_trig(s, so)

                @block.gpsimd
                def _(g):
                    g.wait_ge(pve, pv_done)
                    if y_wait is not None:
                        g.wait_ge(ctx.sems[y_wait[0]], y_wait[1])
                    for cb in range(8):
                        for ti in range(8):
                            idx = cb * 8 + ti
                            slot = idx % 8
                            g.wait_ge(och, och0 + idx + 1)
                            g.dma_start(
                                out=y_out[ti * 128:(ti + 1) * 128,
                                          cb * 512:(cb + 1) * 512],
                                in_=tmp[slot][:],
                                accum_op=ALU.add).then_inc(oyss[slot], 16)
                            ctx.inc(f"{tag}_oys{slot}", 16)

                @block.tensor
                def _(t):
                    t.wait_ge(rle, rl_done)
                    for cb in range(8):
                        if cb >= 1:
                            t.wait_ge(och, och0 + 8 * cb)
                        for j in range(8):
                            sidx = cb * 8 + j
                            t.wait_ge(wcs, wcs0 + sidx + 1)
                            for ti in range(8):
                                ins = t.matmul(
                                    pso[ti][:],
                                    obf[:, j * 1024 + ti * 128:j * 1024 + (ti + 1) * 128],
                                    wos[sidx % 4][:],
                                    start=(j == 0), stop=(j == 7))
                            ins.then_inc(sst, 1)
                            ctx.inc(f"{tag}_sst")

                @block.vector
                def _(v):
                    v.wait_ge(rle, rl_done)
                    for cb in range(8):
                        v.wait_ge(sst, sst0 + (cb + 1) * 8)
                        for ti in range(8):
                            idx = cb * 8 + ti
                            slot = idx % 8
                            if idx >= 8:
                                v.wait_ge(oyss[slot], 16 * (idx // 8))
                            v.scalar_tensor_tensor(
                                tmp[slot][:], pso[ti][:], rlT[:, ti:ti + 1],
                                bo0B[:, cb * 512:(cb + 1) * 512],
                                ALU.mult, ALU.add).then_inc(och, 1)
                            ctx.inc(f"{tag}_och")

    ctx.cnt[f"{tag}_done"] = [(f"{tag}_oys{i}", ctx.val(f"{tag}_oys{i}"))
                              for i in range(8)]



def emit_ffn1(ctx, block, xlnT, wgT_dram, wuT_dram, wdT_dram, ag_dram,
              au_dram, ad_dram, h_dram, y_out, tag, xr_wait=None,
              y_wait=None, close_between=None):
    with ExitStack() as esA:
        wg = [ctx.sb(f"{tag}_wg{i}", [128, 8192], BF16, es=esA) for i in range(2)]
        wu = [ctx.sb(f"{tag}_wu{i}", [128, 8192], BF16, es=esA) for i in range(2)]
        g_s = [ctx.sb(f"{tag}_gs{i}", [128, 1024], F32, es=esA) for i in range(2)]
        ht = [ctx.sb(f"{tag}_ht{i}", [128, 1024], BF16, es=esA) for i in range(4)]
        ag_pm = ctx.sb(f"{tag}_agpm", [128, 86], F32, es=esA)
        au_pm = ctx.sb(f"{tag}_aupm", [128, 86], F32, es=esA)
        stga = [ctx.sb(f"{tag}_sga{i}", [128, 1024], F32, es=esA)
                for i in range(8)]

        lga = [ctx.sem(f"{tag}_lga{i}") for i in range(8)]
        cga = ctx.sem(f"{tag}_cga")
        gbd = ctx.sem(f"{tag}_gbd")
        gac = ctx.sem(f"{tag}_gac")
        gdv = ctx.sem(f"{tag}_gdv")
        hsts = [ctx.sem(f"{tag}_hst{i}") for i in range(4)]
        pvv = ctx.sem(f"{tag}_pv")

        @block.sync
        def _(sy):
            sy.dma_start(out=ag_pm[:],
                         in_=ag_dram.rearrange("(c p) -> p c", p=128),
                         allow_slow_non_contiguous=True).then_inc(pvv, 16)
            sy.dma_start(out=au_pm[:],
                         in_=au_dram.rearrange("(c p) -> p c", p=128),
                         allow_slow_non_contiguous=True).then_inc(pvv, 16)
            ctx.inc(f"{tag}_pv", 32)

        with ExitStack() as esP:
            ps = [ctx.ps(f"{tag}_ps{i}", [128, 512], es=esP) for i in range(8)]
            gbd0 = ctx.val(f"{tag}_gbd")
            gac0 = ctx.val(f"{tag}_gac")
            gdv0 = ctx.val(f"{tag}_gdv")
            cga0 = ctx.val(f"{tag}_cga")
            lga0 = [ctx.val(f"{tag}_lga{i}") for i in range(8)]

            def _gu_trig(eng, c):
                ig, j = divmod(c, 16)
                src = wgT_dram if j < 8 else wuT_dram
                jj = j % 8
                eng.dma_start(
                    out=stga[c % 8][:].rearrange("p (a x) -> p a x", a=4),
                    in_=src.rearrange("(a p) i -> p a i", p=128)
                        [:, 4 * jj:4 * jj + 4, ig * 256:(ig + 1) * 256],
                ).then_inc(lga[c % 8], 16)
                ctx.inc(f"{tag}_lga{c % 8}", 16)

            @block.sync
            def _(sy):
                def _h_store(ig):
                    for c2 in range(2):
                        ic = 2 * ig + c2
                        slot = c2 * 2 + ig % 2
                        sy.wait_ge(gdv, gdv0 + ig * 4 + c2 * 2 + 2)
                        sy.dma_start(out=h_dram[ic * 128:(ic + 1) * 128, :],
                                     in_=ht[slot][:]).then_inc(hsts[slot], 16)
                        ctx.inc(f"{tag}_hst{slot}", 16)

                for c in range(0, 688, 2):
                    if c >= 8:
                        sy.wait_ge(cga, cga0 + c - 7)
                    elif cga0 > 0:
                        sy.wait_ge(cga, cga0)
                    _gu_trig(sy, c)
                    if c % 16 == 0 and c >= 32:
                        _h_store(c // 16 - 2)
                for ig in (41, 42):
                    _h_store(ig)

            @block.tensor
            def _(t):
                if xr_wait is not None:
                    t.wait_ge(ctx.sems[xr_wait[0]], xr_wait[1])
                for ig in range(43):
                    par = ig % 2
                    t.wait_ge(cga, cga0 + 16 * (ig + 1))
                    if ig >= 1:
                        t.wait_ge(gac, gac0 + 4 * ig)
                        t.wait_ge(gdv, gdv0 + 4 * ig)
                    for c2 in range(2):
                        for gu in range(2):
                            wt = wg[par] if gu == 0 else wu[par]
                            for th in range(2):
                                bank = c2 * 4 + gu * 2 + th
                                for k in range(32):
                                    ins = t.matmul(
                                        ps[bank][:],
                                        wt[:, k * 256 + c2 * 128:k * 256 + (c2 + 1) * 128],
                                        xlnT[:, k * 1024 + th * 512:k * 1024 + (th + 1) * 512],
                                        start=(k == 0), stop=(k == 31))
                    ins.then_inc(gbd, 1)
                    ctx.inc(f"{tag}_gbd")

            def _gu_silu(s, ig):
                s.wait_ge(gbd, gbd0 + ig + 1)
                if ig >= 1:
                    s.wait_ge(gdv, gdv0 + 4 * ig)
                for c2 in range(2):
                    ic = 2 * ig + c2
                    for th in range(2):
                        s.activation(g_s[c2][:, th * 512:(th + 1) * 512],
                                     ps[c2 * 4 + th][:], AF.Silu, bias=0.0,
                                     scale=ag_pm[:, ic:ic + 1]).then_inc(gac, 1)
                        ctx.inc(f"{tag}_gac")

            @block.scalar
            def _(s):
                s.wait_ge(pvv, ctx.val(f"{tag}_pv"))
                for c in (1, 3, 5, 7):
                    _gu_trig(s, c)
                for ci in range(688):
                    ig, j = divmod(ci, 16)
                    if j == 0 and ig >= 2:
                        s.wait_ge(gbd, gbd0 + ig - 1)
                    s.wait_ge(lga[ci % 8], lga0[ci % 8] + 16 * (ci // 8 + 1))
                    mat = wg[ig % 2] if j < 8 else wu[ig % 2]
                    jj = j % 8
                    s.activation(mat[:, 1024 * jj:1024 * (jj + 1)],
                                 stga[ci % 8][:], AF.Copy,
                                 bias=0.0, scale=1.0).then_inc(cga, 1)
                    ctx.inc(f"{tag}_cga")
                    co = ci + 8
                    if co % 2 == 1 and co < 688:
                        s.wait_ge(cga, cga0 + co - 7)
                        _gu_trig(s, co)
                    if j == 15 and ig >= 1:
                        _gu_silu(s, ig - 1)
                for igl in (42,):
                    _gu_silu(s, igl)

            @block.vector
            def _(v):
                for ig in range(43):
                    for c2 in range(2):
                        ic = 2 * ig + c2
                        slot = c2 * 2 + ig % 2
                        for th in range(2):
                            v.wait_ge(gac, gac0 + ig * 4 + c2 * 2 + th + 1)
                            if ig >= 2:
                                v.wait_ge(hsts[slot], 16 * (ig // 2))
                            v.scalar_tensor_tensor(
                                ht[slot][:, th * 512:(th + 1) * 512],
                                ps[c2 * 4 + 2 + th][:], au_pm[:, ic:ic + 1],
                                g_s[c2][:, th * 512:(th + 1) * 512],
                                ALU.mult, ALU.mult).then_inc(gdv, 1)
                            ctx.inc(f"{tag}_gdv")

        h_done = [(f"{tag}_hst{i}", ctx.val(f"{tag}_hst{i}")) for i in range(4)]

    if close_between is not None:
        close_between.close()
    with ExitStack() as esB:
        halfadB = ctx.sb(f"{tag}_hadB", [128, 4096], F32, es=esB)
        emit_bcast(ctx, block, ad_dram, 4096, halfadB, f"{tag}_ad", scale=0.5)
        ad_done = ctx.val(f"{tag}_ad_done")
        ad_row_ld = ctx.val("bc_ld")
        ad_row_mm = ctx.val("bc_mm")
        hres = ctx.sb(f"{tag}_hres", [128, 43 * 1024], BF16, es=esB)
        wds = [ctx.sb(f"{tag}_wds{i}", [128, 512], BF16, es=esB) for i in range(4)]
        stgb = [ctx.sb(f"{tag}_sgb{i}", [128, 256], F32, es=esB) for i in range(4)]
        tmp = [ctx.sb(f"{tag}_tmp{i}", [128, 512], F32, es=esB) for i in range(8)]

        hl = ctx.sem(f"{tag}_hl")
        ldb = [ctx.sem(f"{tag}_ldb{i}") for i in range(4)]
        cgb = ctx.sem(f"{tag}_cgb")
        dmm = ctx.sem(f"{tag}_dmm")
        dev = ctx.sem(f"{tag}_dev")
        dyss = [ctx.sem(f"{tag}_dys{i}") for i in range(8)]

        with ExitStack() as esP:
            psd = [ctx.ps(f"{tag}_psd{i}", [128, 512], es=esP) for i in range(8)]
            for ih in range(2):
                hl0 = ctx.val(f"{tag}_hl")
                dmm0 = ctx.val(f"{tag}_dmm")
                dev0 = ctx.val(f"{tag}_dev")
                dys0 = [ctx.val(f"{tag}_dys{i}") for i in range(8)]
                cgb0 = ctx.val(f"{tag}_cgb")
                ldb0 = [ctx.val(f"{tag}_ldb{i}") for i in range(4)]

                def _dn_trig(eng, c, ih=ih, cgb0=cgb0):
                    sidx, h = divmod(c, 2)
                    n, i = divmod(sidx, 43)
                    eng.dma_start(
                        out=stgb[c % 4][:],
                        in_=wdT_dram[(ih * 43 + i) * 128:
                                     (ih * 43 + i + 1) * 128,
                                     n * 512 + h * 256:n * 512 + (h + 1) * 256]
                        ).then_inc(ldb[c % 4], 16)
                    ctx.inc(f"{tag}_ldb{c % 4}", 16)

                @block.sync
                def _(sy, ih=ih, dmm0=dmm0):
                    if ih == 0:
                        for sname, sval in h_done:
                            sy.wait_ge(ctx.sems[sname], sval)
                        sy.wait_ge(ctx.sems["bc_ld"], ad_row_ld)
                        sy.wait_ge(ctx.sems["bc_mm"], ad_row_mm)
                    else:
                        sy.wait_ge(ctx.sems[f"{tag}_dmm"], dmm0)
                    for i in range(43):
                        sy.dma_start(out=hres[:, i * 1024:(i + 1) * 1024],
                                     in_=h_dram[(ih * 43 + i) * 128:
                                                (ih * 43 + i + 1) * 128, :]
                                     ).then_inc(hl, 16)
                        ctx.inc(f"{tag}_hl", 16)
                    for c in range(0, 688, 2):
                        if c >= 4:
                            sy.wait_ge(cgb, cgb0 + c - 3)
                        elif cgb0 > 0:
                            sy.wait_ge(cgb, cgb0)
                        _dn_trig(sy, c)

                @block.scalar
                def _(s, ih=ih, dmm0=dmm0, cgb0=cgb0, ldb0=ldb0):
                    for c in (1, 3):
                        if cgb0 > 0:
                            s.wait_ge(cgb, cgb0)
                        _dn_trig(s, c)
                    for ci in range(688):
                        sidx, h = divmod(ci, 2)
                        s.wait_ge(ldb[ci % 4], ldb0[ci % 4] + 16 * (ci // 4 + 1))
                        if h == 0:
                            if sidx >= 4:
                                s.wait_ge(dmm, dmm0 + sidx - 3)
                            elif dmm0 > 0:
                                s.wait_ge(dmm, dmm0)
                        s.activation(wds[sidx % 4][:, h * 256:(h + 1) * 256],
                                     stgb[ci % 4][:], AF.Copy,
                                     bias=0.0, scale=1.0).then_inc(cgb, 1)
                        ctx.inc(f"{tag}_cgb")
                        co = ci + 4
                        if co % 2 == 1 and co < 688:
                            s.wait_ge(cgb, cgb0 + co - 3)
                            _dn_trig(s, co)

                @block.gpsimd
                def _(g, ih=ih, dev0=dev0, dys0=dys0):
                    for n in range(8):
                        for ti in range(8):
                            idx = n * 8 + ti
                            slot = idx % 8
                            g.wait_ge(dev, dev0 + idx + 1)
                            if ih == 1:
                                g.wait_ge(dyss[slot], dys0[slot])
                            g.dma_start(
                                out=y_out[ti * 128:(ti + 1) * 128,
                                          n * 512:(n + 1) * 512],
                                in_=tmp[slot][:],
                                accum_op=ALU.add).then_inc(dyss[slot], 16)
                            ctx.inc(f"{tag}_dys{slot}", 16)

                @block.tensor
                def _(t, ih=ih, hl0=hl0, dmm0=dmm0, dev0=dev0, cgb0=cgb0):
                    if ih == 0:
                        t.wait_ge(ctx.sems["bc_ev"], ad_done)
                    t.wait_ge(hl, hl0 + 16 * 43)
                    for n in range(8):
                        if n >= 1:
                            t.wait_ge(dev, dev0 + 8 * n)
                        elif dev0 > 0:
                            t.wait_ge(dev, dev0)
                        for i in range(43):
                            sidx = n * 43 + i
                            t.wait_ge(cgb, cgb0 + 2 * (sidx + 1))
                            for ti in range(8):
                                ins = t.matmul(
                                    psd[ti][:],
                                    hres[:, i * 1024 + ti * 128:i * 1024 + (ti + 1) * 128],
                                    wds[sidx % 4][:],
                                    start=(i == 0), stop=(i == 42))
                            ins.then_inc(dmm, 1)
                            ctx.inc(f"{tag}_dmm")

                @block.vector
                def _(v, ih=ih, dmm0=dmm0, dys0=dys0):
                    v.wait_ge(ctx.sems["bc_ev"], ad_done)
                    if ih == 0 and y_wait is not None:
                        v.wait_ge(ctx.sems[y_wait[0]], y_wait[1])
                    for n in range(8):
                        v.wait_ge(dmm, dmm0 + 43 * (n + 1))
                        for ti in range(8):
                            idx = n * 8 + ti
                            slot = idx % 8
                            if idx >= 8:
                                v.wait_ge(dyss[slot], dys0[slot] + 16 * (idx // 8))
                            elif dys0[slot] > 0:
                                v.wait_ge(dyss[slot], dys0[slot])
                            v.tensor_mul(tmp[slot][:], psd[ti][:],
                                         halfadB[:, n * 512:(n + 1) * 512]
                                         ).then_inc(dev, 1)
                            ctx.inc(f"{tag}_dev")

    ctx.cnt[f"{tag}_done"] = [(f"{tag}_dys{i}", ctx.val(f"{tag}_dys{i}"))
                              for i in range(8)]



def emit_proj_tok(ctx, block, xlnT, wT_dram, bias_dram, out_dram, F, tag,
                  xr_wait=None):
    blocks = []
    off = 0
    while off < F:
        w = min(512, F - off)
        blocks.append((off, w))
        off += w
    with ExitStack() as es:
        wb = [ctx.sb(f"{tag}_wb{i}", [128, 16384], BF16, es=es) for i in range(2)]
        bB = ctx.sb(f"{tag}_bB", [128, F], F32, es=es)
        ot = [ctx.sb(f"{tag}_ot{i}", [128, 512], BF16, es=es) for i in range(2)]
        emit_bcast(ctx, block, bias_dram, F, bB, f"{tag}_b",
                   extra_waits=([xr_wait] if xr_wait is not None else []))
        b_done = ctx.val(f"{tag}_b_done")

        wpa = [ctx.sem(f"{tag}_wpa{i}") for i in range(2)]
        pmm = ctx.sem(f"{tag}_pmm")
        pev = ctx.sem(f"{tag}_pev")
        psts = [ctx.sem(f"{tag}_pst{i}") for i in range(2)]

        with ExitStack() as esP:
            ps = [ctx.ps(f"{tag}_ps{i}", [128, 512], es=esP) for i in range(8)]
            wpa0 = [ctx.val(f"{tag}_wpa{i}") for i in range(2)]
            pmm0, pev0 = ctx.val(f"{tag}_pmm"), ctx.val(f"{tag}_pev")
            pst0 = [ctx.val(f"{tag}_pst{i}") for i in range(2)]

            @block.gpsimd
            def _(g):
                for fb, (off, w) in enumerate(blocks):
                    par = fb % 2
                    if fb >= 2:
                        g.wait_ge(pmm, pmm0 + fb - 1)
                    for q in range(4):
                        g.dma_start(
                            out=wb[par][:, 0:32 * w].rearrange("p (a x) -> p a x", a=32)
                                [:, q * 8:(q + 1) * 8, :],
                            in_=wT_dram.rearrange("(a p) f -> p a f", p=128)
                                [:, q * 8:(q + 1) * 8, off:off + w],
                        ).then_inc(wpa[par], 16)
                        ctx.inc(f"{tag}_wpa{par}", 16)

            @block.tensor
            def _(t):
                if xr_wait is not None:
                    t.wait_ge(ctx.sems[xr_wait[0]], xr_wait[1])
                t.wait_ge(ctx.sems["bc_ev"], b_done)
                for fb, (off, w) in enumerate(blocks):
                    par = fb % 2
                    t.wait_ge(wpa[par], wpa0[par] + 64 * (fb // 2 + 1))
                    if fb >= 1:
                        t.wait_ge(pev, pev0 + 8 * fb)
                    for k in range(32):
                        for ti in range(8):
                            ins = t.matmul(
                                ps[ti][:, 0:w],
                                xlnT[:, k * 1024 + ti * 128:k * 1024 + (ti + 1) * 128],
                                wb[par][:, k * w:(k + 1) * w],
                                start=(k == 0), stop=(k == 31))
                    ins.then_inc(pmm, 1)
                    ctx.inc(f"{tag}_pmm")

            @block.vector
            def _(v):
                for fb, (off, w) in enumerate(blocks):
                    v.wait_ge(pmm, pmm0 + fb + 1)
                    for ti in range(8):
                        idx = fb * 8 + ti
                        slot = idx % 2
                        if idx >= 2:
                            v.wait_ge(psts[slot], pst0[slot] + 16 * (idx // 2))
                        v.tensor_add(ot[slot][:, 0:w], ps[ti][:, 0:w],
                                     bB[:, off:off + w]).then_inc(pev, 1)
                        ctx.inc(f"{tag}_pev")

            @block.sync
            def _(sy):
                for fb, (off, w) in enumerate(blocks):
                    for ti in range(8):
                        idx = fb * 8 + ti
                        slot = idx % 2
                        sy.wait_ge(pev, pev0 + idx + 1)
                        sy.dma_start(out=out_dram[ti * 128:(ti + 1) * 128,
                                                  off:off + w],
                                     in_=ot[slot][:, 0:w]).then_inc(psts[slot], 16)
                        ctx.inc(f"{tag}_pst{slot}", 16)

    ctx.cnt[f"{tag}_done"] = [(f"{tag}_pst{i}", ctx.val(f"{tag}_pst{i}"))
                              for i in range(2)]



def emit_s5(ctx, block, qc_dram, kvcf_dram, wquT_dram, bqu_dram,
            wkvuT_dram, bkvu_dram, wo1T_dram, bo1_dram, y_out,
            tag="s5", y_wait=None):
    SC = 1.0 / float(np.sqrt(192.0))
    with ExitStack() as esA:
        qcT = ctx.sb(f"{tag}_qcT", [128, 12 * 1024], BF16, es=esA)
        kvT = ctx.sb(f"{tag}_kvT", [128, 4 * 4096], BF16, es=esA)
        krop = ctx.sb(f"{tag}_krop", [65, 4096], BF16, es=esA)
        bquA = ctx.sb(f"{tag}_bquA", [128, 32], F32, es=esA)
        bquB = ctx.sb(f"{tag}_bquB", [64, 32], F32, es=esA)
        bkvK = ctx.sb(f"{tag}_bkvK", [128, 32], F32, es=esA)
        bkvV = ctx.sb(f"{tag}_bkvV", [128, 32], F32, es=esA)
        wo1sb = ctx.sb(f"{tag}_wo1", [128, 4096], BF16, es=esA)
        osum = ctx.sb(f"{tag}_osum", [128, 8 * 128], F32, es=esA)
        ropen2 = ctx.sb(f"{tag}_rn2", [1, 4096], BF16, es=esA)
        kvrt = ctx.sb(f"{tag}_kvrt", [128, 32 * 64], BF16, es=esA)

        ld = ctx.sem(f"{tag}_ld")
        pdv = ctx.sem(f"{tag}_pdv")

        @block.sync
        def _(sy):
            for a in range(12):
                sy.dma_start(out=qcT[:, a * 1024:(a + 1) * 1024],
                             in_=qc_dram[:, a * 128:(a + 1) * 128],
                             transpose=True).then_inc(ld, 16)
                ctx.inc(f"{tag}_ld", 16)
            for a in range(4):
                sy.dma_start(out=kvT[:, a * 4096:(a + 1) * 4096],
                             in_=kvcf_dram[:, a * 128:(a + 1) * 128],
                             transpose=True).then_inc(ld, 16)
                ctx.inc(f"{tag}_ld", 16)
            for q in range(4):
                sy.dma_start(
                    out=kvrt[:].rearrange("p (a x) -> p a x", a=32)
                        [:, q * 8:(q + 1) * 8, :],
                    in_=kvcf_dram.rearrange("(a p) f -> p a f", p=128)
                        [:, q * 8:(q + 1) * 8, 512:576]).then_inc(ld, 16)
                ctx.inc(f"{tag}_ld", 16)
            for dst, src, x in ((bquA, bqu_dram, (192, 0, 128)),
                                (bquB, bqu_dram, (192, 128, 192)),
                                (bkvK, bkvu_dram, (256, 0, 128)),
                                (bkvV, bkvu_dram, (256, 128, 256))):
                per, lo, hi = x
                sy.dma_start(out=dst[:],
                             in_=src.rearrange("(h x) -> x h", x=per)[lo:hi, :],
                             allow_slow_non_contiguous=True).then_inc(ld, 16)
                ctx.inc(f"{tag}_ld", 16)

        gld = ctx.sem(f"{tag}_gld")

        @block.gpsimd
        def _(g):
            g.wait_ge(ctx.sems["g_init"], ctx.cnt["g_ready"])
            g.memset(krop[64:65, :], 1.0).then_inc(pdv, 1)
            ctx.inc(f"{tag}_pdv")
            g.dma_start(out=wo1sb[:], in_=wo1T_dram).then_inc(gld, 16)
            ctx.inc(f"{tag}_gld", 16)
        ld_all = ctx.val(f"{tag}_ld")

        @block.vector
        def _(v):
            v.wait_ge(ld, ld_all)
            v.wait_ge(gld, ctx.val(f"{tag}_gld"))
            v.tensor_scalar_mul(bquA[:], bquA[:], SC).then_inc(pdv, 1)
            ctx.inc(f"{tag}_pdv")
            v.wait_ge(pdv, ctx.val(f"{tag}_pdv"))
            v.tensor_scalar_mul(bquB[:], bquB[:], SC).then_inc(pdv, 1)
            ctx.inc(f"{tag}_pdv")
            v.memset(osum[:], 0.0).then_inc(pdv, 1)
            ctx.inc(f"{tag}_pdv")
        prep_pdv = ctx.val(f"{tag}_pdv")

        with ExitStack() as esB:
            qfA = [ctx.sb(f"{tag}_qfA{i}", [128, 1024], BF16, es=esB) for i in range(2)]
            qfB = [ctx.sb(f"{tag}_qfB{i}", [65, 1024], BF16, es=esB) for i in range(2)]
            knT = [ctx.sb(f"{tag}_knT{i}", [128, 4096], BF16, es=esB) for i in range(2)]
            vft = ctx.sb(f"{tag}_vft", [128, 4096], BF16, es=esB)
            vtok = [ctx.sb(f"{tag}_vtk{i}", [128, 32 * 128], BF16, es=esB) for i in range(2)]
            wqu = [ctx.sb(f"{tag}_wqu{i}", [128, 12 * 192], BF16, es=esB) for i in range(2)]
            wkvu = [ctx.sb(f"{tag}_wkvu{i}", [128, 4 * 256], BF16, es=esB) for i in range(2)]
            pt = [ctx.sb(f"{tag}_pt{i}", [128, 512], BF16, es=esB) for i in range(32)]
            sqs = ctx.sb(f"{tag}_sqs", [128, 1024], BF16, es=esB)
            obuf = ctx.sb(f"{tag}_obuf", [128, 1024], BF16, es=esB)
            qn2row = ctx.sb(f"{tag}_qn2", [1, 1024], F32, es=esB)
            rlrow = ctx.sb(f"{tag}_rlr", [1, 1024], F32, es=esB)
            rlT = ctx.sb(f"{tag}_rlT", [128, 8], F32, es=esB)
            kmx = ctx.sb(f"{tag}_kmx", [1, 4], F32, es=esB)

            wls = [ctx.sem(f"{tag}_wl{i}") for i in range(4)]
            cmm = ctx.sem(f"{tag}_cmm")
            cac = ctx.sem(f"{tag}_cac")
            cdv = ctx.sem(f"{tag}_cdv")
            nmm = ctx.sem(f"{tag}_nmm")
            nac = ctx.sem(f"{tag}_nac")
            nrm = ctx.sem(f"{tag}_nrm")
            smm = ctx.sem(f"{tag}_smm")
            sex = ctx.sem(f"{tag}_sex")
            lmm = ctx.sem(f"{tag}_lmm")
            pvm = ctx.sem(f"{tag}_pvm")
            pve = ctx.sem(f"{tag}_pve")
            rle = ctx.sem(f"{tag}_rle")
            ocv = ctx.sem(f"{tag}_ocv")
            trp = ctx.sem(f"{tag}_trp")

            with ExitStack() as esP:
                psA = [ctx.ps(f"{tag}_psA{i}", [128, 512], es=esP) for i in range(4)]
                psB = ctx.ps(f"{tag}_psB", [128, 1024], BF16, es=esP)
                ps1 = ctx.ps(f"{tag}_ps1", [1, 512], es=esP)

                trp_k0 = ctx.val(f"{tag}_trp")
                cdv_k0 = ctx.val(f"{tag}_cdv")

                @block.tensor
                def _(t):
                    t.wait_ge(ctx.sems["g_init"], ctx.cnt["g_ready"])
                    t.wait_ge(ld, ld_all)
                    for pk in range(4):
                        for j in range(8):
                            blk = pk * 8 + j
                            ins = t.transpose(psB[0:64, j * 128:(j + 1) * 128],
                                              kvrt[:, blk * 64:(blk + 1) * 64],
                                              ctx.ident[:])
                        ins.then_inc(trp, 1)
                        ctx.inc(f"{tag}_trp")
                        t.wait_ge(cdv, cdv_k0 + pk + 1)

                @block.vector
                def _(v):
                    for pk in range(4):
                        v.wait_ge(trp, trp_k0 + pk + 1)
                        v.tensor_copy(krop[0:64, pk * 1024:(pk + 1) * 1024],
                                      psB[0:64, :]).then_inc(cdv, 1)
                        ctx.inc(f"{tag}_cdv")
                krop_cdv = ctx.val(f"{tag}_cdv")

                nac_r0 = ctx.val(f"{tag}_nac")
                nmm_r0 = ctx.val(f"{tag}_nmm")
                nrm_r0 = ctx.val(f"{tag}_nrm")

                @block.scalar
                def _(s):
                    s.wait_ge(ld, ld_all)
                    s.wait_ge(pdv, prep_pdv)
                    s.wait_ge(cdv, krop_cdv)
                    for b in range(4):
                        if b >= 1:
                            s.wait_ge(nmm, nmm_r0 + 2 * b)
                        s.activation(sqs[0:64, :],
                                     krop[0:64, b * 1024:(b + 1) * 1024],
                                     AF.Square, bias=0.0, scale=1.0).then_inc(nac, 1)
                        ctx.inc(f"{tag}_nac")

                @block.tensor
                def _(t):
                    t.wait_ge(ctx.sems["g_init"], ctx.cnt["g_ready"])
                    for i in range(8):
                        b, j = divmod(i, 2)
                        t.wait_ge(nac, nac_r0 + b + 1)
                        if i >= 1:
                            t.wait_ge(nrm, nrm_r0 + i)
                        t.matmul(ps1[:], ctx.ones_c64[:],
                                 sqs[0:64, j * 512:(j + 1) * 512],
                                 start=True, stop=True).then_inc(nmm, 1)
                        ctx.inc(f"{tag}_nmm")

                @block.vector
                def _(v):
                    for i in range(8):
                        v.wait_ge(nmm, nmm_r0 + i + 1)
                        v.tensor_copy(ropen2[0:1, i * 512:(i + 1) * 512],
                                      ps1[:]).then_inc(nrm, 1)
                        ctx.inc(f"{tag}_nrm")
                rope_nrm = ctx.val(f"{tag}_nrm")

                for h in range(32):
                    par = h % 2
                    wl0 = ctx.val(f"{tag}_wl{par}")
                    cmm0 = ctx.val(f"{tag}_cmm")
                    cac0 = ctx.val(f"{tag}_cac")
                    cdv0 = ctx.val(f"{tag}_cdv")
                    pve_pre = ctx.val(f"{tag}_pve")
                    rle_pre = ctx.val(f"{tag}_rle")
                    ocv_pre = ctx.val(f"{tag}_ocv")

                    @block.gpsimd
                    def _(g, h=h, par=par, cmm0=cmm0):
                        if h >= 2:
                            g.wait_ge(cmm, cmm0 - 20)
                        for half2 in range(2):
                            g.dma_start(
                                out=wqu[par][:].rearrange("p (a x) -> p a x", a=12)
                                    [:, half2 * 6:(half2 + 1) * 6, :],
                                in_=wquT_dram.rearrange("(a p) f -> p a f", p=128)
                                    [:, half2 * 6:(half2 + 1) * 6,
                                     h * 192:(h + 1) * 192],
                            ).then_inc(wls[par], 16)
                            ctx.inc(f"{tag}_wl{par}", 16)
                        g.dma_start(
                            out=wkvu[par][:].rearrange("p (a x) -> p a x", a=4),
                            in_=wkvuT_dram.rearrange("(a p) f -> p a f", p=128)
                                [:, :, h * 256:(h + 1) * 256],
                        ).then_inc(wls[par], 16)
                        ctx.inc(f"{tag}_wl{par}", 16)

                    @block.tensor
                    def _(t, h=h, par=par, wl0=wl0, cac0=cac0,
                          pve_pre=pve_pre, ocv_pre=ocv_pre):
                        t.wait_ge(wls[par], wl0 + 48)
                        if h >= 1:
                            t.wait_ge(cac, cac0)
                        if pve_pre > 0:
                            t.wait_ge(pve, pve_pre)
                        else:
                            t.wait_ge(nrm, rope_nrm)
                        if ocv_pre > 0:
                            t.wait_ge(ocv, ocv_pre)
                        g_i = 0
                        for dc in range(2):
                            dw = 128 if dc == 0 else 64
                            for tb in range(2):
                                if g_i >= 4:
                                    t.wait_ge(cac, cac0 + g_i - 3)
                                for a in range(12):
                                    ins = t.matmul(
                                        psA[g_i % 4][0:dw, :],
                                        wqu[par][:, a * 192 + dc * 128:
                                                 a * 192 + dc * 128 + dw],
                                        qcT[:, a * 1024 + tb * 512:
                                            a * 1024 + (tb + 1) * 512],
                                        start=(a == 0), stop=(a == 11))
                                ins.then_inc(cmm, 1)
                                ctx.inc(f"{tag}_cmm")
                                g_i += 1
                        for prod in range(2):
                            for tb in range(8):
                                if g_i >= 4:
                                    t.wait_ge(cac, cac0 + g_i - 3)
                                for a in range(4):
                                    ins = t.matmul(
                                        psA[g_i % 4][:],
                                        wkvu[par][:, a * 256 + prod * 128:
                                                  a * 256 + (prod + 1) * 128],
                                        kvT[:, a * 4096 + tb * 512:
                                            a * 4096 + (tb + 1) * 512],
                                        start=(a == 0), stop=(a == 3))
                                ins.then_inc(cmm, 1)
                                ctx.inc(f"{tag}_cmm")
                                g_i += 1

                    @block.scalar
                    def _(s, h=h, par=par, cmm0=cmm0):
                        g_i = 0
                        for dc in range(2):
                            dw = 128 if dc == 0 else 64
                            bias = bquA if dc == 0 else bquB
                            dst = qfA[par] if dc == 0 else qfB[par]
                            for tb in range(2):
                                s.wait_ge(cmm, cmm0 + g_i + 1)
                                s.activation(dst[0:dw, tb * 512:(tb + 1) * 512],
                                             psA[g_i % 4][0:dw, :], AF.Identity,
                                             bias=bias[0:dw, h:h + 1],
                                             scale=ctx.consts[0:dw, 4:5]
                                             ).then_inc(cac, 1)
                                ctx.inc(f"{tag}_cac")
                                g_i += 1
                        for prod in range(2):
                            bias = bkvK if prod == 0 else bkvV
                            for tb in range(8):
                                s.wait_ge(cmm, cmm0 + g_i + 1)
                                dst = knT[par] if prod == 0 else vft
                                s.activation(dst[:, tb * 512:(tb + 1) * 512],
                                             psA[g_i % 4][:], AF.Identity,
                                             bias=bias[:, h:h + 1],
                                             scale=1.0).then_inc(cac, 1)
                                ctx.inc(f"{tag}_cac")
                                g_i += 1
                    cac_prod = ctx.val(f"{tag}_cac")

                    @block.tensor
                    def _(t, h=h, cac_prod=cac_prod, cdv0=cdv0,
                          ocv_pre=ocv_pre):
                        t.wait_ge(cac, cac_prod)
                        if ocv_pre > 0:
                            t.wait_ge(ocv, ocv_pre)
                        for pk in range(4):
                            if pk >= 1:
                                t.wait_ge(cdv, cdv0 + pk)
                            for j in range(8):
                                blk = pk * 8 + j
                                ins = t.transpose(psB[:, j * 128:(j + 1) * 128],
                                                  vft[:, blk * 128:(blk + 1) * 128],
                                                  ctx.ident[:])
                            ins.then_inc(trp, 1)
                            ctx.inc(f"{tag}_trp")

                    trp0 = ctx.val(f"{tag}_trp") - 4

                    @block.vector
                    def _(v, h=h, par=par, trp0=trp0):
                        for pk in range(4):
                            v.wait_ge(trp, trp0 + pk + 1)
                            v.tensor_copy(
                                vtok[par][:, pk * 1024:(pk + 1) * 1024],
                                psB[:]).then_inc(cdv, 1)
                            ctx.inc(f"{tag}_cdv")

                    nac0 = ctx.val(f"{tag}_nac")
                    nmm0 = ctx.val(f"{tag}_nmm")
                    nrm0 = ctx.val(f"{tag}_nrm")

                    @block.scalar
                    def _(s, h=h, par=par, nmm0=nmm0):
                        for q2 in range(2):
                            if q2 == 1:
                                s.wait_ge(nmm, nmm0 + 1)
                            s.activation(sqs[:, 0:512],
                                         qfA[par][:, q2 * 512:(q2 + 1) * 512],
                                         AF.Square, bias=0.0, scale=1.0)
                            s.activation(sqs[0:64, 512:1024],
                                         qfB[par][0:64, q2 * 512:(q2 + 1) * 512],
                                         AF.Square, bias=0.0, scale=1.0
                                         ).then_inc(nac, 1)
                            ctx.inc(f"{tag}_nac")
                        for b in range(8):
                            s.wait_ge(nmm, nmm0 + 2 + b)
                            s.activation(sqs[:, 0:512],
                                         knT[par][:, b * 512:(b + 1) * 512],
                                         AF.Square, bias=0.0, scale=1.0
                                         ).then_inc(nac, 1)
                            ctx.inc(f"{tag}_nac")

                    @block.tensor
                    def _(t, h=h, nac0=nac0, nrm0=nrm0, rle_pre=rle_pre):
                        if rle_pre > 0:
                            t.wait_ge(rle, rle_pre)
                        for q2 in range(2):
                            t.wait_ge(nac, nac0 + q2 + 1)
                            if q2 >= 1:
                                t.wait_ge(nrm, nrm0 + 1)
                            t.matmul(ps1[:], ctx.ones_col[:], sqs[:, 0:512],
                                     start=True, stop=False)
                            t.matmul(ps1[:], ctx.ones_c64[:], sqs[0:64, 512:1024],
                                     start=False, stop=True).then_inc(nmm, 1)
                            ctx.inc(f"{tag}_nmm")
                        for b in range(8):
                            t.wait_ge(nac, nac0 + 2 + b + 1)
                            t.wait_ge(nrm, (nrm0 + 2) if b == 0 else (nrm0 + 2 * b + 2))
                            t.matmul(ps1[:], ctx.ones_col[:], sqs[:, 0:512],
                                     start=True, stop=False)
                            t.matmul(ps1[:], ctx.ones_bf[0:1, 0:1],
                                     ropen2[0:1, b * 512:(b + 1) * 512],
                                     start=False, stop=True).then_inc(nmm, 1)
                            ctx.inc(f"{tag}_nmm")

                    @block.vector
                    def _(v, h=h, nmm0=nmm0, nrm0=nrm0):
                        for q2 in range(2):
                            v.wait_ge(nmm, nmm0 + q2 + 1)
                            v.tensor_copy(qn2row[0:1, q2 * 512:(q2 + 1) * 512],
                                          ps1[:]).then_inc(nrm, 1)
                            ctx.inc(f"{tag}_nrm")
                        v.memset(kmx[0:1, 0:1], 0.0).then_inc(nrm, 1)
                        ctx.inc(f"{tag}_nrm")
                        for b in range(8):
                            v.wait_ge(nmm, nmm0 + 2 + b + 1)
                            v.wait_ge(nrm, ctx.val(f"{tag}_nrm"))
                            v.reduce_max(kmx[0:1, 1:2], ps1[:],
                                         mybir.AxisListType.X).then_inc(nrm, 1)
                            ctx.inc(f"{tag}_nrm")
                            v.wait_ge(nrm, ctx.val(f"{tag}_nrm"))
                            v.tensor_max(kmx[0:1, 0:1], kmx[0:1, 0:1],
                                         kmx[0:1, 1:2]).then_inc(nrm, 1)
                            ctx.inc(f"{tag}_nrm")

                    @block.scalar
                    def _(s, h=h):
                        s.wait_ge(nrm, ctx.val(f"{tag}_nrm"))
                        s.activation(kmx[0:1, 2:3], kmx[0:1, 0:1], AF.Sqrt,
                                     bias=0.0, scale=1.0).then_inc(nac, 1)
                        ctx.inc(f"{tag}_nac")
                        s.wait_ge(nac, ctx.val(f"{tag}_nac"))
                        s.activation(qn2row[:], qn2row[:], AF.Sqrt,
                                     bias=0.0, scale=1.0).then_inc(nac, 1)
                        ctx.inc(f"{tag}_nac")

                    @block.vector
                    def _(v, h=h, par=par):
                        v.wait_ge(nac, ctx.val(f"{tag}_nac"))
                        v.tensor_scalar(qfB[par][64:65, :], qn2row[:],
                                        kmx[0:1, 2:3], -1.01,
                                        ALU.mult, ALU.mult).then_inc(nrm, 1)
                        ctx.inc(f"{tag}_nrm")
                    negb_nrm = ctx.val(f"{tag}_nrm")

                    for qb in range(2):
                        smm0 = ctx.val(f"{tag}_smm")
                        sex0 = ctx.val(f"{tag}_sex")
                        pvm0 = ctx.val(f"{tag}_pvm")
                        lmm0 = ctx.val(f"{tag}_lmm")
                        rle0 = ctx.val(f"{tag}_rle")
                        pve0 = ctx.val(f"{tag}_pve")

                        @block.tensor
                        def _(t, h=h, par=par, qb=qb, smm0=smm0, sex0=sex0,
                              pve0=pve0, rle0=rle0, cdv0=cdv0):
                            t.wait_ge(nrm, negb_nrm)
                            if pve0 > 0:
                                t.wait_ge(pve, pve0)
                            if qb == 0:
                                t.wait_ge(cdv, cdv0 + 4)
                            for kt in range(32):
                                if kt >= 3:
                                    t.wait_ge(sex, sex0 + kt - 2)
                                t.matmul(psA[kt % 3][:],
                                         knT[par][:, kt * 128:(kt + 1) * 128],
                                         qfA[par][:, qb * 512:(qb + 1) * 512],
                                         start=True, stop=False)
                                t.matmul(psA[kt % 3][:],
                                         krop[:, kt * 128:(kt + 1) * 128],
                                         qfB[par][:, qb * 512:(qb + 1) * 512],
                                         start=False, stop=True).then_inc(smm, 1)
                                ctx.inc(f"{tag}_smm")
                            if rle0 > 0:
                                t.wait_ge(rle, rle0)
                            for kt in range(32):
                                t.wait_ge(sex, sex0 + kt + 1)
                                t.matmul(ps1[:], ctx.ones_col[:], pt[kt][:],
                                         start=(kt == 0), stop=(kt == 31)
                                         ).then_inc(lmm, 1)
                                ctx.inc(f"{tag}_lmm")
                            for kt in range(32):
                                ins = t.matmul(
                                    psA[3][:],
                                    vtok[par][:, kt * 128:(kt + 1) * 128],
                                    pt[kt][:],
                                    start=(kt == 0), stop=(kt == 31))
                            ins.then_inc(pvm, 1)
                            ctx.inc(f"{tag}_pvm")

                        @block.scalar
                        def _(s, h=h, qb=qb, smm0=smm0):
                            for kt in range(32):
                                s.wait_ge(smm, smm0 + kt + 1)
                                s.activation(pt[kt][:], psA[kt % 3][:], AF.Exp,
                                             bias=0.0, scale=1.0).then_inc(sex, 1)
                                ctx.inc(f"{tag}_sex")

                        @block.vector
                        def _(v, h=h, qb=qb, pvm0=pvm0, lmm0=lmm0):
                            v.wait_ge(lmm, lmm0 + 32)
                            v.reciprocal(rlrow[0:1, qb * 512:(qb + 1) * 512],
                                         ps1[:]).then_inc(rle, 1)
                            ctx.inc(f"{tag}_rle")
                            v.wait_ge(pvm, pvm0 + 1)
                            v.tensor_copy(obuf[:, qb * 512:(qb + 1) * 512],
                                          psA[3][:]).then_inc(pve, 1)
                            ctx.inc(f"{tag}_pve")

                    rle_h = ctx.val(f"{tag}_rle")
                    pve_h = ctx.val(f"{tag}_pve")
                    trp_h = ctx.val(f"{tag}_trp")
                    ocv_h = ctx.val(f"{tag}_ocv")

                    @block.tensor
                    def _(t, h=h, rle_h=rle_h, pve_h=pve_h, ocv_h=ocv_h):
                        t.wait_ge(rle, rle_h)
                        if ocv_h > 0:
                            t.wait_ge(ocv, ocv_h)
                        for j in range(8):
                            ins = t.transpose(psA[0][:, j:j + 1],
                                              rlrow[0:1, j * 128:(j + 1) * 128],
                                              ctx.ident_f[0:1, 0:1])
                        ins.then_inc(trp, 1)
                        ctx.inc(f"{tag}_trp")
                        t.wait_ge(pve, pve_h)
                        for j in range(8):
                            ins = t.transpose(psB[:, j * 128:(j + 1) * 128],
                                              obuf[:, j * 128:(j + 1) * 128],
                                              ctx.ident[:])
                        ins.then_inc(trp, 1)
                        ctx.inc(f"{tag}_trp")

                    trp_o = ctx.val(f"{tag}_trp")

                    @block.vector
                    def _(v, h=h, trp_o=trp_o):
                        v.wait_ge(trp, trp_o - 1)
                        v.tensor_copy(rlT[:], psA[0][:, 0:8]).then_inc(ocv, 1)
                        ctx.inc(f"{tag}_ocv")
                        v.wait_ge(trp, trp_o)
                        for ti in range(8):
                            v.wait_ge(ocv, ctx.val(f"{tag}_ocv"))
                            v.scalar_tensor_tensor(
                                osum[:, ti * 128:(ti + 1) * 128],
                                psB[:, ti * 128:(ti + 1) * 128],
                                rlT[:, ti:ti + 1],
                                osum[:, ti * 128:(ti + 1) * 128],
                                ALU.mult, ALU.add).then_inc(ocv, 1)
                            ctx.inc(f"{tag}_ocv")

        ocv_done = ctx.val(f"{tag}_ocv")
        with ExitStack() as esC:
            bo1B = ctx.sb(f"{tag}_bo1B", [128, 4096], F32, es=esC)
            emit_bcast(ctx, block, bo1_dram, 4096, bo1B, f"{tag}_bo1",
                       extra_waits=[(f"{tag}_ocv", ctx.val(f"{tag}_ocv"))])
            bo1_done = ctx.val(f"{tag}_bo1_done")
            of = ctx.sb(f"{tag}_of", [128, 1024], BF16, es=esC)
            tmp = [ctx.sb(f"{tag}_tmp{i}", [128, 512], F32, es=esC) for i in range(8)]
            omm = ctx.sem(f"{tag}_omm")
            oev = ctx.sem(f"{tag}_oev")
            och = ctx.sem(f"{tag}_och")
            oyss = [ctx.sem(f"{tag}_oys{i}") for i in range(8)]
            omm0 = ctx.val(f"{tag}_omm")
            oev0 = ctx.val(f"{tag}_oev")
            with ExitStack() as esP:
                psT = ctx.ps(f"{tag}_psT", [128, 1024], F32, es=esP)

                @block.tensor
                def _(t):
                    t.wait_ge(ctx.sems[f"{tag}_ocv"], ocv_done)
                    t.wait_ge(ctx.sems["bc_ev"], bo1_done)
                    for j in range(8):
                        ins = t.transpose(
                            psT[:, (j // 4) * 512 + (j % 4) * 128:
                                (j // 4) * 512 + (j % 4) * 128 + 128],
                            osum[:, j * 128:(j + 1) * 128],
                            ctx.ident_f[:])
                        if j in (3, 7):
                            ins.then_inc(omm, 1)
                            ctx.inc(f"{tag}_omm")

                @block.vector
                def _(v):
                    for half in range(2):
                        v.wait_ge(omm, omm0 + half + 1)
                        v.tensor_copy(of[:, half * 512:(half + 1) * 512],
                                      psT[:, half * 512:(half + 1) * 512]
                                      ).then_inc(oev, 1)
                        ctx.inc(f"{tag}_oev")
            of_oev = ctx.val(f"{tag}_oev")
            omm1 = ctx.val(f"{tag}_omm")
            oev1 = ctx.val(f"{tag}_oev")

            with ExitStack() as esP:
                psO = [ctx.ps(f"{tag}_psO{i}", [128, 512], es=esP) for i in range(8)]
                och0 = ctx.val(f"{tag}_och")

                @block.tensor
                def _(t):
                    t.wait_ge(oev, of_oev)
                    for cb in range(8):
                        if cb >= 1:
                            t.wait_ge(och, och0 + 8 * cb)
                        for ti in range(8):
                            ins = t.matmul(psO[ti][:],
                                           of[:, ti * 128:(ti + 1) * 128],
                                           wo1sb[:, cb * 512:(cb + 1) * 512],
                                           start=True, stop=True)
                            if ti == 7:
                                ins.then_inc(omm, 1)
                                ctx.inc(f"{tag}_omm")

                @block.vector
                def _(v):
                    v.wait_ge(ctx.sems["bc_ev"], bo1_done)
                    for cb in range(8):
                        v.wait_ge(omm, omm1 + cb + 1)
                        for ti in range(8):
                            idx = cb * 8 + ti
                            slot = idx % 8
                            if idx >= 8:
                                v.wait_ge(oyss[slot], 16 * (idx // 8))
                            v.scalar_tensor_tensor(
                                tmp[slot][:], psO[ti][:], 1.0,
                                bo1B[:, cb * 512:(cb + 1) * 512],
                                ALU.mult, ALU.add).then_inc(och, 1)
                            ctx.inc(f"{tag}_och")

                @block.gpsimd
                def _(g):
                    if y_wait is not None:
                        g.wait_ge(ctx.sems[y_wait[0]], y_wait[1])
                    for cb in range(8):
                        for ti in range(8):
                            idx = cb * 8 + ti
                            slot = idx % 8
                            g.wait_ge(och, och0 + idx + 1)
                            g.dma_start(
                                out=y_out[ti * 128:(ti + 1) * 128,
                                          cb * 512:(cb + 1) * 512],
                                in_=tmp[slot][:],
                                accum_op=ALU.add).then_inc(oyss[slot], 16)
                            ctx.inc(f"{tag}_oys{slot}", 16)

    ctx.cnt[f"{tag}_done"] = [(f"{tag}_oys{i}", ctx.val(f"{tag}_oys{i}"))
                              for i in range(8)]



def emit_ycopy(ctx, block, src, dst, tag="ycp", queue="scalar"):
    cp = ctx.sem(tag)

    def _emit(eng):
        for ti in range(8):
            eng.dma_start(out=dst[ti * 128:(ti + 1) * 128, :],
                          in_=src[ti * 128:(ti + 1) * 128, :]).then_inc(cp, 16)
            ctx.inc(tag, 16)

    if queue == "scalar":
        @block.scalar
        def _(s):
            _emit(s)
    else:
        @block.gpsimd
        def _(g):
            _emit(g)


def build_L1():
    nc = bass.Bass("TRN2", target_bir_lowering=False, debug=False)
    x = nc.dram_tensor("x", [1024, 4096], F32, kind="ExternalInput").ap()
    qtg = nc.dram_tensor("qtg", [1024, 3, 4096], I32, kind="ExternalInput").ap()
    zp0 = nc.dram_tensor("zp0", [4096], F32, kind="ExternalInput").ap()
    sc0 = nc.dram_tensor("sc0", [4096], F32, kind="ExternalInput").ap()
    b0 = nc.dram_tensor("b0", [4096], F32, kind="ExternalInput").ap()
    wqkv = nc.dram_tensor("wqkv", [4096, 3072], F32, kind="ExternalInput").ap()
    bqkv = nc.dram_tensor("bqkv", [3072], F32, kind="ExternalInput").ap()

    y1 = nc.dram_tensor("y1", [1024, 4096], F32, kind="ExternalOutput").ap()
    w0t = nc.dram_tensor("w0t", [4096, 4096], BF16, kind="ExternalOutput").ap()
    qfeat = nc.dram_tensor("qfeat", [1024, 1024], BF16, kind="ExternalOutput").ap()
    kown = nc.dram_tensor("kown", [1024, 1024], BF16, kind="ExternalOutput").ap()
    vown = nc.dram_tensor("vown", [1024, 1024], BF16, kind="ExternalOutput").ap()

    ctx = Ctx(nc)
    with nc.Block() as block:
        common_setup(ctx, block)
        emit_ycopy(ctx, block, x, y1, queue="gpsimd")
        ctx.xlnT = ctx.sb("xlnT", [128, 32 * 1024], BF16)
        halfb0B = ctx.sb("halfb0B", [128, 4096], F32)
        emit_bcast(ctx, block, b0, 4096, halfb0B, "hb0", scale=0.5)

        es_f = ExitStack()
        f0bufs = alloc_ffn0_bufs(ctx, es_f, ntmp=6)
        es_ln = ExitStack()
        lnbufs = alloc_ln_bufs(ctx, es_ln)
        es_dq = ExitStack()

        with nc.named_scope("dq"):
            emit_dequant(ctx, block, es_dq, qtg, zp0, sc0, w0t, tag="dq")
        with nc.named_scope("ln1"):
            emit_ln(ctx, block, lnbufs, x, ctx.xlnT, tag="ln")
        emit_ffn0(ctx, block, f0bufs, ctx.xlnT, w0t, halfb0B, y1, tag="s1",
                  xr_wait=("ln_xr", ctx.val("ln_done")),
                  w_wait=ctx.cnt["dq_done"],
                  b_wait=ctx.val("hb0_done"), y_wait=("ycp", 128))
        es_dq.close()
        emit_fence(ctx, block, ctx.cnt["s1_done"] + ctx.cnt["dq_done"])
        emit_ln(ctx, block, lnbufs, y1, ctx.xlnT, tag="ln",
                start_waits=ctx.cnt["s1_done"])
        es_ln.close()
        emit_fence(ctx, block, [("ln_xr", ctx.val("ln_xr")),
                                ("ln_ac", ctx.val("ln_ac")),
                                ("ln_pe", ctx.val("ln_pe"))])

        es_q = ExitStack()
        emit_qkv(ctx, block, es_q, ctx.xlnT, wqkv, bqkv, qfeat, kown, vown,
                 tag="qkv",
                 start_waits=[("ln_pe", ctx.val("ln_pe")),
                              ("ln_ac", ctx.val("ln_ac"))],
                 xr_wait=("ln_xr", ctx.val("ln_done")))

        @block.sync
        def _(sy):
            for sname, sval in ctx.cnt["s1_done"] + ctx.cnt["qkv_done"]:
                sy.wait_ge(ctx.sems[sname], sval)
            for sn, sv in ctx.cnt["dq_done"]:
                sy.wait_ge(ctx.sems[sn], sv)
        es_q.close()
        es_f.close()
    ctx.close()
    return nc


def build_L2():
    nc = bass.Bass("TRN2", target_bir_lowering=False, debug=False)
    y1 = nc.dram_tensor("y1", [1024, 4096], F32, kind="ExternalInput").ap()
    qfeat = nc.dram_tensor("qfeat", [1024, 1024], BF16, kind="ExternalInput").ap()
    kfull = nc.dram_tensor("kfull", [4096, 1024], BF16, kind="ExternalInput").ap()
    vfull = nc.dram_tensor("vfull", [4096, 1024], BF16, kind="ExternalInput").ap()
    wo0 = nc.dram_tensor("wo0", [1024, 4096], F32, kind="ExternalInput").ap()
    bo0 = nc.dram_tensor("bo0", [4096], F32, kind="ExternalInput").ap()
    w0t = nc.dram_tensor("w0t", [4096, 4096], BF16, kind="ExternalInput").ap()
    b0 = nc.dram_tensor("b0", [4096], F32, kind="ExternalInput").ap()
    wgT = nc.dram_tensor("wgT", [4096, 11008], F32, kind="ExternalInput").ap()
    wuT = nc.dram_tensor("wuT", [4096, 11008], F32, kind="ExternalInput").ap()
    wdT = nc.dram_tensor("wdT", [11008, 4096], F32, kind="ExternalInput").ap()
    ag = nc.dram_tensor("ag", [11008], F32, kind="ExternalInput").ap()
    au = nc.dram_tensor("au", [11008], F32, kind="ExternalInput").ap()
    ad = nc.dram_tensor("ad", [4096], F32, kind="ExternalInput").ap()
    wqcT = nc.dram_tensor("wqcT", [4096, 1536], F32, kind="ExternalInput").ap()
    bqc = nc.dram_tensor("bqc", [1536], F32, kind="ExternalInput").ap()
    wkvcT = nc.dram_tensor("wkvcT", [4096, 576], F32, kind="ExternalInput").ap()
    bkvc = nc.dram_tensor("bkvc", [576], F32, kind="ExternalInput").ap()

    y2 = nc.dram_tensor("y2", [1024, 4096], F32, kind="ExternalOutput").ap()
    qc = nc.dram_tensor("qc", [1024, 1536], BF16, kind="ExternalOutput").ap()
    kvc = nc.dram_tensor("kvc", [1024, 576], BF16, kind="ExternalOutput").ap()
    h_dram = nc.dram_tensor("h_scr", [11008, 1024], BF16).ap()

    ctx = Ctx(nc)
    with nc.Block() as block:
        common_setup(ctx, block)
        emit_ycopy(ctx, block, y1, y2)
        emit_s2attn(ctx, block, qfeat, kfull, vfull, wo0, bo0, y2, tag="s2",
                    y_wait=("ycp", 128))

        emit_fence(ctx, block, ctx.cnt["s2_done"])
        ctx.xlnT = ctx.sb("xlnT", [128, 32 * 1024], BF16)
        es_ln = ExitStack()
        lnbufs = alloc_ln_bufs(ctx, es_ln)

        es_b1 = ExitStack()
        halfb0B = ctx.sb("halfb0B", [128, 4096], F32, es=es_b1)
        emit_bcast(ctx, block, b0, 4096, halfb0B, "hb0", scale=0.5)
        f0bufs = alloc_ffn0_bufs2(ctx, es_b1)
        emit_ln(ctx, block, lnbufs, y2, ctx.xlnT, tag="ln",
                start_waits=ctx.cnt["s2_done"])
        emit_ffn0(ctx, block, f0bufs, ctx.xlnT, w0t, halfb0B, y2, tag="s3",
                  xr_wait=("ln_xr", ctx.val("ln_done")),
                  b_wait=ctx.val("hb0_done"))
        es_b1.close()
        emit_fence(ctx, block, ctx.cnt["s3_done"])

        emit_ln(ctx, block, lnbufs, y2, ctx.xlnT, tag="ln",
                start_waits=ctx.cnt["s3_done"])
        emit_ffn1(ctx, block, ctx.xlnT, wgT, wuT, wdT, ag, au, ad, h_dram,
                  y2, tag="s4", xr_wait=("ln_xr", ctx.val("ln_done")),
                  close_between=es_ln)
        emit_fence(ctx, block, ctx.cnt["s4_done"])
        es_ln2 = ExitStack()
        lnbufs2 = alloc_ln_bufs(ctx, es_ln2, pfx="lnb")
        emit_ln(ctx, block, lnbufs2, y2, ctx.xlnT, tag="ln",
                start_waits=ctx.cnt["s4_done"])
        es_ln2.close()
        emit_fence(ctx, block, [("ln_xr", ctx.val("ln_xr")),
                                ("ln_ac", ctx.val("ln_ac")),
                                ("ln_pe", ctx.val("ln_pe"))])

        emit_proj_tok(ctx, block, ctx.xlnT, wqcT, bqc, qc, 1536, tag="pq",
                      xr_wait=("ln_xr", ctx.val("ln_done")))
        emit_fence(ctx, block, ctx.cnt["pq_done"]
                   + [("pq_pmm", ctx.val("pq_pmm")),
                      ("pq_pev", ctx.val("pq_pev")),
                      ("bc_ev", ctx.val("bc_ev"))])
        emit_proj_tok(ctx, block, ctx.xlnT, wkvcT, bkvc, kvc, 576, tag="pk",
                      xr_wait=("ln_xr", ctx.val("ln_done")))

        @block.sync
        def _(sy):
            for sname, sval in (ctx.cnt["s4_done"] + ctx.cnt["pq_done"]
                                + ctx.cnt["pk_done"]):
                sy.wait_ge(ctx.sems[sname], sval)
    ctx.close()
    return nc


def build_L3():
    nc = bass.Bass("TRN2", target_bir_lowering=False, debug=False)
    y2 = nc.dram_tensor("y2", [1024, 4096], F32, kind="ExternalInput").ap()
    qc = nc.dram_tensor("qc", [1024, 1536], BF16, kind="ExternalInput").ap()
    kvcf = nc.dram_tensor("kvcf", [4096, 576], BF16, kind="ExternalInput").ap()
    wquT = nc.dram_tensor("wquT", [1536, 6144], F32, kind="ExternalInput").ap()
    bqu = nc.dram_tensor("bqu", [6144], F32, kind="ExternalInput").ap()
    wkvuT = nc.dram_tensor("wkvuT", [512, 8192], F32, kind="ExternalInput").ap()
    bkvu = nc.dram_tensor("bkvu", [8192], F32, kind="ExternalInput").ap()
    wo1T = nc.dram_tensor("wo1T", [128, 4096], F32, kind="ExternalInput").ap()
    bo1 = nc.dram_tensor("bo1", [4096], F32, kind="ExternalInput").ap()
    wgT = nc.dram_tensor("wgT", [4096, 11008], F32, kind="ExternalInput").ap()
    wuT = nc.dram_tensor("wuT", [4096, 11008], F32, kind="ExternalInput").ap()
    wdT = nc.dram_tensor("wdT", [11008, 4096], F32, kind="ExternalInput").ap()
    ag = nc.dram_tensor("ag", [11008], F32, kind="ExternalInput").ap()
    au = nc.dram_tensor("au", [11008], F32, kind="ExternalInput").ap()
    ad = nc.dram_tensor("ad", [4096], F32, kind="ExternalInput").ap()

    yout = nc.dram_tensor("yout", [1024, 4096], F32, kind="ExternalOutput").ap()
    h_dram = nc.dram_tensor("h_scr", [11008, 1024], BF16).ap()

    ctx = Ctx(nc)
    with nc.Block() as block:
        common_setup(ctx, block)
        emit_ycopy(ctx, block, y2, yout)
        emit_s5(ctx, block, qc, kvcf, wquT, bqu, wkvuT, bkvu, wo1T, bo1,
                yout, tag="s5", y_wait=("ycp", 128))
        emit_fence(ctx, block, ctx.cnt["s5_done"])

        ctx.xlnT = ctx.sb("xlnT", [128, 32 * 1024], BF16)
        es_ln = ExitStack()
        lnbufs = alloc_ln_bufs(ctx, es_ln)
        emit_ln(ctx, block, lnbufs, yout, ctx.xlnT, tag="ln",
                start_waits=ctx.cnt["s5_done"])
        emit_ffn1(ctx, block, ctx.xlnT, wgT, wuT, wdT, ag, au, ad, h_dram,
                  yout, tag="s6", xr_wait=("ln_xr", ctx.val("ln_done")),
                  close_between=es_ln)

        @block.sync
        def _(sy):
            for sname, sval in ctx.cnt["s6_done"]:
                sy.wait_ge(ctx.sems[sname], sval)
    ctx.close()
    return nc



_BUILT = {}


def _get(name, fn):
    if name not in _BUILT:
        _BUILT[name] = fn()
    return _BUILT[name]


def run_l1(inputs):
    x = np.asarray(inputs["L0_attn_x"])
    qw = np.asarray(inputs["L0_ffn_qweight"])
    qtg = np.ascontiguousarray(qw.reshape(4096, 1024, 3).transpose(1, 2, 0))
    zp0 = np.ascontiguousarray(np.asarray(inputs["L0_ffn_weight_zero_point"]).reshape(-1))
    sc0 = np.ascontiguousarray(np.asarray(inputs["L0_ffn_weight_scale"]).reshape(-1))
    b0 = np.asarray(inputs["L0_ffn_bias"])
    wqkv = np.asarray(inputs["L0_attn_Wqkv"])
    bqkv = np.asarray(inputs["L0_attn_bqkv"])

    xs = x.reshape(8, 1024, 4096)
    nc = _get("L1", build_L1)
    in_maps = [dict(x=np.ascontiguousarray(xs[c]), qtg=qtg, zp0=zp0, sc0=sc0,
                    b0=b0, wqkv=wqkv, bqkv=bqkv) for c in range(8)]
    res = run_bass_kernel_spmd(nc, in_maps, core_ids=list(range(8)))
    return res.results


def run_l2(inputs, l1_results):
    asc = np.ascontiguousarray
    sh = _l2_shared(inputs)
    in_maps = []
    for c in range(8):
        grp = [0, 1, 2, 3] if c < 4 else [4, 5, 6, 7]
        kfull = np.concatenate([np.asarray(l1_results[g]["kown"]) for g in grp], 0)
        vfull = np.concatenate([np.asarray(l1_results[g]["vown"]) for g in grp], 0)
        in_maps.append(dict(
            y1=np.asarray(l1_results[c]["y1"]),
            qfeat=np.asarray(l1_results[c]["qfeat"]),
            kfull=kfull, vfull=vfull,
            wo0=np.asarray(inputs["L0_attn_Wo"]),
            bo0=np.asarray(inputs["L0_attn_bo"]),
            w0t=np.asarray(l1_results[c]["w0t"]),
            b0=np.asarray(inputs["L0_ffn_bias"]),
            wgT=sh["wgT"], wuT=sh["wuT"], wdT=sh["wdT"],
            ag=np.asarray(inputs["L1_ffn_ag"]),
            au=np.asarray(inputs["L1_ffn_au"]),
            ad=np.asarray(inputs["L1_ffn_ad"]),
            wqcT=sh["wqcT"], bqc=np.asarray(inputs["L1_attn_bq_c"]),
            wkvcT=sh["wkvcT"], bkvc=np.asarray(inputs["L1_attn_bkv_c"]),
        ))
    nc = _get("L2", build_L2)
    return run_bass_kernel_spmd(nc, in_maps, core_ids=list(range(8))).results


def run_l3(inputs, l2_results):
    sh = _l2_shared(inputs)
    in_maps = []
    for c in range(8):
        grp = [0, 1, 2, 3] if c < 4 else [4, 5, 6, 7]
        kvcf = np.concatenate([np.asarray(l2_results[g]["kvc"]) for g in grp], 0)
        in_maps.append(dict(
            y2=np.asarray(l2_results[c]["y2"]),
            qc=np.asarray(l2_results[c]["qc"]),
            kvcf=kvcf,
            wquT=sh["wquT"], bqu=np.asarray(inputs["L1_attn_bq_up"]),
            wkvuT=sh["wkvuT"], bkvu=np.asarray(inputs["L1_attn_bkv_up"]),
            wo1T=sh["wo1T"], bo1=np.asarray(inputs["L1_attn_bo"]),
            wgT=sh["wgT"], wuT=sh["wuT"], wdT=sh["wdT"],
            ag=np.asarray(inputs["L1_ffn_ag"]),
            au=np.asarray(inputs["L1_ffn_au"]),
            ad=np.asarray(inputs["L1_ffn_ad"]),
        ))
    nc = _get("L3", build_L3)
    return run_bass_kernel_spmd(nc, in_maps, core_ids=list(range(8))).results


_L2_SHARED = {}


def _l2_shared(inputs):
    if not _L2_SHARED:
        asc = np.ascontiguousarray
        _L2_SHARED.update(
            wgT=asc(np.asarray(inputs["L1_ffn_Wg"]).T),
            wuT=asc(np.asarray(inputs["L1_ffn_Wu"]).T),
            wdT=asc(np.asarray(inputs["L1_ffn_Wd"]).T),
            wqcT=asc(np.asarray(inputs["L1_attn_Wq_c"]).T),
            wkvcT=asc(np.asarray(inputs["L1_attn_Wkv_c"]).T),
            wquT=asc(np.asarray(inputs["L1_attn_Wq_up"]).T),
            wkvuT=asc(np.asarray(inputs["L1_attn_Wkv_up"]).T),
            wo1T=asc(np.asarray(inputs["L1_attn_Wo"]).T),
        )
    return _L2_SHARED


def kernel(**inputs):
    _L2_SHARED.clear()
    r1 = run_l1(inputs)
    r2 = run_l2(inputs, r1)
    r3 = run_l3(inputs, r2)
    out = np.concatenate([np.asarray(r3[c]["yout"], np.float32)[None]
                          for c in range(8)], 0)
    return out.reshape(2, 4096, 4096)


# version 06
# speedup vs baseline: 1.0184x; 1.0184x over previous
import numpy as np
import ml_dtypes
from contextlib import ExitStack

import concourse.bass as bass
import concourse.mybir as mybir
from concourse.bass_utils import run_bass_kernel_spmd

F32 = mybir.dt.float32
BF16 = mybir.dt.bfloat16
I32 = mybir.dt.int32
AF = mybir.ActivationFunctionType
ALU = mybir.AluOpType
NPBF16 = ml_dtypes.bfloat16


class Ctx:
    def __init__(self, nc):
        self.nc = nc
        self.es = ExitStack()
        self.sems = {}
        self.cnt = {}

    def sb(self, name, shape, dtype, es=None):
        return (es or self.es).enter_context(self.nc.sbuf_tensor(name, shape, dtype))

    def ps(self, name, shape, dtype=F32, es=None):
        return (es or self.es).enter_context(self.nc.psum_tensor(name, shape, dtype))

    def sem(self, name):
        if name not in self.sems:
            self.sems[name] = self.es.enter_context(self.nc.semaphore(name))
            self.cnt[name] = 0
        return self.sems[name]

    def inc(self, name, by=1):
        self.cnt[name] += by
        return self.cnt[name]

    def val(self, name):
        return self.cnt[name]

    def close(self):
        self.es.close()


def common_setup(ctx, block):
    ctx.ident = ctx.sb("c_ident", [128, 128], BF16)
    ctx.ones_bf = ctx.sb("c_ones_bf", [1, 512], BF16)
    ctx.ones_col = ctx.sb("c_ones_col", [128, 1], BF16)
    ctx.ones_c64 = ctx.sb("c_ones_c64", [64, 1], BF16)
    ctx.ones_f32 = ctx.sb("c_ones_f32", [1, 128], F32)
    ctx.ident_f = ctx.sb("c_ident_f", [128, 128], F32)
    ctx.consts = ctx.sb("c_consts", [128, 8], F32)
    gsem = ctx.sem("g_init")

    @block.gpsimd
    def _(g):
        g.memset(ctx.ident[:], 0.0).then_inc(gsem, 1)
        ctx.inc("g_init")
        g.wait_ge(gsem, ctx.val("g_init"))
        g.affine_select(out=ctx.ident[:], in_=ctx.ident[:],
                        compare_op=ALU.not_equal, fill=1.0, base=0,
                        pattern=[[-1, 128]], channel_multiplier=1)
        g.memset(ctx.ones_bf[:], 1.0)
        g.memset(ctx.ones_col[:], 1.0)
        g.memset(ctx.ones_c64[:], 1.0)
        g.memset(ctx.ones_f32[:], 1.0)
        g.memset(ctx.ident_f[:], 0.0).then_inc(gsem, 1)
        ctx.inc("g_init")
        g.wait_ge(gsem, ctx.val("g_init"))
        g.affine_select(out=ctx.ident_f[:], in_=ctx.ident_f[:],
                        compare_op=ALU.not_equal, fill=1.0, base=0,
                        pattern=[[-1, 128]], channel_multiplier=1)
        g.memset(ctx.consts[:, 0:1], 1e-5)
        g.memset(ctx.consts[:, 1:2], -0.5)
        g.memset(ctx.consts[:, 2:3], 0.5)
        g.memset(ctx.consts[:, 3:4], 1.0 / 32.0)
        g.memset(ctx.consts[:, 4:5], 1.0 / float(np.sqrt(192.0)))
        g.memset(ctx.consts[:, 5:6], -1.01).then_inc(gsem, 1)
        ctx.inc("g_init")
    ctx.cnt["g_ready"] = ctx.val("g_init")


def emit_bcast(ctx, block, row_dram, width, out_sb, tag, scale=None, extra_waits=()):
    if len(row_dram.shape) == 1:
        row_dram = row_dram.rearrange("(a b) -> a b", a=1)
    with ExitStack() as es:
        row_sb = ctx.sb(f"{tag}_row", [1, width], F32, es=es)
        psum = ctx.ps(f"{tag}_ps", [128, 512], es=es)
        ld = ctx.sem("bc_ld")
        mm = ctx.sem("bc_mm")
        ev = ctx.sem("bc_ev")
        nblk = (width + 511) // 512
        ld0, ev0 = ctx.val("bc_ld"), ctx.val("bc_ev")
        mm0 = ctx.val("bc_mm")

        @block.sync
        def _(sy):
            sy.wait_ge(mm, mm0)
            for sname, sval in extra_waits:
                sy.wait_ge(ctx.sems[sname], sval)
            sy.dma_start(out=row_sb[:], in_=row_dram).then_inc(ld, 16)
            ctx.inc("bc_ld", 16)

        @block.tensor
        def _(t):
            t.wait_ge(ld, ld0 + 16)
            t.wait_ge(ctx.sems["g_init"], ctx.cnt["g_ready"])
            for sname, sval in extra_waits:
                t.wait_ge(ctx.sems[sname], sval)
            for b in range(nblk):
                w = min(512, width - b * 512)
                t.wait_ge(ev, ev0 + b)
                t.matmul(psum[:, 0:w], ctx.ones_f32[0:1, :],
                         row_sb[0:1, b * 512:b * 512 + w],
                         start=True, stop=True).then_inc(mm, 1)
                ctx.inc("bc_mm")

        @block.vector
        def _(v):
            for b in range(nblk):
                w = min(512, width - b * 512)
                v.wait_ge(mm, mm0 + b + 1)
                if scale is None:
                    v.tensor_copy(out_sb[:, b * 512:b * 512 + w],
                                  psum[:, 0:w]).then_inc(ev, 1)
                else:
                    v.tensor_scalar_mul(out_sb[:, b * 512:b * 512 + w],
                                        psum[:, 0:w], scale).then_inc(ev, 1)
                ctx.inc("bc_ev")
    ctx.cnt[f"{tag}_done"] = ctx.val("bc_ev")


def emit_fence(ctx, block, waits):
    waits = [(n, v) for (n, v) in waits if v > 0]
    if not waits:
        return

    def w(eng):
        for sname, sval in waits:
            eng.wait_ge(ctx.sems[sname], sval)

    @block.sync
    def _(sy):
        w(sy)

    @block.gpsimd
    def _(g):
        w(g)

    @block.scalar
    def _(sc):
        w(sc)

    @block.vector
    def _(v):
        w(v)

    @block.tensor
    def _(t):
        w(t)



def alloc_ln_bufs(ctx, es, pfx="ln"):
    return dict(
        ybuf=ctx.sb(f"{pfx}_ybuf", [128, 4096], F32, es=es),
        xbuf=ctx.sb(f"{pfx}_xbuf", [128, 4096], BF16, es=es),
        stats=ctx.sb(f"{pfx}_stats", [128, 48], F32, es=es),
        aggr=ctx.sb(f"{pfx}_aggr", [128, 2], F32, es=es),
        scr=ctx.sb(f"{pfx}_scr", [128, 4], F32, es=es),
    )


def emit_ln(ctx, block, bufs, y_dram, xlnT, tag, start_waits=()):
    ysem = [ctx.sem(f"{tag}_y{i}") for i in range(2)]
    dv = ctx.sem(f"{tag}_dv")
    ac = ctx.sem(f"{tag}_ac")
    pe = ctx.sem(f"{tag}_pe")
    xr = ctx.sem(f"{tag}_xr")
    ybuf, xbuf = bufs["ybuf"], bufs["xbuf"]
    stats, aggr, scr = bufs["stats"], bufs["aggr"], bufs["scr"]

    y0_y = [ctx.val(f"{tag}_y{i}") for i in range(2)]
    y0_dv = ctx.val(f"{tag}_dv")
    y0_ac, y0_pe, y0_xr = ctx.val(f"{tag}_ac"), ctx.val(f"{tag}_pe"), ctx.val(f"{tag}_xr")

    uq = ctx.cnt.setdefault("_ln_uq", 0)
    ctx.cnt["_ln_uq"] = uq + 1
    with ExitStack() as es:
        pstr = [ctx.ps(f"{tag}{uq}_pstr{i}", [128, 1024], BF16, es=es)
                for i in range(2)]

        @block.sync
        def _(sy):
            for sname, sval in start_waits:
                sy.wait_ge(ctx.sems[sname], sval)
            for ti in range(8):
                for c in range(8):
                    if ti >= 1:
                        sy.wait_ge(ac, y0_ac + (ti - 1) * 10 + 2 + c + 1)
                    if c >= 2:
                        sy.wait_ge(dv, y0_dv + ti * 10 + c - 1)
                    sy.dma_start(out=ybuf[:, c * 512:(c + 1) * 512],
                                 in_=y_dram[ti * 128:(ti + 1) * 128,
                                            c * 512:(c + 1) * 512]
                                 ).then_inc(ysem[c % 2], 16)
                    ctx.inc(f"{tag}_y{c % 2}", 16)

        @block.vector
        def _(v):
            def _pack(ti):
                for pk in range(4):
                    p_idx = ti * 4 + pk
                    v.wait_ge(pe, y0_pe + p_idx + 1)
                    out_ap = (xlnT[:].rearrange("p (a x) -> p a x", a=32)
                              [:, pk * 8:(pk + 1) * 8, ti * 128:(ti + 1) * 128])
                    in_ap = pstr[p_idx % 2][:].rearrange("p (a x) -> p a x", a=8)
                    v.tensor_copy(out_ap, in_ap).then_inc(xr, 1)
                    ctx.inc(f"{tag}_xr")

            for ti in range(8):
                for c in range(8):
                    v.wait_ge(ysem[c % 2], y0_y[c % 2]
                              + 16 * (ti * 4 + c // 2 + 1))
                    v.bn_stats(stats[:, c * 6:(c + 1) * 6],
                               ybuf[:, c * 512:(c + 1) * 512]).then_inc(dv, 1)
                    ctx.inc(f"{tag}_dv")
                v.wait_ge(dv, ctx.val(f"{tag}_dv"))
                v.bn_aggr(aggr[:], stats[:]).then_inc(dv, 1)
                ctx.inc(f"{tag}_dv")
                v.wait_ge(ac, y0_ac + ti * 10 + 2)
                v.tensor_scalar(scr[:, 2:3], aggr[:, 0:1], scr[:, 1:2], -1.0,
                                ALU.mult, ALU.mult).then_inc(dv, 1)
                ctx.inc(f"{tag}_dv")
                if ti >= 1:
                    _pack(ti - 1)
            _pack(7)

        @block.scalar
        def _(s):
            s.wait_ge(ctx.sems["g_init"], ctx.cnt["g_ready"])
            for ti in range(8):
                s.wait_ge(dv, y0_dv + ti * 10 + 9)
                s.activation(scr[:, 0:1], aggr[:, 1:2], AF.Ln,
                             bias=ctx.consts[:, 0:1], scale=1.0).then_inc(ac, 1)
                ctx.inc(f"{tag}_ac")
                s.wait_ge(ac, y0_ac + ti * 10 + 1)
                s.activation(scr[:, 1:2], scr[:, 0:1], AF.Exp,
                             bias=0.0, scale=ctx.consts[:, 1:2]).then_inc(ac, 1)
                ctx.inc(f"{tag}_ac")
                s.wait_ge(dv, y0_dv + ti * 10 + 10)
                for c in range(8):
                    if ti >= 1:
                        s.wait_ge(pe, y0_pe + (ti - 1) * 4 + c // 2 + 1)
                    s.activation(xbuf[:, c * 512:(c + 1) * 512],
                                 ybuf[:, c * 512:(c + 1) * 512], AF.Identity,
                                 bias=scr[:, 2:3], scale=scr[:, 1:2]).then_inc(ac, 1)
                    ctx.inc(f"{tag}_ac")

        @block.tensor
        def _(t):
            t.wait_ge(ctx.sems["g_init"], ctx.cnt["g_ready"])
            for ti in range(8):
                for pk in range(4):
                    t.wait_ge(ac, y0_ac + ti * 10 + 2 + 2 * pk + 2)
                    p_idx = ti * 4 + pk
                    if p_idx >= 2:
                        t.wait_ge(xr, y0_xr + p_idx - 1)
                    for j in range(8):
                        k = pk * 8 + j
                        ins = t.transpose(pstr[p_idx % 2][:, j * 128:(j + 1) * 128],
                                          xbuf[:, k * 128:(k + 1) * 128],
                                          ctx.ident[:])
                        if j == 7:
                            ins.then_inc(pe, 1)
                    ctx.inc(f"{tag}_pe")

    ctx.cnt[f"{tag}_done"] = ctx.val(f"{tag}_xr")



def emit_dequant(ctx, block, es, qtg_dram, zp_dram, scale_dram, w0t_dram,
                 tag="dq"):
    qs = ctx.sem(f"{tag}_q")
    dv = ctx.sem(f"{tag}_dv")
    st = [ctx.sem(f"{tag}_st{i}") for i in range(2)]

    qbuf = ctx.sb(f"{tag}_qb", [128, 2048], I32, es=es)
    vint = ctx.sb(f"{tag}_vint", [128, 2048], I32, es=es)
    vf = ctx.sb(f"{tag}_vf", [128, 2048], F32, es=es)
    wout = [ctx.sb(f"{tag}_wo{i}", [128, 2048], BF16, es=es) for i in range(2)]
    zpB = ctx.sb(f"{tag}_zpB", [128, 2048], F32, es=es)
    scB = ctx.sb(f"{tag}_scB", [128, 2048], F32, es=es)

    q0, dv0 = ctx.val(f"{tag}_q"), ctx.val(f"{tag}_dv")
    for h in range(2):
        nsl = slice(h * 2048, (h + 1) * 2048)
        emit_bcast(ctx, block, zp_dram[nsl], 2048, zpB, f"{tag}_zb{h}")
        emit_bcast(ctx, block, scale_dram[nsl], 2048, scB, f"{tag}_sb{h}")
        h_dv0 = ctx.val(f"{tag}_dv")
        h_q0 = ctx.val(f"{tag}_q")
        h_st0 = [ctx.val(f"{tag}_st{i}") for i in range(2)]

        @block.sync
        def _(sy, h=h, nsl=nsl, h_dv0=h_dv0):
            for gt in range(8):
                base = h_dv0 + gt * 12
                if gt >= 1:
                    sy.wait_ge(dv, base)
                elif h >= 1:
                    sy.wait_ge(dv, h_dv0)
                sy.dma_start(out=qbuf[:],
                             in_=qtg_dram[gt * 128:(gt + 1) * 128, 0, nsl]
                             ).then_inc(qs, 16)
                ctx.inc(f"{tag}_q", 16)
                sy.wait_ge(dv, base + 6)
                sy.dma_start(out=qbuf[:],
                             in_=qtg_dram[gt * 128:(gt + 1) * 128, 1, nsl]
                             ).then_inc(qs, 16)
                ctx.inc(f"{tag}_q", 16)
                sy.wait_ge(dv, base + 9)
                sy.dma_start(out=qbuf[:],
                             in_=qtg_dram[gt * 128:(gt + 1) * 128, 2, nsl]
                             ).then_inc(qs, 16)
                ctx.inc(f"{tag}_q", 16)

        zs_done = ctx.val("bc_ev")

        @block.vector
        def _(v, h=h, h_q0=h_q0, h_st0=h_st0, zs_done=zs_done):
            v.wait_ge(ctx.sems["bc_ev"], zs_done)
            for gt in range(8):
                for jj in range(4):
                    if jj in (0, 1):
                        v.wait_ge(qs, h_q0 + gt * 48 + 16)
                    elif jj == 2:
                        v.wait_ge(qs, h_q0 + gt * 48 + 32)
                    else:
                        v.wait_ge(qs, h_q0 + gt * 48 + 48)
                    p = gt * 4 + jj
                    if p >= 2:
                        v.wait_ge(st[p % 2], h_st0[p % 2] + 16 * (p // 2))
                    elif h >= 1:
                        v.wait_ge(st[p % 2], h_st0[p % 2])
                    if jj == 1:
                        v.tensor_single_scalar(vint[:], qbuf[:], 6,
                                               ALU.logical_shift_right).then_inc(dv, 1)
                    else:
                        v.tensor_single_scalar(vint[:], qbuf[:], 63,
                                               ALU.bitwise_and).then_inc(dv, 1)
                    ctx.inc(f"{tag}_dv")
                    v.wait_ge(dv, ctx.val(f"{tag}_dv"))
                    v.tensor_sub(vf[:], vint[:], zpB[:]).then_inc(dv, 1)
                    ctx.inc(f"{tag}_dv")
                    v.wait_ge(dv, ctx.val(f"{tag}_dv"))
                    v.tensor_mul(wout[(gt * 4 + jj) % 2][:], vf[:],
                                 scB[:]).then_inc(dv, 1)
                    ctx.inc(f"{tag}_dv")

        @block.scalar
        def _(sc, h=h, h_dv0=h_dv0, nsl=nsl):
            for gt in range(8):
                for jj in range(4):
                    sc.wait_ge(dv, h_dv0 + gt * 12 + jj * 3 + 3)
                    sc.dma_start(
                        out=w0t_dram.rearrange("(gg four) n -> gg four n", four=4)
                            [gt * 128:(gt + 1) * 128, jj, nsl],
                        in_=wout[(gt * 4 + jj) % 2][:]
                        ).then_inc(st[(gt * 4 + jj) % 2], 16)
                    ctx.inc(f"{tag}_st{(gt * 4 + jj) % 2}", 16)

    ctx.cnt[f"{tag}_done"] = [(f"{tag}_st{i}", ctx.val(f"{tag}_st{i}"))
                              for i in range(2)]



def alloc_ffn0_bufs(ctx, es, ntmp=8):
    return dict(
        wblk=ctx.sb("f0_wblk", [128, 32 * 512], BF16, es=es),
        tmp=[ctx.sb(f"f0_tmp{i}", [128, 512], F32, es=es) for i in range(ntmp)],
    )


alloc_ffn0_bufs2 = alloc_ffn0_bufs


def emit_ffn0(ctx, block, bufs, xlnT, w0t_dram, halfb0B, y_out, tag,
              xr_wait=None, w_wait=None, b_wait=0, y_wait=None):
    wq = [ctx.sem(f"{tag}_w{q}") for q in range(4)]
    mmq = ctx.sem(f"{tag}_mmq")
    mm = ctx.sem(f"{tag}_mm")
    ev = ctx.sem(f"{tag}_ev")
    ysts = [ctx.sem(f"{tag}_yst{i}") for i in range(8)]
    wblk, tmp = bufs["wblk"], bufs["tmp"]

    w0 = [ctx.val(f"{tag}_w{q}") for q in range(4)]
    mq0, mm0 = ctx.val(f"{tag}_mmq"), ctx.val(f"{tag}_mm")
    ev0 = ctx.val(f"{tag}_ev")
    yst0 = [ctx.val(f"{tag}_yst{i}") for i in range(8)]

    with ExitStack() as es:
        psums = [ctx.ps(f"{tag}_pb{i}", [128, 512], es=es) for i in range(8)]

        @block.sync
        def _(sy):
            for sn, sv in (w_wait or []):
                sy.wait_ge(ctx.sems[sn], sv)
            for n in range(8):
                for q in range(4):
                    if n >= 1:
                        if q < 3:
                            sy.wait_ge(mmq, mq0 + 3 * (n - 1) + q + 1)
                        else:
                            sy.wait_ge(mm, mm0 + n)
                    sy.dma_start(
                        out=wblk[:].rearrange("p (a x) -> p a x", a=32)
                            [:, q * 8:(q + 1) * 8, :],
                        in_=w0t_dram.rearrange("(a p) n -> p a n", p=128)
                            [:, q * 8:(q + 1) * 8, n * 512:(n + 1) * 512],
                    ).then_inc(wq[q], 16)
                    ctx.inc(f"{tag}_w{q}", 16)

        @block.tensor
        def _(t):
            if xr_wait is not None:
                t.wait_ge(ctx.sems[xr_wait[0]], xr_wait[1])
            for n in range(8):
                if n >= 1:
                    t.wait_ge(ev, ev0 + 8 * n)
                for k in range(32):
                    if k % 8 == 0:
                        t.wait_ge(wq[k // 8], w0[k // 8] + 16 * (n + 1))
                    for ti in range(8):
                        ins = t.matmul(
                            psums[ti][:],
                            xlnT[:, k * 1024 + ti * 128:k * 1024 + (ti + 1) * 128],
                            wblk[:, k * 512:(k + 1) * 512],
                            start=(k == 0), stop=(k == 31))
                        if ti == 7 and k % 8 == 7:
                            if k == 31:
                                ins.then_inc(mm, 1)
                                ctx.inc(f"{tag}_mm")
                            else:
                                ins.then_inc(mmq, 1)
                                ctx.inc(f"{tag}_mmq")

        @block.vector
        def _(v):
            v.wait_ge(ctx.sems["bc_ev"], b_wait)
            if y_wait is not None:
                v.wait_ge(ctx.sems[y_wait[0]], y_wait[1])
            for n in range(8):
                v.wait_ge(mm, mm0 + n + 1)
                for ti in range(8):
                    idx = n * 8 + ti
                    slot = idx % len(tmp)
                    if idx >= len(tmp):
                        v.wait_ge(ysts[slot], yst0[slot] + 16 * (idx // len(tmp)))
                    v.scalar_tensor_tensor(tmp[slot][:], psums[ti][:], 0.5,
                                           halfb0B[:, n * 512:(n + 1) * 512],
                                           ALU.mult, ALU.add).then_inc(ev, 1)
                    ctx.inc(f"{tag}_ev")

        @block.gpsimd
        def _(g):
            for n in range(8):
                for ti in range(8):
                    idx = n * 8 + ti
                    slot = idx % len(tmp)
                    g.wait_ge(ev, ev0 + idx + 1)
                    g.dma_start(out=y_out[ti * 128:(ti + 1) * 128,
                                          n * 512:(n + 1) * 512],
                                in_=tmp[slot][:],
                                accum_op=ALU.add).then_inc(ysts[slot], 16)
                    ctx.inc(f"{tag}_yst{slot}", 16)
    ctx.cnt[f"{tag}_done"] = [(f"{tag}_yst{i}", ctx.val(f"{tag}_yst{i}"))
                              for i in range(8)]



def emit_qkv(ctx, block, es, xlnT, wqkv_dram, bqkv_dram, qfeat_dram,
             kown_dram, vown_dram, tag="qkv", start_waits=(), xr_wait=None):
    wkv = ctx.sb(f"{tag}_wkv", [128, 16384], BF16, es=es)
    qf = [ctx.sb(f"{tag}_qf{i}", [128, 1024], BF16, es=es) for i in range(2)]
    kvt = [ctx.sb(f"{tag}_kvt{i}", [128, 512], BF16, es=es) for i in range(2)]
    bkvB = ctx.sb(f"{tag}_bkvB", [128, 2048], F32, es=es)
    bq_pm = ctx.sb(f"{tag}_bqpm", [128, 8], F32, es=es)
    bq_sc = ctx.sb(f"{tag}_bqsc", [128, 8], F32, es=es)

    wqa = [ctx.sem(f"{tag}_wqa{i}") for i in range(2)]
    kw = [ctx.sem(f"{tag}_kw{q}") for q in range(4)]
    qmm = ctx.sem(f"{tag}_qmm")
    qev = ctx.sem(f"{tag}_qev")
    qsts = [ctx.sem(f"{tag}_qst{i}") for i in range(2)]
    kmq = ctx.sem(f"{tag}_kmq")
    kbd = ctx.sem(f"{tag}_kbd")
    kev = ctx.sem(f"{tag}_kev")
    ksts = [ctx.sem(f"{tag}_kst{i}") for i in range(2)]
    pvl = ctx.sem(f"{tag}_pvl")
    dvx = ctx.sem(f"{tag}_dvx")

    @block.sync
    def _(sy):
        sy.dma_start(out=bq_pm[:],
                     in_=bqkv_dram[0:1024].rearrange("(c p) -> p c", p=128),
                     allow_slow_non_contiguous=True).then_inc(pvl, 16)
        ctx.inc(f"{tag}_pvl", 16)

    @block.vector
    def _(v):
        v.wait_ge(pvl, ctx.val(f"{tag}_pvl"))
        v.tensor_scalar_mul(bq_sc[:], bq_pm[:], 1.0 / 32.0).then_inc(dvx, 1)
        ctx.inc(f"{tag}_dvx")

    emit_bcast(ctx, block, bqkv_dram[1024:3072], 2048, bkvB, f"{tag}_bkv",
               extra_waits=([xr_wait] if xr_wait is not None else []))
    bkv_done = ctx.val(f"{tag}_bkv_done")

    with ExitStack() as pses:
        psq = [ctx.ps(f"{tag}_psq{i}", [128, 512], es=pses) for i in range(4)]
        psk = [ctx.ps(f"{tag}_psk{i}", [128, 512], es=pses) for i in range(4)]
        allps = psq + psk

        wqa0 = [ctx.val(f"{tag}_wqa{i}") for i in range(2)]
        qmm0, qev0 = ctx.val(f"{tag}_qmm"), ctx.val(f"{tag}_qev")
        qst0 = [ctx.val(f"{tag}_qst{i}") for i in range(2)]

        @block.gpsimd
        def _(g):
            for sname, sval in start_waits:
                g.wait_ge(ctx.sems[sname], sval)
            for fc in range(8):
                if fc >= 2:
                    g.wait_ge(qmm, qmm0 + fc - 1)
                half = (fc % 2) * 8192
                for q in range(4):
                    g.dma_start(
                        out=wkv[:, half:half + 4096].rearrange("p (a x) -> p a x", a=32)
                            [:, q * 8:(q + 1) * 8, :],
                        in_=wqkv_dram.rearrange("(a p) f -> p a f", p=128)
                            [:, q * 8:(q + 1) * 8, fc * 128:(fc + 1) * 128],
                    ).then_inc(wqa[fc % 2], 16)
                    ctx.inc(f"{tag}_wqa{fc % 2}", 16)

        @block.tensor
        def _(t):
            if xr_wait is not None:
                t.wait_ge(ctx.sems[xr_wait[0]], xr_wait[1])
            t.wait_ge(ctx.sems["bc_ev"], bkv_done)
            for fc in range(8):
                t.wait_ge(wqa[fc % 2], wqa0[fc % 2] + 64 * (fc // 2 + 1))
                half = (fc % 2) * 8192
                if fc >= 2:
                    t.wait_ge(qev, qev0 + 2 * (fc - 1))
                for k in range(32):
                    for th in range(2):
                        ins = t.matmul(
                            psq[(fc % 2) * 2 + th][:],
                            wkv[:, half + k * 128:half + (k + 1) * 128],
                            xlnT[:, k * 1024 + th * 512:k * 1024 + (th + 1) * 512],
                            start=(k == 0), stop=(k == 31))
                        if k == 31 and th == 1:
                            ins.then_inc(qmm, 1)
                ctx.inc(f"{tag}_qmm")

        @block.scalar
        def _(s):
            s.wait_ge(dvx, ctx.val(f"{tag}_dvx"))
            s.wait_ge(ctx.sems["g_init"], ctx.cnt["g_ready"])
            for fc in range(8):
                s.wait_ge(qmm, qmm0 + fc + 1)
                if fc >= 2:
                    s.wait_ge(qsts[fc % 2], qst0[fc % 2] + 16 * (fc // 2))
                for th in range(2):
                    s.activation(qf[fc % 2][:, th * 512:(th + 1) * 512],
                                 psq[(fc % 2) * 2 + th][:], AF.Identity,
                                 bias=bq_sc[:, fc:fc + 1],
                                 scale=ctx.consts[:, 3:4]).then_inc(qev, 1)
                    ctx.inc(f"{tag}_qev")

        @block.sync
        def _(sy):
            for fc in range(8):
                sy.wait_ge(qev, qev0 + 2 * (fc + 1))
                sy.dma_start(out=qfeat_dram[fc * 128:(fc + 1) * 128, :],
                             in_=qf[fc % 2][:]).then_inc(qsts[fc % 2], 16)
                ctx.inc(f"{tag}_qst{fc % 2}", 16)

        kw0 = [ctx.val(f"{tag}_kw{q}") for q in range(4)]
        kmq0, kev0 = ctx.val(f"{tag}_kmq"), ctx.val(f"{tag}_kev")
        kst0 = [ctx.val(f"{tag}_kst{i}") for i in range(2)]
        kbd0 = ctx.val(f"{tag}_kbd")

        @block.gpsimd
        def _(g):
            g.wait_ge(qmm, qmm0 + 8)
            for fv in range(4):
                for q in range(4):
                    if fv >= 1:
                        if q < 3:
                            g.wait_ge(kmq, kmq0 + 3 * (fv - 1) + q + 1)
                        else:
                            g.wait_ge(kbd, kbd0 + fv)
                    g.dma_start(
                        out=wkv[:].rearrange("p (a x) -> p a x", a=32)
                            [:, q * 8:(q + 1) * 8, :],
                        in_=wqkv_dram.rearrange("(a p) f -> p a f", p=128)
                            [:, q * 8:(q + 1) * 8,
                             1024 + fv * 512:1024 + (fv + 1) * 512],
                    ).then_inc(kw[q], 16)
                    ctx.inc(f"{tag}_kw{q}", 16)

        @block.tensor
        def _(t):
            t.wait_ge(qev, qev0 + 16)
            for fv in range(4):
                if fv >= 1:
                    t.wait_ge(kev, kev0 + 8 * fv)
                for k in range(32):
                    if k % 8 == 0:
                        t.wait_ge(kw[k // 8], kw0[k // 8] + 16 * (fv + 1))
                    for ti in range(8):
                        ins = t.matmul(
                            allps[ti][:],
                            xlnT[:, k * 1024 + ti * 128:k * 1024 + (ti + 1) * 128],
                            wkv[:, k * 512:(k + 1) * 512],
                            start=(k == 0), stop=(k == 31))
                        if ti == 7 and k % 8 == 7:
                            if k == 31:
                                ins.then_inc(kbd, 1)
                                ctx.inc(f"{tag}_kbd")
                            else:
                                ins.then_inc(kmq, 1)
                                ctx.inc(f"{tag}_kmq")

        @block.vector
        def _(v):
            v.wait_ge(ctx.sems["bc_ev"], bkv_done)
            for fv in range(4):
                v.wait_ge(kbd, kbd0 + fv + 1)
                for ti in range(8):
                    idx = fv * 8 + ti
                    slot = idx % 2
                    if idx >= 2:
                        v.wait_ge(ksts[slot], kst0[slot] + 16 * (idx // 2))
                    v.tensor_add(kvt[slot][:], allps[ti][:],
                                 bkvB[:, fv * 512:(fv + 1) * 512]).then_inc(kev, 1)
                    ctx.inc(f"{tag}_kev")

        @block.sync
        def _(sy):
            for fv in range(4):
                for ti in range(8):
                    idx = fv * 8 + ti
                    slot = idx % 2
                    sy.wait_ge(kev, kev0 + idx + 1)
                    dst = kown_dram if fv < 2 else vown_dram
                    col = (fv % 2) * 512
                    sy.dma_start(out=dst[ti * 128:(ti + 1) * 128, col:col + 512],
                                 in_=kvt[slot][:]).then_inc(ksts[slot], 16)
                    ctx.inc(f"{tag}_kst{slot}", 16)

    ctx.cnt[f"{tag}_done"] = ([(f"{tag}_kst{i}", ctx.val(f"{tag}_kst{i}")) for i in range(2)]
                              + [(f"{tag}_qst{i}", ctx.val(f"{tag}_qst{i}")) for i in range(2)])



def emit_s2attn(ctx, block, qfeat_dram, kfull_dram, vfull_dram, wo_dram,
                bo_dram, y_out, tag="s2", y_wait=None):
    with ExitStack() as esA:
        bo0B = ctx.sb(f"{tag}_bo0B", [128, 4096], F32, es=esA)
        emit_bcast(ctx, block, bo_dram, 4096, bo0B, f"{tag}_bo")
        obf = ctx.sb(f"{tag}_obf", [128, 8 * 1024], BF16, es=esA)
        rlrow = ctx.sb(f"{tag}_rlrow", [1, 1024], F32, es=esA)
        rlT = ctx.sb(f"{tag}_rlT", [128, 8], F32, es=esA)

        ld = ctx.sem(f"{tag}_ld")
        nrm = ctx.sem(f"{tag}_nrm")
        nmm = ctx.sem(f"{tag}_nmm")
        nac = ctx.sem(f"{tag}_nac")
        smm = ctx.sem(f"{tag}_smm")
        sex = ctx.sem(f"{tag}_sex")
        lmm = ctx.sem(f"{tag}_lmm")
        pvm = ctx.sem(f"{tag}_pvm")
        pve = ctx.sem(f"{tag}_pve")
        vts = [ctx.sem(f"{tag}_vt{i}") for i in range(4)]
        rle = ctx.sem(f"{tag}_rle")

        with ExitStack() as esB:
            kf = ctx.sb(f"{tag}_kf", [128, 8 * 4096], BF16, es=esB)
            qsb = ctx.sb(f"{tag}_qsb", [128, 8 * 1024], BF16, es=esB)
            pt = [ctx.sb(f"{tag}_pt{i}", [128, 512], BF16, es=esB) for i in range(32)]
            sq = ctx.sb(f"{tag}_sq", [128, 1024], BF16, es=esB)
            qn2row = ctx.sb(f"{tag}_qn2", [1, 1024], F32, es=esB)
            sqrow = ctx.sb(f"{tag}_sqrow", [1, 1024], F32, es=esB)
            negb = ctx.sb(f"{tag}_negb", [1, 1024], BF16, es=esB)
            kmx = ctx.sb(f"{tag}_kmx", [1, 2], F32, es=esB)
            vt = [ctx.sb(f"{tag}_vtb{i}", [128, 512], BF16, es=esB) for i in range(4)]

            @block.sync
            def _(sy):
                for c in range(8):
                    sy.dma_start(out=qsb[:, c * 1024:(c + 1) * 1024],
                                 in_=qfeat_dram[c * 128:(c + 1) * 128, :]
                                 ).then_inc(ld, 16)
                    ctx.inc(f"{tag}_ld", 16)
                for c in range(8):
                    sy.dma_start(out=kf[:, c * 4096:(c + 1) * 4096],
                                 in_=kfull_dram[:, c * 128:(c + 1) * 128],
                                 transpose=True).then_inc(ld, 16)
                    ctx.inc(f"{tag}_ld", 16)
            kld = ctx.val(f"{tag}_ld")

            with ExitStack() as esP:
                psn = ctx.ps(f"{tag}_psn", [1, 512], es=esP)
                psq2 = [ctx.ps(f"{tag}_psq2{i}", [1, 512], es=esP) for i in range(2)]
                nac0 = ctx.val(f"{tag}_nac")
                nmm0 = ctx.val(f"{tag}_nmm")
                nrm0 = ctx.val(f"{tag}_nrm")

                @block.scalar
                def _(s):
                    s.wait_ge(ld, kld)
                    for i in range(64):
                        if i >= 1:
                            s.wait_ge(nmm, nmm0 + i)
                        b, c = divmod(i, 8)
                        s.activation(sq[:, 0:512],
                                     kf[:, c * 4096 + b * 512:c * 4096 + (b + 1) * 512],
                                     AF.Square, bias=0.0, scale=1.0).then_inc(nac, 1)
                        ctx.inc(f"{tag}_nac")
                    for c in range(8):
                        s.wait_ge(nmm, nmm0 + 64 + 2 * c)
                        s.activation(sq[:], qsb[:, c * 1024:(c + 1) * 1024],
                                     AF.Square, bias=0.0, scale=1.0).then_inc(nac, 1)
                        ctx.inc(f"{tag}_nac")

                @block.tensor
                def _(t):
                    t.wait_ge(ctx.sems["g_init"], ctx.cnt["g_ready"])
                    t.wait_ge(ctx.sems["bc_ev"], ctx.val(f"{tag}_bo_done"))
                    for i in range(64):
                        b, c = divmod(i, 8)
                        t.wait_ge(nac, nac0 + i + 1)
                        if c == 0 and b > 0:
                            t.wait_ge(nrm, nrm0 + 2 * b)
                        t.matmul(psn[:], ctx.ones_col[:], sq[:, 0:512],
                                 start=(c == 0), stop=(c == 7)).then_inc(nmm, 1)
                        ctx.inc(f"{tag}_nmm")
                    for c in range(8):
                        t.wait_ge(nac, nac0 + 64 + c + 1)
                        for q2 in range(2):
                            t.matmul(psq2[q2][:], ctx.ones_col[:],
                                     sq[:, q2 * 512:(q2 + 1) * 512],
                                     start=(c == 0), stop=(c == 7)).then_inc(nmm, 1)
                            ctx.inc(f"{tag}_nmm")

                @block.vector
                def _(v):
                    v.memset(kmx[0:1, 0:1], 0.0).then_inc(nrm, 1)
                    ctx.inc(f"{tag}_nrm")
                    for b in range(8):
                        v.wait_ge(nmm, nmm0 + (b + 1) * 8)
                        v.wait_ge(nrm, ctx.val(f"{tag}_nrm"))
                        v.reduce_max(kmx[0:1, 1:2], psn[:],
                                     mybir.AxisListType.X).then_inc(nrm, 1)
                        ctx.inc(f"{tag}_nrm")
                        v.wait_ge(nrm, ctx.val(f"{tag}_nrm"))
                        v.tensor_max(kmx[0:1, 0:1], kmx[0:1, 0:1],
                                     kmx[0:1, 1:2]).then_inc(nrm, 1)
                        ctx.inc(f"{tag}_nrm")
                    v.wait_ge(nmm, nmm0 + 64 + 16)
                    v.tensor_copy(qn2row[0:1, 0:512], psq2[0][:])
                    v.tensor_copy(qn2row[0:1, 512:1024], psq2[1][:]).then_inc(nrm, 1)
                    ctx.inc(f"{tag}_nrm")

                @block.scalar
                def _(s):
                    s.wait_ge(nrm, ctx.val(f"{tag}_nrm"))
                    s.activation(kmx[0:1, 1:2], kmx[0:1, 0:1], AF.Sqrt,
                                 bias=0.0, scale=1.0).then_inc(nac, 1)
                    ctx.inc(f"{tag}_nac")
                    s.wait_ge(nac, ctx.val(f"{tag}_nac"))
                    s.activation(sqrow[:], qn2row[:], AF.Sqrt,
                                 bias=0.0, scale=1.0).then_inc(nac, 1)
                    ctx.inc(f"{tag}_nac")

                @block.vector
                def _(v):
                    v.wait_ge(nac, ctx.val(f"{tag}_nac"))
                    v.tensor_scalar(negb[:], sqrow[:], kmx[0:1, 1:2], -1.01,
                                    ALU.mult, ALU.mult).then_inc(nrm, 1)
                    ctx.inc(f"{tag}_nrm")
            nrm_done = ctx.val(f"{tag}_nrm")

            with ExitStack() as esP:
                pss = [ctx.ps(f"{tag}_pss{i}", [128, 512], es=esP) for i in range(4)]
                psl = ctx.ps(f"{tag}_psl", [1, 512], es=esP)

                for qb in range(2):
                    smm0 = ctx.val(f"{tag}_smm")
                    sex0 = ctx.val(f"{tag}_sex")
                    lmm0 = ctx.val(f"{tag}_lmm")

                    @block.tensor
                    def _(t, qb=qb, smm0=smm0, sex0=sex0):
                        t.wait_ge(nrm, nrm_done)
                        if qb == 1:
                            t.wait_ge(pve, ctx.val(f"{tag}_pve"))
                            t.wait_ge(rle, ctx.val(f"{tag}_rle"))
                        for kt in range(32):
                            if kt >= 4:
                                t.wait_ge(sex, sex0 + kt - 3)
                            for c in range(8):
                                t.matmul(pss[kt % 4][:],
                                         kf[:, c * 4096 + kt * 128:c * 4096 + (kt + 1) * 128],
                                         qsb[:, c * 1024 + qb * 512:c * 1024 + (qb + 1) * 512],
                                         start=(c == 0), stop=False)
                            t.matmul(pss[kt % 4][:], ctx.ones_bf[0:1, 0:128],
                                     negb[0:1, qb * 512:(qb + 1) * 512],
                                     start=False, stop=True).then_inc(smm, 1)
                            ctx.inc(f"{tag}_smm")
                        for kt in range(32):
                            t.wait_ge(sex, sex0 + kt + 1)
                            t.matmul(psl[:], ctx.ones_col[:], pt[kt][:],
                                     start=(kt == 0), stop=(kt == 31)).then_inc(lmm, 1)
                            ctx.inc(f"{tag}_lmm")

                    @block.scalar
                    def _(s, qb=qb, smm0=smm0):
                        for kt in range(32):
                            s.wait_ge(smm, smm0 + kt + 1)
                            s.activation(pt[kt][:], pss[kt % 4][:], AF.Exp,
                                         bias=0.0, scale=1.0).then_inc(sex, 1)
                            ctx.inc(f"{tag}_sex")

                    @block.vector
                    def _(v, qb=qb):
                        v.wait_ge(lmm, ctx.val(f"{tag}_lmm"))
                        v.reciprocal(rlrow[0:1, qb * 512:(qb + 1) * 512],
                                     psl[:]).then_inc(rle, 1)
                        ctx.inc(f"{tag}_rle")

                    for half in range(2):
                        vbase = [ctx.val(f"{tag}_vt{i}") for i in range(4)]
                        pvm0 = ctx.val(f"{tag}_pvm")

                        @block.sync
                        def _(sy, qb=qb, half=half, vbase=vbase, pvm0=pvm0):
                            for kt in range(32):
                                slot = kt % 4
                                if kt >= 4:
                                    sy.wait_ge(pvm, pvm0 + kt - 3)
                                elif pvm0 > 0:
                                    sy.wait_ge(pvm, pvm0)
                                sy.dma_start(
                                    out=vt[slot][:],
                                    in_=vfull_dram[kt * 128:(kt + 1) * 128,
                                                   half * 512:(half + 1) * 512]
                                    ).then_inc(vts[slot], 16)
                                ctx.inc(f"{tag}_vt{slot}", 16)

                        @block.tensor
                        def _(t, qb=qb, half=half, vbase=vbase, smm0=smm0,
                              lmm0=lmm0):
                            if half == 0:
                                t.wait_ge(sex, sex0 + 32)
                            if ctx.val(f"{tag}_pve") > 0:
                                t.wait_ge(pve, ctx.val(f"{tag}_pve"))
                            for kt in range(32):
                                slot = kt % 4
                                t.wait_ge(vts[slot], vbase[slot] + 16 * (kt // 4 + 1))
                                for vc in range(4):
                                    ins = t.matmul(
                                        pss[vc][:],
                                        vt[slot][:, vc * 128:(vc + 1) * 128],
                                        pt[kt][:],
                                        start=(kt == 0), stop=(kt == 31))
                                ins.then_inc(pvm, 1)
                                ctx.inc(f"{tag}_pvm")

                        @block.vector
                        def _(v, qb=qb, half=half):
                            v.wait_ge(pvm, ctx.val(f"{tag}_pvm"))
                            for vc in range(4):
                                vdc = half * 4 + vc
                                v.tensor_copy(
                                    obf[:, vdc * 1024 + qb * 512:vdc * 1024 + (qb + 1) * 512],
                                    pss[vc][:]).then_inc(pve, 1)
                                ctx.inc(f"{tag}_pve")

        with ExitStack() as esP:
            pst = ctx.ps(f"{tag}_pst", [128, 8], es=esP)

            @block.tensor
            def _(t):
                t.wait_ge(rle, ctx.val(f"{tag}_rle"))
                t.wait_ge(pve, ctx.val(f"{tag}_pve"))
                for j in range(8):
                    ins = t.transpose(pst[:, j:j + 1],
                                      rlrow[0:1, j * 128:(j + 1) * 128],
                                      ctx.ident_f[0:1, 0:1])
                ins.then_inc(smm, 1)
                ctx.inc(f"{tag}_smm")

            @block.vector
            def _(v):
                v.wait_ge(smm, ctx.val(f"{tag}_smm"))
                v.tensor_copy(rlT[:], pst[:]).then_inc(rle, 1)
                ctx.inc(f"{tag}_rle")
        rl_done = ctx.val(f"{tag}_rle")

        with ExitStack() as esC:
            wos = [ctx.sb(f"{tag}_wos{i}", [128, 512], BF16, es=esC) for i in range(4)]
            stgo = [ctx.sb(f"{tag}_sgo{i}", [128, 512], F32, es=esC) for i in range(4)]
            tmp = [ctx.sb(f"{tag}_tmp{i}", [128, 512], F32, es=esC) for i in range(8)]
            wls = [ctx.sem(f"{tag}_wl{i}") for i in range(4)]
            wcs = ctx.sem(f"{tag}_wcs")
            sst = ctx.sem(f"{tag}_sst")
            oyss = [ctx.sem(f"{tag}_oys{i}") for i in range(8)]
            och = ctx.sem(f"{tag}_och")
            with ExitStack() as esP:
                pso = [ctx.ps(f"{tag}_pso{i}", [128, 512], es=esP) for i in range(8)]
                sst0 = ctx.val(f"{tag}_sst")
                och0 = ctx.val(f"{tag}_och")
                wcs0 = ctx.val(f"{tag}_wcs")
                wls0 = [ctx.val(f"{tag}_wl{i}") for i in range(4)]
                pv_done = ctx.val(f"{tag}_pve")

                def _wo_trig(eng, sidx):
                    cb, j = divmod(sidx, 8)
                    eng.dma_start(
                        out=stgo[sidx % 4][:],
                        in_=wo_dram[j * 128:(j + 1) * 128,
                                    cb * 512:(cb + 1) * 512]
                        ).then_inc(wls[sidx % 4], 16)
                    ctx.inc(f"{tag}_wl{sidx % 4}", 16)

                @block.sync
                def _(sy):
                    for sidx in range(0, 64, 2):
                        if sidx >= 4:
                            sy.wait_ge(wcs, wcs0 + sidx - 3)
                        _wo_trig(sy, sidx)

                @block.scalar
                def _(s):
                    for sidx in (1, 3):
                        _wo_trig(s, sidx)
                    for sidx in range(64):
                        s.wait_ge(wls[sidx % 4], wls0[sidx % 4]
                                  + 16 * (sidx // 4 + 1))
                        if sidx >= 4:
                            s.wait_ge(sst, sst0 + sidx - 3)
                        s.activation(wos[sidx % 4][:], stgo[sidx % 4][:],
                                     AF.Copy, bias=0.0, scale=1.0
                                     ).then_inc(wcs, 1)
                        ctx.inc(f"{tag}_wcs")
                        so = sidx + 4
                        if so % 2 == 1 and so < 64:
                            s.wait_ge(wcs, wcs0 + so - 3)
                            _wo_trig(s, so)

                @block.gpsimd
                def _(g):
                    g.wait_ge(pve, pv_done)
                    if y_wait is not None:
                        g.wait_ge(ctx.sems[y_wait[0]], y_wait[1])
                    for cb in range(8):
                        for ti in range(8):
                            idx = cb * 8 + ti
                            slot = idx % 8
                            g.wait_ge(och, och0 + idx + 1)
                            g.dma_start(
                                out=y_out[ti * 128:(ti + 1) * 128,
                                          cb * 512:(cb + 1) * 512],
                                in_=tmp[slot][:],
                                accum_op=ALU.add).then_inc(oyss[slot], 16)
                            ctx.inc(f"{tag}_oys{slot}", 16)

                @block.tensor
                def _(t):
                    t.wait_ge(rle, rl_done)
                    for cb in range(8):
                        if cb >= 1:
                            t.wait_ge(och, och0 + 8 * cb)
                        for j in range(8):
                            sidx = cb * 8 + j
                            t.wait_ge(wcs, wcs0 + sidx + 1)
                            for ti in range(8):
                                ins = t.matmul(
                                    pso[ti][:],
                                    obf[:, j * 1024 + ti * 128:j * 1024 + (ti + 1) * 128],
                                    wos[sidx % 4][:],
                                    start=(j == 0), stop=(j == 7))
                            ins.then_inc(sst, 1)
                            ctx.inc(f"{tag}_sst")

                @block.vector
                def _(v):
                    v.wait_ge(rle, rl_done)
                    for cb in range(8):
                        v.wait_ge(sst, sst0 + (cb + 1) * 8)
                        for ti in range(8):
                            idx = cb * 8 + ti
                            slot = idx % 8
                            if idx >= 8:
                                v.wait_ge(oyss[slot], 16 * (idx // 8))
                            v.scalar_tensor_tensor(
                                tmp[slot][:], pso[ti][:], rlT[:, ti:ti + 1],
                                bo0B[:, cb * 512:(cb + 1) * 512],
                                ALU.mult, ALU.add).then_inc(och, 1)
                            ctx.inc(f"{tag}_och")

    ctx.cnt[f"{tag}_done"] = [(f"{tag}_oys{i}", ctx.val(f"{tag}_oys{i}"))
                              for i in range(8)]



def emit_ffn1(ctx, block, xlnT, wgT_dram, wuT_dram, wdT_dram, ag_dram,
              au_dram, ad_dram, h_dram, y_out, tag, xr_wait=None,
              y_wait=None, close_between=None):
    with ExitStack() as esA:
        wg = [ctx.sb(f"{tag}_wg{i}", [128, 8192], BF16, es=esA) for i in range(2)]
        wu = [ctx.sb(f"{tag}_wu{i}", [128, 8192], BF16, es=esA) for i in range(2)]
        g_s = [ctx.sb(f"{tag}_gs{i}", [128, 1024], F32, es=esA) for i in range(2)]
        ht = [ctx.sb(f"{tag}_ht{i}", [128, 1024], BF16, es=esA) for i in range(4)]
        ag_pm = ctx.sb(f"{tag}_agpm", [128, 86], F32, es=esA)
        au_pm = ctx.sb(f"{tag}_aupm", [128, 86], F32, es=esA)
        stga = [ctx.sb(f"{tag}_sga{i}", [128, 1024], F32, es=esA)
                for i in range(8)]

        lga = [ctx.sem(f"{tag}_lga{i}") for i in range(8)]
        cga = ctx.sem(f"{tag}_cga")
        gbd = ctx.sem(f"{tag}_gbd")
        gac = ctx.sem(f"{tag}_gac")
        gdv = ctx.sem(f"{tag}_gdv")
        hsts = [ctx.sem(f"{tag}_hst{i}") for i in range(4)]
        pvv = ctx.sem(f"{tag}_pv")

        @block.sync
        def _(sy):
            sy.dma_start(out=ag_pm[:],
                         in_=ag_dram.rearrange("(c p) -> p c", p=128),
                         allow_slow_non_contiguous=True).then_inc(pvv, 16)
            sy.dma_start(out=au_pm[:],
                         in_=au_dram.rearrange("(c p) -> p c", p=128),
                         allow_slow_non_contiguous=True).then_inc(pvv, 16)
            ctx.inc(f"{tag}_pv", 32)

        with ExitStack() as esP:
            ps = [ctx.ps(f"{tag}_ps{i}", [128, 512], es=esP) for i in range(8)]
            gbd0 = ctx.val(f"{tag}_gbd")
            gac0 = ctx.val(f"{tag}_gac")
            gdv0 = ctx.val(f"{tag}_gdv")
            cga0 = ctx.val(f"{tag}_cga")
            lga0 = [ctx.val(f"{tag}_lga{i}") for i in range(8)]

            def _gu_trig(eng, c):
                ig, j = divmod(c, 16)
                src = wgT_dram if j < 8 else wuT_dram
                jj = j % 8
                eng.dma_start(
                    out=stga[c % 8][:].rearrange("p (a x) -> p a x", a=4),
                    in_=src.rearrange("(a p) i -> p a i", p=128)
                        [:, 4 * jj:4 * jj + 4, ig * 256:(ig + 1) * 256],
                ).then_inc(lga[c % 8], 16)
                ctx.inc(f"{tag}_lga{c % 8}", 16)

            @block.sync
            def _(sy):
                def _h_store(ig):
                    for c2 in range(2):
                        ic = 2 * ig + c2
                        slot = c2 * 2 + ig % 2
                        sy.wait_ge(gdv, gdv0 + ig * 4 + c2 * 2 + 2)
                        sy.dma_start(out=h_dram[ic * 128:(ic + 1) * 128, :],
                                     in_=ht[slot][:]).then_inc(hsts[slot], 16)
                        ctx.inc(f"{tag}_hst{slot}", 16)

                for c in range(0, 688, 2):
                    if c >= 8:
                        sy.wait_ge(cga, cga0 + c - 7)
                    elif cga0 > 0:
                        sy.wait_ge(cga, cga0)
                    _gu_trig(sy, c)
                    if c % 16 == 0 and c >= 32:
                        _h_store(c // 16 - 2)
                for ig in (41, 42):
                    _h_store(ig)

            @block.tensor
            def _(t):
                if xr_wait is not None:
                    t.wait_ge(ctx.sems[xr_wait[0]], xr_wait[1])
                for ig in range(43):
                    par = ig % 2
                    t.wait_ge(cga, cga0 + 16 * (ig + 1))
                    if ig >= 1:
                        t.wait_ge(gac, gac0 + 4 * ig)
                        t.wait_ge(gdv, gdv0 + 4 * ig)
                    for c2 in range(2):
                        for gu in range(2):
                            wt = wg[par] if gu == 0 else wu[par]
                            for th in range(2):
                                bank = c2 * 4 + gu * 2 + th
                                for k in range(32):
                                    ins = t.matmul(
                                        ps[bank][:],
                                        wt[:, k * 256 + c2 * 128:k * 256 + (c2 + 1) * 128],
                                        xlnT[:, k * 1024 + th * 512:k * 1024 + (th + 1) * 512],
                                        start=(k == 0), stop=(k == 31))
                    ins.then_inc(gbd, 1)
                    ctx.inc(f"{tag}_gbd")

            def _gu_silu(s, ig):
                s.wait_ge(gbd, gbd0 + ig + 1)
                if ig >= 1:
                    s.wait_ge(gdv, gdv0 + 4 * ig)
                for c2 in range(2):
                    ic = 2 * ig + c2
                    for th in range(2):
                        s.activation(g_s[c2][:, th * 512:(th + 1) * 512],
                                     ps[c2 * 4 + th][:], AF.Silu, bias=0.0,
                                     scale=ag_pm[:, ic:ic + 1]).then_inc(gac, 1)
                        ctx.inc(f"{tag}_gac")

            @block.scalar
            def _(s):
                s.wait_ge(pvv, ctx.val(f"{tag}_pv"))
                for c in (1, 3, 5, 7):
                    _gu_trig(s, c)
                for ci in range(688):
                    ig, j = divmod(ci, 16)
                    if j == 0 and ig >= 2:
                        s.wait_ge(gbd, gbd0 + ig - 1)
                    s.wait_ge(lga[ci % 8], lga0[ci % 8] + 16 * (ci // 8 + 1))
                    mat = wg[ig % 2] if j < 8 else wu[ig % 2]
                    jj = j % 8
                    s.activation(mat[:, 1024 * jj:1024 * (jj + 1)],
                                 stga[ci % 8][:], AF.Copy,
                                 bias=0.0, scale=1.0).then_inc(cga, 1)
                    ctx.inc(f"{tag}_cga")
                    co = ci + 8
                    if co % 2 == 1 and co < 688:
                        s.wait_ge(cga, cga0 + co - 7)
                        _gu_trig(s, co)
                    if j == 15 and ig >= 1:
                        _gu_silu(s, ig - 1)
                for igl in (42,):
                    _gu_silu(s, igl)

            @block.vector
            def _(v):
                for ig in range(43):
                    for c2 in range(2):
                        ic = 2 * ig + c2
                        slot = c2 * 2 + ig % 2
                        for th in range(2):
                            v.wait_ge(gac, gac0 + ig * 4 + c2 * 2 + th + 1)
                            if ig >= 2:
                                v.wait_ge(hsts[slot], 16 * (ig // 2))
                            v.scalar_tensor_tensor(
                                ht[slot][:, th * 512:(th + 1) * 512],
                                ps[c2 * 4 + 2 + th][:], au_pm[:, ic:ic + 1],
                                g_s[c2][:, th * 512:(th + 1) * 512],
                                ALU.mult, ALU.mult).then_inc(gdv, 1)
                            ctx.inc(f"{tag}_gdv")

        h_done = [(f"{tag}_hst{i}", ctx.val(f"{tag}_hst{i}")) for i in range(4)]

    if close_between is not None:
        close_between.close()
    with ExitStack() as esB:
        halfadB = ctx.sb(f"{tag}_hadB", [128, 4096], F32, es=esB)
        emit_bcast(ctx, block, ad_dram, 4096, halfadB, f"{tag}_ad", scale=0.5)
        ad_done = ctx.val(f"{tag}_ad_done")
        ad_row_ld = ctx.val("bc_ld")
        ad_row_mm = ctx.val("bc_mm")
        hres = ctx.sb(f"{tag}_hres", [128, 43 * 1024], BF16, es=esB)
        wds = [ctx.sb(f"{tag}_wds{i}", [128, 512], BF16, es=esB) for i in range(4)]
        stgb = [ctx.sb(f"{tag}_sgb{i}", [128, 256], F32, es=esB) for i in range(4)]
        tmp = [ctx.sb(f"{tag}_tmp{i}", [128, 512], F32, es=esB) for i in range(8)]

        hl = ctx.sem(f"{tag}_hl")
        ldb = [ctx.sem(f"{tag}_ldb{i}") for i in range(4)]
        cgb = ctx.sem(f"{tag}_cgb")
        dmm = ctx.sem(f"{tag}_dmm")
        dev = ctx.sem(f"{tag}_dev")
        dyss = [ctx.sem(f"{tag}_dys{i}") for i in range(8)]

        with ExitStack() as esP:
            psd = [ctx.ps(f"{tag}_psd{i}", [128, 512], es=esP) for i in range(8)]
            for ih in range(2):
                hl0 = ctx.val(f"{tag}_hl")
                dmm0 = ctx.val(f"{tag}_dmm")
                dev0 = ctx.val(f"{tag}_dev")
                dys0 = [ctx.val(f"{tag}_dys{i}") for i in range(8)]
                cgb0 = ctx.val(f"{tag}_cgb")
                ldb0 = [ctx.val(f"{tag}_ldb{i}") for i in range(4)]

                def _dn_trig(eng, c, ih=ih, cgb0=cgb0):
                    sidx, h = divmod(c, 2)
                    n, i = divmod(sidx, 43)
                    eng.dma_start(
                        out=stgb[c % 4][:],
                        in_=wdT_dram[(ih * 43 + i) * 128:
                                     (ih * 43 + i + 1) * 128,
                                     n * 512 + h * 256:n * 512 + (h + 1) * 256]
                        ).then_inc(ldb[c % 4], 16)
                    ctx.inc(f"{tag}_ldb{c % 4}", 16)

                @block.sync
                def _(sy, ih=ih, dmm0=dmm0):
                    if ih == 0:
                        for sname, sval in h_done:
                            sy.wait_ge(ctx.sems[sname], sval)
                        sy.wait_ge(ctx.sems["bc_ld"], ad_row_ld)
                        sy.wait_ge(ctx.sems["bc_mm"], ad_row_mm)
                    else:
                        sy.wait_ge(ctx.sems[f"{tag}_dmm"], dmm0)
                    for i in range(43):
                        sy.dma_start(out=hres[:, i * 1024:(i + 1) * 1024],
                                     in_=h_dram[(ih * 43 + i) * 128:
                                                (ih * 43 + i + 1) * 128, :]
                                     ).then_inc(hl, 16)
                        ctx.inc(f"{tag}_hl", 16)
                    for c in range(0, 688, 2):
                        if c >= 4:
                            sy.wait_ge(cgb, cgb0 + c - 3)
                        elif cgb0 > 0:
                            sy.wait_ge(cgb, cgb0)
                        _dn_trig(sy, c)

                @block.scalar
                def _(s, ih=ih, dmm0=dmm0, cgb0=cgb0, ldb0=ldb0):
                    for c in (1, 3):
                        if cgb0 > 0:
                            s.wait_ge(cgb, cgb0)
                        _dn_trig(s, c)
                    for ci in range(688):
                        sidx, h = divmod(ci, 2)
                        s.wait_ge(ldb[ci % 4], ldb0[ci % 4] + 16 * (ci // 4 + 1))
                        if h == 0:
                            if sidx >= 4:
                                s.wait_ge(dmm, dmm0 + sidx - 3)
                            elif dmm0 > 0:
                                s.wait_ge(dmm, dmm0)
                        s.activation(wds[sidx % 4][:, h * 256:(h + 1) * 256],
                                     stgb[ci % 4][:], AF.Copy,
                                     bias=0.0, scale=1.0).then_inc(cgb, 1)
                        ctx.inc(f"{tag}_cgb")
                        co = ci + 4
                        if co % 2 == 1 and co < 688:
                            s.wait_ge(cgb, cgb0 + co - 3)
                            _dn_trig(s, co)

                @block.gpsimd
                def _(g, ih=ih, dev0=dev0, dys0=dys0):
                    for n in range(8):
                        for ti in range(8):
                            idx = n * 8 + ti
                            slot = idx % 8
                            g.wait_ge(dev, dev0 + idx + 1)
                            if ih == 1:
                                g.wait_ge(dyss[slot], dys0[slot])
                            g.dma_start(
                                out=y_out[ti * 128:(ti + 1) * 128,
                                          n * 512:(n + 1) * 512],
                                in_=tmp[slot][:],
                                accum_op=ALU.add).then_inc(dyss[slot], 16)
                            ctx.inc(f"{tag}_dys{slot}", 16)

                @block.tensor
                def _(t, ih=ih, hl0=hl0, dmm0=dmm0, dev0=dev0, cgb0=cgb0):
                    if ih == 0:
                        t.wait_ge(ctx.sems["bc_ev"], ad_done)
                    t.wait_ge(hl, hl0 + 16 * 43)
                    for n in range(8):
                        if n >= 1:
                            t.wait_ge(dev, dev0 + 8 * n)
                        elif dev0 > 0:
                            t.wait_ge(dev, dev0)
                        for i in range(43):
                            sidx = n * 43 + i
                            t.wait_ge(cgb, cgb0 + 2 * (sidx + 1))
                            for ti in range(8):
                                ins = t.matmul(
                                    psd[ti][:],
                                    hres[:, i * 1024 + ti * 128:i * 1024 + (ti + 1) * 128],
                                    wds[sidx % 4][:],
                                    start=(i == 0), stop=(i == 42))
                            ins.then_inc(dmm, 1)
                            ctx.inc(f"{tag}_dmm")

                @block.vector
                def _(v, ih=ih, dmm0=dmm0, dys0=dys0):
                    v.wait_ge(ctx.sems["bc_ev"], ad_done)
                    if ih == 0 and y_wait is not None:
                        v.wait_ge(ctx.sems[y_wait[0]], y_wait[1])
                    for n in range(8):
                        v.wait_ge(dmm, dmm0 + 43 * (n + 1))
                        for ti in range(8):
                            idx = n * 8 + ti
                            slot = idx % 8
                            if idx >= 8:
                                v.wait_ge(dyss[slot], dys0[slot] + 16 * (idx // 8))
                            elif dys0[slot] > 0:
                                v.wait_ge(dyss[slot], dys0[slot])
                            v.tensor_mul(tmp[slot][:], psd[ti][:],
                                         halfadB[:, n * 512:(n + 1) * 512]
                                         ).then_inc(dev, 1)
                            ctx.inc(f"{tag}_dev")

    ctx.cnt[f"{tag}_done"] = [(f"{tag}_dys{i}", ctx.val(f"{tag}_dys{i}"))
                              for i in range(8)]



def emit_proj_tok(ctx, block, xlnT, wT_dram, bias_dram, out_dram, F, tag,
                  xr_wait=None):
    blocks = []
    off = 0
    while off < F:
        w = min(512, F - off)
        blocks.append((off, w))
        off += w
    with ExitStack() as es:
        wb = [ctx.sb(f"{tag}_wb{i}", [128, 16384], BF16, es=es) for i in range(2)]
        bB = ctx.sb(f"{tag}_bB", [128, F], F32, es=es)
        ot = [ctx.sb(f"{tag}_ot{i}", [128, 512], BF16, es=es) for i in range(2)]
        emit_bcast(ctx, block, bias_dram, F, bB, f"{tag}_b",
                   extra_waits=([xr_wait] if xr_wait is not None else []))
        b_done = ctx.val(f"{tag}_b_done")

        wpa = [ctx.sem(f"{tag}_wpa{i}") for i in range(2)]
        pmm = ctx.sem(f"{tag}_pmm")
        pev = ctx.sem(f"{tag}_pev")
        psts = [ctx.sem(f"{tag}_pst{i}") for i in range(2)]

        with ExitStack() as esP:
            ps = [ctx.ps(f"{tag}_ps{i}", [128, 512], es=esP) for i in range(8)]
            wpa0 = [ctx.val(f"{tag}_wpa{i}") for i in range(2)]
            pmm0, pev0 = ctx.val(f"{tag}_pmm"), ctx.val(f"{tag}_pev")
            pst0 = [ctx.val(f"{tag}_pst{i}") for i in range(2)]

            @block.gpsimd
            def _(g):
                for fb, (off, w) in enumerate(blocks):
                    par = fb % 2
                    if fb >= 2:
                        g.wait_ge(pmm, pmm0 + fb - 1)
                    for q in range(4):
                        g.dma_start(
                            out=wb[par][:, 0:32 * w].rearrange("p (a x) -> p a x", a=32)
                                [:, q * 8:(q + 1) * 8, :],
                            in_=wT_dram.rearrange("(a p) f -> p a f", p=128)
                                [:, q * 8:(q + 1) * 8, off:off + w],
                        ).then_inc(wpa[par], 16)
                        ctx.inc(f"{tag}_wpa{par}", 16)

            @block.tensor
            def _(t):
                if xr_wait is not None:
                    t.wait_ge(ctx.sems[xr_wait[0]], xr_wait[1])
                t.wait_ge(ctx.sems["bc_ev"], b_done)
                for fb, (off, w) in enumerate(blocks):
                    par = fb % 2
                    t.wait_ge(wpa[par], wpa0[par] + 64 * (fb // 2 + 1))
                    if fb >= 1:
                        t.wait_ge(pev, pev0 + 8 * fb)
                    for k in range(32):
                        for ti in range(8):
                            ins = t.matmul(
                                ps[ti][:, 0:w],
                                xlnT[:, k * 1024 + ti * 128:k * 1024 + (ti + 1) * 128],
                                wb[par][:, k * w:(k + 1) * w],
                                start=(k == 0), stop=(k == 31))
                    ins.then_inc(pmm, 1)
                    ctx.inc(f"{tag}_pmm")

            @block.vector
            def _(v):
                for fb, (off, w) in enumerate(blocks):
                    v.wait_ge(pmm, pmm0 + fb + 1)
                    for ti in range(8):
                        idx = fb * 8 + ti
                        slot = idx % 2
                        if idx >= 2:
                            v.wait_ge(psts[slot], pst0[slot] + 16 * (idx // 2))
                        v.tensor_add(ot[slot][:, 0:w], ps[ti][:, 0:w],
                                     bB[:, off:off + w]).then_inc(pev, 1)
                        ctx.inc(f"{tag}_pev")

            @block.sync
            def _(sy):
                for fb, (off, w) in enumerate(blocks):
                    for ti in range(8):
                        idx = fb * 8 + ti
                        slot = idx % 2
                        sy.wait_ge(pev, pev0 + idx + 1)
                        sy.dma_start(out=out_dram[ti * 128:(ti + 1) * 128,
                                                  off:off + w],
                                     in_=ot[slot][:, 0:w]).then_inc(psts[slot], 16)
                        ctx.inc(f"{tag}_pst{slot}", 16)

    ctx.cnt[f"{tag}_done"] = [(f"{tag}_pst{i}", ctx.val(f"{tag}_pst{i}"))
                              for i in range(2)]



def emit_s5(ctx, block, qc_dram, kvcf_dram, wquT_dram, bqu_dram,
            wkvuT_dram, bkvu_dram, wo1T_dram, bo1_dram, y_out,
            tag="s5", y_wait=None):
    SC = 1.0 / float(np.sqrt(192.0))
    with ExitStack() as esA:
        qcT = ctx.sb(f"{tag}_qcT", [128, 12 * 1024], BF16, es=esA)
        kvT = ctx.sb(f"{tag}_kvT", [128, 4 * 4096], BF16, es=esA)
        krop = ctx.sb(f"{tag}_krop", [65, 4096], BF16, es=esA)
        bquA = ctx.sb(f"{tag}_bquA", [128, 32], F32, es=esA)
        bquB = ctx.sb(f"{tag}_bquB", [64, 32], F32, es=esA)
        bkvK = ctx.sb(f"{tag}_bkvK", [128, 32], F32, es=esA)
        bkvV = ctx.sb(f"{tag}_bkvV", [128, 32], F32, es=esA)
        wo1sb = ctx.sb(f"{tag}_wo1", [128, 4096], BF16, es=esA)
        osum = ctx.sb(f"{tag}_osum", [128, 8 * 128], F32, es=esA)
        ropen2 = ctx.sb(f"{tag}_rn2", [1, 4096], BF16, es=esA)
        kvrt = ctx.sb(f"{tag}_kvrt", [128, 32 * 64], BF16, es=esA)

        ld = ctx.sem(f"{tag}_ld")
        pdv = ctx.sem(f"{tag}_pdv")

        @block.sync
        def _(sy):
            for a in range(12):
                sy.dma_start(out=qcT[:, a * 1024:(a + 1) * 1024],
                             in_=qc_dram[:, a * 128:(a + 1) * 128],
                             transpose=True).then_inc(ld, 16)
                ctx.inc(f"{tag}_ld", 16)
            for a in range(4):
                sy.dma_start(out=kvT[:, a * 4096:(a + 1) * 4096],
                             in_=kvcf_dram[:, a * 128:(a + 1) * 128],
                             transpose=True).then_inc(ld, 16)
                ctx.inc(f"{tag}_ld", 16)
            for q in range(4):
                sy.dma_start(
                    out=kvrt[:].rearrange("p (a x) -> p a x", a=32)
                        [:, q * 8:(q + 1) * 8, :],
                    in_=kvcf_dram.rearrange("(a p) f -> p a f", p=128)
                        [:, q * 8:(q + 1) * 8, 512:576]).then_inc(ld, 16)
                ctx.inc(f"{tag}_ld", 16)
            for dst, src, x in ((bquA, bqu_dram, (192, 0, 128)),
                                (bquB, bqu_dram, (192, 128, 192)),
                                (bkvK, bkvu_dram, (256, 0, 128)),
                                (bkvV, bkvu_dram, (256, 128, 256))):
                per, lo, hi = x
                sy.dma_start(out=dst[:],
                             in_=src.rearrange("(h x) -> x h", x=per)[lo:hi, :],
                             allow_slow_non_contiguous=True).then_inc(ld, 16)
                ctx.inc(f"{tag}_ld", 16)

        gld = ctx.sem(f"{tag}_gld")

        @block.gpsimd
        def _(g):
            g.wait_ge(ctx.sems["g_init"], ctx.cnt["g_ready"])
            g.memset(krop[64:65, :], 1.0).then_inc(pdv, 1)
            ctx.inc(f"{tag}_pdv")
            g.dma_start(out=wo1sb[:], in_=wo1T_dram).then_inc(gld, 16)
            ctx.inc(f"{tag}_gld", 16)
        ld_all = ctx.val(f"{tag}_ld")

        @block.vector
        def _(v):
            v.wait_ge(ld, ld_all)
            v.wait_ge(gld, ctx.val(f"{tag}_gld"))
            v.tensor_scalar_mul(bquA[:], bquA[:], SC).then_inc(pdv, 1)
            ctx.inc(f"{tag}_pdv")
            v.wait_ge(pdv, ctx.val(f"{tag}_pdv"))
            v.tensor_scalar_mul(bquB[:], bquB[:], SC).then_inc(pdv, 1)
            ctx.inc(f"{tag}_pdv")
            v.memset(osum[:], 0.0).then_inc(pdv, 1)
            ctx.inc(f"{tag}_pdv")
        prep_pdv = ctx.val(f"{tag}_pdv")

        with ExitStack() as esB:
            qfA = [ctx.sb(f"{tag}_qfA{i}", [128, 1024], BF16, es=esB) for i in range(2)]
            qfB = [ctx.sb(f"{tag}_qfB{i}", [65, 1024], BF16, es=esB) for i in range(2)]
            knT = [ctx.sb(f"{tag}_knT{i}", [128, 4096], BF16, es=esB) for i in range(2)]
            vft = ctx.sb(f"{tag}_vft", [128, 4096], BF16, es=esB)
            vtok = [ctx.sb(f"{tag}_vtk{i}", [128, 32 * 128], BF16, es=esB) for i in range(2)]
            wqu = [ctx.sb(f"{tag}_wqu{i}", [128, 12 * 192], BF16, es=esB) for i in range(2)]
            wkvu = [ctx.sb(f"{tag}_wkvu{i}", [128, 4 * 256], BF16, es=esB) for i in range(2)]
            pt = [ctx.sb(f"{tag}_pt{i}", [128, 512], BF16, es=esB) for i in range(32)]
            sqs = ctx.sb(f"{tag}_sqs", [128, 1024], BF16, es=esB)
            obuf = ctx.sb(f"{tag}_obuf", [128, 1024], BF16, es=esB)
            qn2row = ctx.sb(f"{tag}_qn2", [1, 1024], F32, es=esB)
            rlrow = ctx.sb(f"{tag}_rlr", [1, 1024], F32, es=esB)
            rlT = ctx.sb(f"{tag}_rlT", [128, 8], F32, es=esB)
            kmx = ctx.sb(f"{tag}_kmx", [1, 4], F32, es=esB)

            wls = [ctx.sem(f"{tag}_wl{i}") for i in range(4)]
            cmm = ctx.sem(f"{tag}_cmm")
            cac = ctx.sem(f"{tag}_cac")
            cdv = ctx.sem(f"{tag}_cdv")
            nmm = ctx.sem(f"{tag}_nmm")
            nac = ctx.sem(f"{tag}_nac")
            nrm = ctx.sem(f"{tag}_nrm")
            smm = ctx.sem(f"{tag}_smm")
            sex = ctx.sem(f"{tag}_sex")
            lmm = ctx.sem(f"{tag}_lmm")
            pvm = ctx.sem(f"{tag}_pvm")
            pve = ctx.sem(f"{tag}_pve")
            rle = ctx.sem(f"{tag}_rle")
            ocv = ctx.sem(f"{tag}_ocv")
            trp = ctx.sem(f"{tag}_trp")

            with ExitStack() as esP:
                psA = [ctx.ps(f"{tag}_psA{i}", [128, 512], es=esP) for i in range(4)]
                psB = ctx.ps(f"{tag}_psB", [128, 1024], BF16, es=esP)
                ps1 = ctx.ps(f"{tag}_ps1", [1, 512], es=esP)

                trp_k0 = ctx.val(f"{tag}_trp")
                cdv_k0 = ctx.val(f"{tag}_cdv")

                @block.tensor
                def _(t):
                    t.wait_ge(ctx.sems["g_init"], ctx.cnt["g_ready"])
                    t.wait_ge(ld, ld_all)
                    for pk in range(4):
                        for j in range(8):
                            blk = pk * 8 + j
                            ins = t.transpose(psB[0:64, j * 128:(j + 1) * 128],
                                              kvrt[:, blk * 64:(blk + 1) * 64],
                                              ctx.ident[:])
                        ins.then_inc(trp, 1)
                        ctx.inc(f"{tag}_trp")
                        t.wait_ge(cdv, cdv_k0 + pk + 1)

                @block.vector
                def _(v):
                    for pk in range(4):
                        v.wait_ge(trp, trp_k0 + pk + 1)
                        v.tensor_copy(krop[0:64, pk * 1024:(pk + 1) * 1024],
                                      psB[0:64, :]).then_inc(cdv, 1)
                        ctx.inc(f"{tag}_cdv")
                krop_cdv = ctx.val(f"{tag}_cdv")

                nac_r0 = ctx.val(f"{tag}_nac")
                nmm_r0 = ctx.val(f"{tag}_nmm")
                nrm_r0 = ctx.val(f"{tag}_nrm")

                @block.scalar
                def _(s):
                    s.wait_ge(ld, ld_all)
                    s.wait_ge(pdv, prep_pdv)
                    s.wait_ge(cdv, krop_cdv)
                    for b in range(4):
                        if b >= 1:
                            s.wait_ge(nmm, nmm_r0 + 2 * b)
                        s.activation(sqs[0:64, :],
                                     krop[0:64, b * 1024:(b + 1) * 1024],
                                     AF.Square, bias=0.0, scale=1.0).then_inc(nac, 1)
                        ctx.inc(f"{tag}_nac")

                @block.tensor
                def _(t):
                    t.wait_ge(ctx.sems["g_init"], ctx.cnt["g_ready"])
                    for i in range(8):
                        b, j = divmod(i, 2)
                        t.wait_ge(nac, nac_r0 + b + 1)
                        if i >= 1:
                            t.wait_ge(nrm, nrm_r0 + i)
                        t.matmul(ps1[:], ctx.ones_c64[:],
                                 sqs[0:64, j * 512:(j + 1) * 512],
                                 start=True, stop=True).then_inc(nmm, 1)
                        ctx.inc(f"{tag}_nmm")

                @block.vector
                def _(v):
                    for i in range(8):
                        v.wait_ge(nmm, nmm_r0 + i + 1)
                        v.tensor_copy(ropen2[0:1, i * 512:(i + 1) * 512],
                                      ps1[:]).then_inc(nrm, 1)
                        ctx.inc(f"{tag}_nrm")
                rope_nrm = ctx.val(f"{tag}_nrm")

                for h in range(32):
                    par = h % 2
                    wl0 = ctx.val(f"{tag}_wl{par}")
                    cmm0 = ctx.val(f"{tag}_cmm")
                    cac0 = ctx.val(f"{tag}_cac")
                    cdv0 = ctx.val(f"{tag}_cdv")
                    pve_pre = ctx.val(f"{tag}_pve")
                    rle_pre = ctx.val(f"{tag}_rle")
                    ocv_pre = ctx.val(f"{tag}_ocv")

                    @block.gpsimd
                    def _(g, h=h, par=par, cmm0=cmm0):
                        if h >= 2:
                            g.wait_ge(cmm, cmm0 - 20)
                        for half2 in range(2):
                            g.dma_start(
                                out=wqu[par][:].rearrange("p (a x) -> p a x", a=12)
                                    [:, half2 * 6:(half2 + 1) * 6, :],
                                in_=wquT_dram.rearrange("(a p) f -> p a f", p=128)
                                    [:, half2 * 6:(half2 + 1) * 6,
                                     h * 192:(h + 1) * 192],
                            ).then_inc(wls[par], 16)
                            ctx.inc(f"{tag}_wl{par}", 16)
                        g.dma_start(
                            out=wkvu[par][:].rearrange("p (a x) -> p a x", a=4),
                            in_=wkvuT_dram.rearrange("(a p) f -> p a f", p=128)
                                [:, :, h * 256:(h + 1) * 256],
                        ).then_inc(wls[par], 16)
                        ctx.inc(f"{tag}_wl{par}", 16)

                    @block.tensor
                    def _(t, h=h, par=par, wl0=wl0, cac0=cac0,
                          pve_pre=pve_pre, ocv_pre=ocv_pre):
                        t.wait_ge(wls[par], wl0 + 48)
                        if h >= 1:
                            t.wait_ge(cac, cac0)
                        if pve_pre > 0:
                            t.wait_ge(pve, pve_pre)
                        else:
                            t.wait_ge(nrm, rope_nrm)
                        if ocv_pre > 0:
                            t.wait_ge(ocv, ocv_pre)
                        g_i = 0
                        for dc in range(2):
                            dw = 128 if dc == 0 else 64
                            for tb in range(2):
                                if g_i >= 4:
                                    t.wait_ge(cac, cac0 + g_i - 3)
                                for a in range(12):
                                    ins = t.matmul(
                                        psA[g_i % 4][0:dw, :],
                                        wqu[par][:, a * 192 + dc * 128:
                                                 a * 192 + dc * 128 + dw],
                                        qcT[:, a * 1024 + tb * 512:
                                            a * 1024 + (tb + 1) * 512],
                                        start=(a == 0), stop=(a == 11))
                                ins.then_inc(cmm, 1)
                                ctx.inc(f"{tag}_cmm")
                                g_i += 1
                        for prod in range(2):
                            for tb in range(8):
                                if g_i >= 4:
                                    t.wait_ge(cac, cac0 + g_i - 3)
                                for a in range(4):
                                    ins = t.matmul(
                                        psA[g_i % 4][:],
                                        wkvu[par][:, a * 256 + prod * 128:
                                                  a * 256 + (prod + 1) * 128],
                                        kvT[:, a * 4096 + tb * 512:
                                            a * 4096 + (tb + 1) * 512],
                                        start=(a == 0), stop=(a == 3))
                                ins.then_inc(cmm, 1)
                                ctx.inc(f"{tag}_cmm")
                                g_i += 1

                    @block.scalar
                    def _(s, h=h, par=par, cmm0=cmm0):
                        g_i = 0
                        for dc in range(2):
                            dw = 128 if dc == 0 else 64
                            bias = bquA if dc == 0 else bquB
                            dst = qfA[par] if dc == 0 else qfB[par]
                            for tb in range(2):
                                s.wait_ge(cmm, cmm0 + g_i + 1)
                                s.activation(dst[0:dw, tb * 512:(tb + 1) * 512],
                                             psA[g_i % 4][0:dw, :], AF.Identity,
                                             bias=bias[0:dw, h:h + 1],
                                             scale=ctx.consts[0:dw, 4:5]
                                             ).then_inc(cac, 1)
                                ctx.inc(f"{tag}_cac")
                                g_i += 1
                        for prod in range(2):
                            bias = bkvK if prod == 0 else bkvV
                            for tb in range(8):
                                s.wait_ge(cmm, cmm0 + g_i + 1)
                                dst = knT[par] if prod == 0 else vft
                                s.activation(dst[:, tb * 512:(tb + 1) * 512],
                                             psA[g_i % 4][:], AF.Identity,
                                             bias=bias[:, h:h + 1],
                                             scale=1.0).then_inc(cac, 1)
                                ctx.inc(f"{tag}_cac")
                                g_i += 1
                    cac_prod = ctx.val(f"{tag}_cac")

                    @block.tensor
                    def _(t, h=h, cac_prod=cac_prod, cdv0=cdv0,
                          ocv_pre=ocv_pre):
                        t.wait_ge(cac, cac_prod)
                        if ocv_pre > 0:
                            t.wait_ge(ocv, ocv_pre)
                        for pk in range(4):
                            if pk >= 1:
                                t.wait_ge(cdv, cdv0 + pk)
                            for j in range(8):
                                blk = pk * 8 + j
                                ins = t.transpose(psB[:, j * 128:(j + 1) * 128],
                                                  vft[:, blk * 128:(blk + 1) * 128],
                                                  ctx.ident[:])
                            ins.then_inc(trp, 1)
                            ctx.inc(f"{tag}_trp")

                    trp0 = ctx.val(f"{tag}_trp") - 4

                    @block.vector
                    def _(v, h=h, par=par, trp0=trp0):
                        for pk in range(4):
                            v.wait_ge(trp, trp0 + pk + 1)
                            v.tensor_copy(
                                vtok[par][:, pk * 1024:(pk + 1) * 1024],
                                psB[:]).then_inc(cdv, 1)
                            ctx.inc(f"{tag}_cdv")

                    nac0 = ctx.val(f"{tag}_nac")
                    nmm0 = ctx.val(f"{tag}_nmm")
                    nrm0 = ctx.val(f"{tag}_nrm")

                    @block.scalar
                    def _(s, h=h, par=par, nmm0=nmm0):
                        for q2 in range(2):
                            if q2 == 1:
                                s.wait_ge(nmm, nmm0 + 1)
                            s.activation(sqs[:, 0:512],
                                         qfA[par][:, q2 * 512:(q2 + 1) * 512],
                                         AF.Square, bias=0.0, scale=1.0)
                            s.activation(sqs[0:64, 512:1024],
                                         qfB[par][0:64, q2 * 512:(q2 + 1) * 512],
                                         AF.Square, bias=0.0, scale=1.0
                                         ).then_inc(nac, 1)
                            ctx.inc(f"{tag}_nac")
                        for b in range(8):
                            s.wait_ge(nmm, nmm0 + 2 + b)
                            s.activation(sqs[:, 0:512],
                                         knT[par][:, b * 512:(b + 1) * 512],
                                         AF.Square, bias=0.0, scale=1.0
                                         ).then_inc(nac, 1)
                            ctx.inc(f"{tag}_nac")

                    @block.tensor
                    def _(t, h=h, nac0=nac0, nrm0=nrm0, rle_pre=rle_pre):
                        if rle_pre > 0:
                            t.wait_ge(rle, rle_pre)
                        for q2 in range(2):
                            t.wait_ge(nac, nac0 + q2 + 1)
                            if q2 >= 1:
                                t.wait_ge(nrm, nrm0 + 1)
                            t.matmul(ps1[:], ctx.ones_col[:], sqs[:, 0:512],
                                     start=True, stop=False)
                            t.matmul(ps1[:], ctx.ones_c64[:], sqs[0:64, 512:1024],
                                     start=False, stop=True).then_inc(nmm, 1)
                            ctx.inc(f"{tag}_nmm")
                        for b in range(8):
                            t.wait_ge(nac, nac0 + 2 + b + 1)
                            t.wait_ge(nrm, (nrm0 + 2) if b == 0 else (nrm0 + 2 * b + 2))
                            t.matmul(ps1[:], ctx.ones_col[:], sqs[:, 0:512],
                                     start=True, stop=False)
                            t.matmul(ps1[:], ctx.ones_bf[0:1, 0:1],
                                     ropen2[0:1, b * 512:(b + 1) * 512],
                                     start=False, stop=True).then_inc(nmm, 1)
                            ctx.inc(f"{tag}_nmm")

                    @block.vector
                    def _(v, h=h, nmm0=nmm0, nrm0=nrm0):
                        for q2 in range(2):
                            v.wait_ge(nmm, nmm0 + q2 + 1)
                            v.tensor_copy(qn2row[0:1, q2 * 512:(q2 + 1) * 512],
                                          ps1[:]).then_inc(nrm, 1)
                            ctx.inc(f"{tag}_nrm")
                        v.memset(kmx[0:1, 0:1], 0.0).then_inc(nrm, 1)
                        ctx.inc(f"{tag}_nrm")
                        for b in range(8):
                            v.wait_ge(nmm, nmm0 + 2 + b + 1)
                            v.wait_ge(nrm, ctx.val(f"{tag}_nrm"))
                            v.reduce_max(kmx[0:1, 1:2], ps1[:],
                                         mybir.AxisListType.X).then_inc(nrm, 1)
                            ctx.inc(f"{tag}_nrm")
                            v.wait_ge(nrm, ctx.val(f"{tag}_nrm"))
                            v.tensor_max(kmx[0:1, 0:1], kmx[0:1, 0:1],
                                         kmx[0:1, 1:2]).then_inc(nrm, 1)
                            ctx.inc(f"{tag}_nrm")

                    @block.scalar
                    def _(s, h=h):
                        s.wait_ge(nrm, ctx.val(f"{tag}_nrm"))
                        s.activation(kmx[0:1, 2:3], kmx[0:1, 0:1], AF.Sqrt,
                                     bias=0.0, scale=1.0).then_inc(nac, 1)
                        ctx.inc(f"{tag}_nac")
                        s.wait_ge(nac, ctx.val(f"{tag}_nac"))
                        s.activation(qn2row[:], qn2row[:], AF.Sqrt,
                                     bias=0.0, scale=1.0).then_inc(nac, 1)
                        ctx.inc(f"{tag}_nac")

                    @block.vector
                    def _(v, h=h, par=par):
                        v.wait_ge(nac, ctx.val(f"{tag}_nac"))
                        v.tensor_scalar(qfB[par][64:65, :], qn2row[:],
                                        kmx[0:1, 2:3], -1.01,
                                        ALU.mult, ALU.mult).then_inc(nrm, 1)
                        ctx.inc(f"{tag}_nrm")
                    negb_nrm = ctx.val(f"{tag}_nrm")

                    for qb in range(2):
                        smm0 = ctx.val(f"{tag}_smm")
                        sex0 = ctx.val(f"{tag}_sex")
                        pvm0 = ctx.val(f"{tag}_pvm")
                        lmm0 = ctx.val(f"{tag}_lmm")
                        rle0 = ctx.val(f"{tag}_rle")
                        pve0 = ctx.val(f"{tag}_pve")

                        @block.tensor
                        def _(t, h=h, par=par, qb=qb, smm0=smm0, sex0=sex0,
                              pve0=pve0, rle0=rle0, cdv0=cdv0):
                            t.wait_ge(nrm, negb_nrm)
                            if pve0 > 0:
                                t.wait_ge(pve, pve0)
                            if qb == 0:
                                t.wait_ge(cdv, cdv0 + 4)
                            for kt in range(32):
                                if kt >= 3:
                                    t.wait_ge(sex, sex0 + kt - 2)
                                t.matmul(psA[kt % 3][:],
                                         knT[par][:, kt * 128:(kt + 1) * 128],
                                         qfA[par][:, qb * 512:(qb + 1) * 512],
                                         start=True, stop=False)
                                t.matmul(psA[kt % 3][:],
                                         krop[:, kt * 128:(kt + 1) * 128],
                                         qfB[par][:, qb * 512:(qb + 1) * 512],
                                         start=False, stop=True).then_inc(smm, 1)
                                ctx.inc(f"{tag}_smm")
                            if rle0 > 0:
                                t.wait_ge(rle, rle0)
                            for kt in range(32):
                                t.wait_ge(sex, sex0 + kt + 1)
                                t.matmul(ps1[:], ctx.ones_col[:], pt[kt][:],
                                         start=(kt == 0), stop=(kt == 31)
                                         ).then_inc(lmm, 1)
                                ctx.inc(f"{tag}_lmm")
                            for kt in range(32):
                                ins = t.matmul(
                                    psA[3][:],
                                    vtok[par][:, kt * 128:(kt + 1) * 128],
                                    pt[kt][:],
                                    start=(kt == 0), stop=(kt == 31))
                            ins.then_inc(pvm, 1)
                            ctx.inc(f"{tag}_pvm")

                        @block.scalar
                        def _(s, h=h, qb=qb, smm0=smm0):
                            for kt in range(32):
                                s.wait_ge(smm, smm0 + kt + 1)
                                s.activation(pt[kt][:], psA[kt % 3][:], AF.Exp,
                                             bias=0.0, scale=1.0).then_inc(sex, 1)
                                ctx.inc(f"{tag}_sex")

                        @block.vector
                        def _(v, h=h, qb=qb, pvm0=pvm0, lmm0=lmm0):
                            v.wait_ge(lmm, lmm0 + 32)
                            v.reciprocal(rlrow[0:1, qb * 512:(qb + 1) * 512],
                                         ps1[:]).then_inc(rle, 1)
                            ctx.inc(f"{tag}_rle")
                            v.wait_ge(pvm, pvm0 + 1)
                            v.tensor_copy(obuf[:, qb * 512:(qb + 1) * 512],
                                          psA[3][:]).then_inc(pve, 1)
                            ctx.inc(f"{tag}_pve")

                    rle_h = ctx.val(f"{tag}_rle")
                    pve_h = ctx.val(f"{tag}_pve")
                    trp_h = ctx.val(f"{tag}_trp")
                    ocv_h = ctx.val(f"{tag}_ocv")

                    @block.tensor
                    def _(t, h=h, rle_h=rle_h, pve_h=pve_h, ocv_h=ocv_h):
                        t.wait_ge(rle, rle_h)
                        if ocv_h > 0:
                            t.wait_ge(ocv, ocv_h)
                        for j in range(8):
                            ins = t.transpose(psA[0][:, j:j + 1],
                                              rlrow[0:1, j * 128:(j + 1) * 128],
                                              ctx.ident_f[0:1, 0:1])
                        ins.then_inc(trp, 1)
                        ctx.inc(f"{tag}_trp")
                        t.wait_ge(pve, pve_h)
                        for j in range(8):
                            ins = t.transpose(psB[:, j * 128:(j + 1) * 128],
                                              obuf[:, j * 128:(j + 1) * 128],
                                              ctx.ident[:])
                        ins.then_inc(trp, 1)
                        ctx.inc(f"{tag}_trp")

                    trp_o = ctx.val(f"{tag}_trp")

                    @block.vector
                    def _(v, h=h, trp_o=trp_o):
                        v.wait_ge(trp, trp_o - 1)
                        v.tensor_copy(rlT[:], psA[0][:, 0:8]).then_inc(ocv, 1)
                        ctx.inc(f"{tag}_ocv")
                        v.wait_ge(trp, trp_o)
                        for ti in range(8):
                            v.wait_ge(ocv, ctx.val(f"{tag}_ocv"))
                            v.scalar_tensor_tensor(
                                osum[:, ti * 128:(ti + 1) * 128],
                                psB[:, ti * 128:(ti + 1) * 128],
                                rlT[:, ti:ti + 1],
                                osum[:, ti * 128:(ti + 1) * 128],
                                ALU.mult, ALU.add).then_inc(ocv, 1)
                            ctx.inc(f"{tag}_ocv")

        ocv_done = ctx.val(f"{tag}_ocv")
        with ExitStack() as esC:
            bo1B = ctx.sb(f"{tag}_bo1B", [128, 4096], F32, es=esC)
            emit_bcast(ctx, block, bo1_dram, 4096, bo1B, f"{tag}_bo1",
                       extra_waits=[(f"{tag}_ocv", ctx.val(f"{tag}_ocv"))])
            bo1_done = ctx.val(f"{tag}_bo1_done")
            of = ctx.sb(f"{tag}_of", [128, 1024], BF16, es=esC)
            tmp = [ctx.sb(f"{tag}_tmp{i}", [128, 512], F32, es=esC) for i in range(8)]
            omm = ctx.sem(f"{tag}_omm")
            oev = ctx.sem(f"{tag}_oev")
            och = ctx.sem(f"{tag}_och")
            oyss = [ctx.sem(f"{tag}_oys{i}") for i in range(8)]
            omm0 = ctx.val(f"{tag}_omm")
            oev0 = ctx.val(f"{tag}_oev")
            with ExitStack() as esP:
                psT = ctx.ps(f"{tag}_psT", [128, 1024], F32, es=esP)

                @block.tensor
                def _(t):
                    t.wait_ge(ctx.sems[f"{tag}_ocv"], ocv_done)
                    t.wait_ge(ctx.sems["bc_ev"], bo1_done)
                    for j in range(8):
                        ins = t.transpose(
                            psT[:, (j // 4) * 512 + (j % 4) * 128:
                                (j // 4) * 512 + (j % 4) * 128 + 128],
                            osum[:, j * 128:(j + 1) * 128],
                            ctx.ident_f[:])
                        if j in (3, 7):
                            ins.then_inc(omm, 1)
                            ctx.inc(f"{tag}_omm")

                @block.vector
                def _(v):
                    for half in range(2):
                        v.wait_ge(omm, omm0 + half + 1)
                        v.tensor_copy(of[:, half * 512:(half + 1) * 512],
                                      psT[:, half * 512:(half + 1) * 512]
                                      ).then_inc(oev, 1)
                        ctx.inc(f"{tag}_oev")
            of_oev = ctx.val(f"{tag}_oev")
            omm1 = ctx.val(f"{tag}_omm")
            oev1 = ctx.val(f"{tag}_oev")

            with ExitStack() as esP:
                psO = [ctx.ps(f"{tag}_psO{i}", [128, 512], es=esP) for i in range(8)]
                och0 = ctx.val(f"{tag}_och")

                @block.tensor
                def _(t):
                    t.wait_ge(oev, of_oev)
                    for cb in range(8):
                        if cb >= 1:
                            t.wait_ge(och, och0 + 8 * cb)
                        for ti in range(8):
                            ins = t.matmul(psO[ti][:],
                                           of[:, ti * 128:(ti + 1) * 128],
                                           wo1sb[:, cb * 512:(cb + 1) * 512],
                                           start=True, stop=True)
                            if ti == 7:
                                ins.then_inc(omm, 1)
                                ctx.inc(f"{tag}_omm")

                @block.vector
                def _(v):
                    v.wait_ge(ctx.sems["bc_ev"], bo1_done)
                    for cb in range(8):
                        v.wait_ge(omm, omm1 + cb + 1)
                        for ti in range(8):
                            idx = cb * 8 + ti
                            slot = idx % 8
                            if idx >= 8:
                                v.wait_ge(oyss[slot], 16 * (idx // 8))
                            v.scalar_tensor_tensor(
                                tmp[slot][:], psO[ti][:], 1.0,
                                bo1B[:, cb * 512:(cb + 1) * 512],
                                ALU.mult, ALU.add).then_inc(och, 1)
                            ctx.inc(f"{tag}_och")

                @block.gpsimd
                def _(g):
                    if y_wait is not None:
                        g.wait_ge(ctx.sems[y_wait[0]], y_wait[1])
                    for cb in range(8):
                        for ti in range(8):
                            idx = cb * 8 + ti
                            slot = idx % 8
                            g.wait_ge(och, och0 + idx + 1)
                            g.dma_start(
                                out=y_out[ti * 128:(ti + 1) * 128,
                                          cb * 512:(cb + 1) * 512],
                                in_=tmp[slot][:],
                                accum_op=ALU.add).then_inc(oyss[slot], 16)
                            ctx.inc(f"{tag}_oys{slot}", 16)

    ctx.cnt[f"{tag}_done"] = [(f"{tag}_oys{i}", ctx.val(f"{tag}_oys{i}"))
                              for i in range(8)]



def emit_ycopy(ctx, block, src, dst, tag="ycp", queue="scalar"):
    cp = ctx.sem(tag)

    def _emit(eng):
        for ti in range(8):
            eng.dma_start(out=dst[ti * 128:(ti + 1) * 128, :],
                          in_=src[ti * 128:(ti + 1) * 128, :]).then_inc(cp, 16)
            ctx.inc(tag, 16)

    if queue == "scalar":
        @block.scalar
        def _(s):
            _emit(s)
    else:
        @block.gpsimd
        def _(g):
            _emit(g)


def build_L1():
    nc = bass.Bass("TRN2", target_bir_lowering=False, debug=False)
    x = nc.dram_tensor("x", [1024, 4096], F32, kind="ExternalInput").ap()
    qtg = nc.dram_tensor("qtg", [1024, 3, 4096], I32, kind="ExternalInput").ap()
    zp0 = nc.dram_tensor("zp0", [4096], F32, kind="ExternalInput").ap()
    sc0 = nc.dram_tensor("sc0", [4096], F32, kind="ExternalInput").ap()
    b0 = nc.dram_tensor("b0", [4096], F32, kind="ExternalInput").ap()
    wqkv = nc.dram_tensor("wqkv", [4096, 3072], F32, kind="ExternalInput").ap()
    bqkv = nc.dram_tensor("bqkv", [3072], F32, kind="ExternalInput").ap()

    y1 = nc.dram_tensor("y1", [1024, 4096], F32, kind="ExternalOutput").ap()
    w0t = nc.dram_tensor("w0t", [4096, 4096], BF16, kind="ExternalOutput").ap()
    qfeat = nc.dram_tensor("qfeat", [1024, 1024], BF16, kind="ExternalOutput").ap()
    kown = nc.dram_tensor("kown", [1024, 1024], BF16, kind="ExternalOutput").ap()
    vown = nc.dram_tensor("vown", [1024, 1024], BF16, kind="ExternalOutput").ap()

    ctx = Ctx(nc)
    with nc.Block() as block:
        common_setup(ctx, block)
        emit_ycopy(ctx, block, x, y1, queue="gpsimd")
        ctx.xlnT = ctx.sb("xlnT", [128, 32 * 1024], BF16)
        halfb0B = ctx.sb("halfb0B", [128, 4096], F32)
        emit_bcast(ctx, block, b0, 4096, halfb0B, "hb0", scale=0.5)

        es_f = ExitStack()
        f0bufs = alloc_ffn0_bufs(ctx, es_f, ntmp=6)
        es_ln = ExitStack()
        lnbufs = alloc_ln_bufs(ctx, es_ln)
        es_dq = ExitStack()

        with nc.named_scope("dq"):
            emit_dequant(ctx, block, es_dq, qtg, zp0, sc0, w0t, tag="dq")
        with nc.named_scope("ln1"):
            emit_ln(ctx, block, lnbufs, x, ctx.xlnT, tag="ln")
        emit_ffn0(ctx, block, f0bufs, ctx.xlnT, w0t, halfb0B, y1, tag="s1",
                  xr_wait=("ln_xr", ctx.val("ln_done")),
                  w_wait=ctx.cnt["dq_done"],
                  b_wait=ctx.val("hb0_done"), y_wait=("ycp", 128))
        es_dq.close()
        emit_fence(ctx, block, ctx.cnt["s1_done"] + ctx.cnt["dq_done"])
        emit_ln(ctx, block, lnbufs, y1, ctx.xlnT, tag="ln",
                start_waits=ctx.cnt["s1_done"])
        es_ln.close()
        emit_fence(ctx, block, [("ln_xr", ctx.val("ln_xr")),
                                ("ln_ac", ctx.val("ln_ac")),
                                ("ln_pe", ctx.val("ln_pe"))])

        es_q = ExitStack()
        emit_qkv(ctx, block, es_q, ctx.xlnT, wqkv, bqkv, qfeat, kown, vown,
                 tag="qkv",
                 start_waits=[("ln_pe", ctx.val("ln_pe")),
                              ("ln_ac", ctx.val("ln_ac"))],
                 xr_wait=("ln_xr", ctx.val("ln_done")))

        @block.sync
        def _(sy):
            for sname, sval in ctx.cnt["s1_done"] + ctx.cnt["qkv_done"]:
                sy.wait_ge(ctx.sems[sname], sval)
            for sn, sv in ctx.cnt["dq_done"]:
                sy.wait_ge(ctx.sems[sn], sv)
        es_q.close()
        es_f.close()
    ctx.close()
    return nc


def build_L2():
    nc = bass.Bass("TRN2", target_bir_lowering=False, debug=False)
    y1 = nc.dram_tensor("y1", [1024, 4096], F32, kind="ExternalInput").ap()
    qfeat = nc.dram_tensor("qfeat", [1024, 1024], BF16, kind="ExternalInput").ap()
    kfull = nc.dram_tensor("kfull", [4096, 1024], BF16, kind="ExternalInput").ap()
    vfull = nc.dram_tensor("vfull", [4096, 1024], BF16, kind="ExternalInput").ap()
    wo0 = nc.dram_tensor("wo0", [1024, 4096], F32, kind="ExternalInput").ap()
    bo0 = nc.dram_tensor("bo0", [4096], F32, kind="ExternalInput").ap()
    w0t = nc.dram_tensor("w0t", [4096, 4096], BF16, kind="ExternalInput").ap()
    b0 = nc.dram_tensor("b0", [4096], F32, kind="ExternalInput").ap()
    wgT = nc.dram_tensor("wgT", [4096, 11008], F32, kind="ExternalInput").ap()
    wuT = nc.dram_tensor("wuT", [4096, 11008], F32, kind="ExternalInput").ap()
    wdT = nc.dram_tensor("wdT", [11008, 4096], F32, kind="ExternalInput").ap()
    ag = nc.dram_tensor("ag", [11008], F32, kind="ExternalInput").ap()
    au = nc.dram_tensor("au", [11008], F32, kind="ExternalInput").ap()
    ad = nc.dram_tensor("ad", [4096], F32, kind="ExternalInput").ap()
    wqcT = nc.dram_tensor("wqcT", [4096, 1536], F32, kind="ExternalInput").ap()
    bqc = nc.dram_tensor("bqc", [1536], F32, kind="ExternalInput").ap()
    wkvcT = nc.dram_tensor("wkvcT", [4096, 576], F32, kind="ExternalInput").ap()
    bkvc = nc.dram_tensor("bkvc", [576], F32, kind="ExternalInput").ap()

    y2 = nc.dram_tensor("y2", [1024, 4096], F32, kind="ExternalOutput").ap()
    qc = nc.dram_tensor("qc", [1024, 1536], BF16, kind="ExternalOutput").ap()
    kvc = nc.dram_tensor("kvc", [1024, 576], BF16, kind="ExternalOutput").ap()
    h_dram = nc.dram_tensor("h_scr", [11008, 1024], BF16).ap()

    ctx = Ctx(nc)
    with nc.Block() as block:
        common_setup(ctx, block)
        emit_ycopy(ctx, block, y1, y2)
        emit_s2attn(ctx, block, qfeat, kfull, vfull, wo0, bo0, y2, tag="s2",
                    y_wait=("ycp", 128))

        emit_fence(ctx, block, ctx.cnt["s2_done"])
        ctx.xlnT = ctx.sb("xlnT", [128, 32 * 1024], BF16)
        es_ln = ExitStack()
        lnbufs = alloc_ln_bufs(ctx, es_ln)

        es_b1 = ExitStack()
        halfb0B = ctx.sb("halfb0B", [128, 4096], F32, es=es_b1)
        emit_bcast(ctx, block, b0, 4096, halfb0B, "hb0", scale=0.5)
        f0bufs = alloc_ffn0_bufs2(ctx, es_b1)
        emit_ln(ctx, block, lnbufs, y2, ctx.xlnT, tag="ln",
                start_waits=ctx.cnt["s2_done"])
        emit_ffn0(ctx, block, f0bufs, ctx.xlnT, w0t, halfb0B, y2, tag="s3",
                  xr_wait=("ln_xr", ctx.val("ln_done")),
                  b_wait=ctx.val("hb0_done"))
        es_b1.close()
        emit_fence(ctx, block, ctx.cnt["s3_done"])

        emit_ln(ctx, block, lnbufs, y2, ctx.xlnT, tag="ln",
                start_waits=ctx.cnt["s3_done"])
        emit_ffn1(ctx, block, ctx.xlnT, wgT, wuT, wdT, ag, au, ad, h_dram,
                  y2, tag="s4", xr_wait=("ln_xr", ctx.val("ln_done")),
                  close_between=es_ln)
        emit_fence(ctx, block, ctx.cnt["s4_done"])
        es_ln2 = ExitStack()
        lnbufs2 = alloc_ln_bufs(ctx, es_ln2, pfx="lnb")
        emit_ln(ctx, block, lnbufs2, y2, ctx.xlnT, tag="ln",
                start_waits=ctx.cnt["s4_done"])
        es_ln2.close()
        emit_fence(ctx, block, [("ln_xr", ctx.val("ln_xr")),
                                ("ln_ac", ctx.val("ln_ac")),
                                ("ln_pe", ctx.val("ln_pe"))])

        emit_proj_tok(ctx, block, ctx.xlnT, wqcT, bqc, qc, 1536, tag="pq",
                      xr_wait=("ln_xr", ctx.val("ln_done")))
        emit_fence(ctx, block, ctx.cnt["pq_done"]
                   + [("pq_pmm", ctx.val("pq_pmm")),
                      ("pq_pev", ctx.val("pq_pev")),
                      ("bc_ev", ctx.val("bc_ev"))])
        emit_proj_tok(ctx, block, ctx.xlnT, wkvcT, bkvc, kvc, 576, tag="pk",
                      xr_wait=("ln_xr", ctx.val("ln_done")))

        @block.sync
        def _(sy):
            for sname, sval in (ctx.cnt["s4_done"] + ctx.cnt["pq_done"]
                                + ctx.cnt["pk_done"]):
                sy.wait_ge(ctx.sems[sname], sval)
    ctx.close()
    return nc


def build_L3():
    nc = bass.Bass("TRN2", target_bir_lowering=False, debug=False)
    y2 = nc.dram_tensor("y2", [1024, 4096], F32, kind="ExternalInput").ap()
    qc = nc.dram_tensor("qc", [1024, 1536], BF16, kind="ExternalInput").ap()
    kvcf = nc.dram_tensor("kvcf", [4096, 576], BF16, kind="ExternalInput").ap()
    wquT = nc.dram_tensor("wquT", [1536, 6144], F32, kind="ExternalInput").ap()
    bqu = nc.dram_tensor("bqu", [6144], F32, kind="ExternalInput").ap()
    wkvuT = nc.dram_tensor("wkvuT", [512, 8192], F32, kind="ExternalInput").ap()
    bkvu = nc.dram_tensor("bkvu", [8192], F32, kind="ExternalInput").ap()
    wo1T = nc.dram_tensor("wo1T", [128, 4096], F32, kind="ExternalInput").ap()
    bo1 = nc.dram_tensor("bo1", [4096], F32, kind="ExternalInput").ap()
    wgT = nc.dram_tensor("wgT", [4096, 11008], F32, kind="ExternalInput").ap()
    wuT = nc.dram_tensor("wuT", [4096, 11008], F32, kind="ExternalInput").ap()
    wdT = nc.dram_tensor("wdT", [11008, 4096], F32, kind="ExternalInput").ap()
    ag = nc.dram_tensor("ag", [11008], F32, kind="ExternalInput").ap()
    au = nc.dram_tensor("au", [11008], F32, kind="ExternalInput").ap()
    ad = nc.dram_tensor("ad", [4096], F32, kind="ExternalInput").ap()

    yout = nc.dram_tensor("yout", [1024, 4096], F32, kind="ExternalOutput").ap()
    h_dram = nc.dram_tensor("h_scr", [11008, 1024], BF16).ap()

    ctx = Ctx(nc)
    with nc.Block() as block:
        common_setup(ctx, block)
        emit_ycopy(ctx, block, y2, yout)
        emit_s5(ctx, block, qc, kvcf, wquT, bqu, wkvuT, bkvu, wo1T, bo1,
                yout, tag="s5", y_wait=("ycp", 128))
        emit_fence(ctx, block, ctx.cnt["s5_done"])

        ctx.xlnT = ctx.sb("xlnT", [128, 32 * 1024], BF16)
        es_ln = ExitStack()
        lnbufs = alloc_ln_bufs(ctx, es_ln)
        emit_ln(ctx, block, lnbufs, yout, ctx.xlnT, tag="ln",
                start_waits=ctx.cnt["s5_done"])
        emit_ffn1(ctx, block, ctx.xlnT, wgT, wuT, wdT, ag, au, ad, h_dram,
                  yout, tag="s6", xr_wait=("ln_xr", ctx.val("ln_done")),
                  close_between=es_ln)

        @block.sync
        def _(sy):
            for sname, sval in ctx.cnt["s6_done"]:
                sy.wait_ge(ctx.sems[sname], sval)
    ctx.close()
    return nc



_BUILT = {}


def _get(name, fn):
    if name not in _BUILT:
        _BUILT[name] = fn()
    return _BUILT[name]


def run_l1(inputs):
    x = np.asarray(inputs["L0_attn_x"])
    qw = np.asarray(inputs["L0_ffn_qweight"])
    qtg = np.ascontiguousarray(qw.reshape(4096, 1024, 3).transpose(1, 2, 0))
    zp0 = np.ascontiguousarray(np.asarray(inputs["L0_ffn_weight_zero_point"]).reshape(-1))
    sc0 = np.ascontiguousarray(np.asarray(inputs["L0_ffn_weight_scale"]).reshape(-1))
    b0 = np.asarray(inputs["L0_ffn_bias"])
    wqkv = np.asarray(inputs["L0_attn_Wqkv"])
    bqkv = np.asarray(inputs["L0_attn_bqkv"])

    xs = x.reshape(8, 1024, 4096)
    nc = _get("L1", build_L1)
    in_maps = [dict(x=np.ascontiguousarray(xs[c]), qtg=qtg, zp0=zp0, sc0=sc0,
                    b0=b0, wqkv=wqkv, bqkv=bqkv) for c in range(8)]
    res = run_bass_kernel_spmd(nc, in_maps, core_ids=list(range(8)))
    return res.results


def run_l2(inputs, l1_results):
    asc = np.ascontiguousarray
    sh = _l2_shared(inputs)
    in_maps = []
    for c in range(8):
        grp = [0, 1, 2, 3] if c < 4 else [4, 5, 6, 7]
        kfull = np.concatenate([np.asarray(l1_results[g]["kown"]) for g in grp], 0)
        vfull = np.concatenate([np.asarray(l1_results[g]["vown"]) for g in grp], 0)
        in_maps.append(dict(
            y1=np.asarray(l1_results[c]["y1"]),
            qfeat=np.asarray(l1_results[c]["qfeat"]),
            kfull=kfull, vfull=vfull,
            wo0=np.asarray(inputs["L0_attn_Wo"]),
            bo0=np.asarray(inputs["L0_attn_bo"]),
            w0t=np.asarray(l1_results[c]["w0t"]),
            b0=np.asarray(inputs["L0_ffn_bias"]),
            wgT=sh["wgT"], wuT=sh["wuT"], wdT=sh["wdT"],
            ag=np.asarray(inputs["L1_ffn_ag"]),
            au=np.asarray(inputs["L1_ffn_au"]),
            ad=np.asarray(inputs["L1_ffn_ad"]),
            wqcT=sh["wqcT"], bqc=np.asarray(inputs["L1_attn_bq_c"]),
            wkvcT=sh["wkvcT"], bkvc=np.asarray(inputs["L1_attn_bkv_c"]),
        ))
    nc = _get("L2", build_L2)
    return run_bass_kernel_spmd(nc, in_maps, core_ids=list(range(8))).results


def run_l3(inputs, l2_results):
    sh = _l2_shared(inputs)
    in_maps = []
    for c in range(8):
        grp = [0, 1, 2, 3] if c < 4 else [4, 5, 6, 7]
        kvcf = np.concatenate([np.asarray(l2_results[g]["kvc"]) for g in grp], 0)
        in_maps.append(dict(
            y2=np.asarray(l2_results[c]["y2"]),
            qc=np.asarray(l2_results[c]["qc"]),
            kvcf=kvcf,
            wquT=sh["wquT"], bqu=np.asarray(inputs["L1_attn_bq_up"]),
            wkvuT=sh["wkvuT"], bkvu=np.asarray(inputs["L1_attn_bkv_up"]),
            wo1T=sh["wo1T"], bo1=np.asarray(inputs["L1_attn_bo"]),
            wgT=sh["wgT"], wuT=sh["wuT"], wdT=sh["wdT"],
            ag=np.asarray(inputs["L1_ffn_ag"]),
            au=np.asarray(inputs["L1_ffn_au"]),
            ad=np.asarray(inputs["L1_ffn_ad"]),
        ))
    nc = _get("L3", build_L3)
    return run_bass_kernel_spmd(nc, in_maps, core_ids=list(range(8))).results


_L2_SHARED = {}


def _l2_shared(inputs):
    if not _L2_SHARED:
        asc = np.ascontiguousarray
        _L2_SHARED.update(
            wgT=asc(np.asarray(inputs["L1_ffn_Wg"]).T),
            wuT=asc(np.asarray(inputs["L1_ffn_Wu"]).T),
            wdT=asc(np.asarray(inputs["L1_ffn_Wd"]).T),
            wqcT=asc(np.asarray(inputs["L1_attn_Wq_c"]).T),
            wkvcT=asc(np.asarray(inputs["L1_attn_Wkv_c"]).T),
            wquT=asc(np.asarray(inputs["L1_attn_Wq_up"]).T),
            wkvuT=asc(np.asarray(inputs["L1_attn_Wkv_up"]).T),
            wo1T=asc(np.asarray(inputs["L1_attn_Wo"]).T),
        )
    return _L2_SHARED


def kernel(**inputs):
    _L2_SHARED.clear()
    r1 = run_l1(inputs)
    r2 = run_l2(inputs, r1)
    r3 = run_l3(inputs, r2)
    out = np.concatenate([np.asarray(r3[c]["yout"], np.float32)[None]
                          for c in range(8)], 0)
    return out.reshape(2, 4096, 4096)
